# version 05
# speedup vs baseline: 1.4598x; 1.0493x over previous
import contextlib
import math
import numpy as np
import ml_dtypes
import concourse.bass as bass
import concourse.mybir as mybir
from concourse.bass_utils import run_bass_kernel_spmd

F32 = mybir.dt.float32
BF16 = mybir.dt.bfloat16
I32 = mybir.dt.int32
AF = mybir.ActivationFunctionType
ALU = mybir.AluOpType
AX = mybir.AxisListType

D = 1024
T = 4096
G = 8
NCORES = 8
LN_EPS = 1e-5

GELU_C2 = 2.0 * math.sqrt(2.0 / math.pi)
GELU_C1 = GELU_C2 * 0.044715

FUSED = True

_UID = [0]


def _mk_alloc(nc, es):
    uid = _UID[0]
    _UID[0] += 1
    sb = lambda n, s, d: es.enter_context(nc.sbuf_tensor(f"u{uid}_{n}", s, d))
    ps = lambda n, s, d=None: es.enter_context(
        nc.psum_tensor(f"u{uid}_{n}", s, F32 if d is None else d))
    return uid, sb, ps

CONST_FLOATS = [1e-5, 1e-6, -0.5, -1.0, -40.0, 0.5, 2.0, 0.125] + \
    [-512.0 * 2.0 ** (-(h + 1) * 0.5) for h in range(16)]



AUTO_OPS = {
    "tensor_tensor", "tensor_scalar", "tensor_copy", "tensor_add",
    "tensor_sub", "tensor_mul", "tensor_max", "scalar_tensor_tensor",
    "tensor_tensor_scan", "tensor_reduce", "reduce_max", "reduce_sum",
    "reciprocal", "memset", "memzero", "select", "copy_predicated",
    "tensor_single_scalar", "tensor_scalar_add", "tensor_scalar_sub",
    "tensor_scalar_mul", "tensor_scalar_max", "tensor_scalar_min",
    "tensor_relu", "engine_nop", "activation", "copy", "mul", "add",
    "square", "sqrt", "sign", "transpose", "max", "bn_stats", "bn_aggr",
    "iota",
}
AUTO_SEM = {"vector": "dv", "scalar": "ac", "gpsimd": "po"}


class Eng:
    def __init__(self, E, name, h):
        self.E = E
        self.name = name
        self.h = h
        self.auto = AUTO_SEM.get(name)

    def wait(self, ev):
        if ev is None:
            return
        if isinstance(ev, list):
            for e in ev:
                self.wait(e)
            return
        sem, val = ev
        if val <= 0:
            return
        key = (self.name, sem)
        if self.E.lastw.get(key, -1) >= val:
            return
        self.E.lastw[key] = val
        self.h.wait_ge(self.E.sems[sem], val)

    def waitb(self, name):
        self.wait(self.E.free.get(name))

    def __getattr__(self, a):
        f = getattr(self.h, a)
        if self.auto is None or a not in AUTO_OPS or not callable(f):
            return f
        E_, sem = self.E, self.auto

        def wrapped(*args, **kw):
            self.wait((sem, E_.cnt[sem]))
            ins = f(*args, **kw)
            E_.cnt[sem] += 1
            ins.then_inc(E_.sems[sem], 1)
            return ins
        return wrapped


class E:
    SEMS = ("qw", "qv", "qg", "qc", "qx", "qs", "pe", "dv", "ac", "po")

    def __init__(self, nc, stack):
        self.nc = nc
        self.sems = {}
        self.cnt = {}
        self.lastw = {}
        self.free = {}
        for s in self.SEMS:
            self.sems[s] = stack.enter_context(nc.semaphore("sem_" + s))
            self.cnt[s] = 0
        self.blk = None

    def i(self, instr, sem, n=1):
        if sem in ("dv", "ac", "po") and n == 1:
            return (sem, self.cnt[sem])
        self.cnt[sem] += n
        instr.then_inc(self.sems[sem], n)
        return (sem, self.cnt[sem])

    def ev(self, sem):
        return (sem, self.cnt[sem])

    def setfree(self, name, ev):
        self.free[name] = ev

    def _sec(self, engname, fn):
        def wrap(h):
            fn(Eng(self, engname, h))
        getattr(self.blk, engname)(wrap)

    def pe(self, fn):
        self._sec("tensor", fn)

    def dv(self, fn):
        self._sec("vector", fn)

    def ac(self, fn):
        self._sec("scalar", fn)

    def po(self, fn):
        self._sec("gpsimd", fn)

    def sy(self, fn):
        self._sec("sync", fn)

    @contextlib.contextmanager
    def block(self):
        with self.nc.Block() as blk:
            self.blk = blk
            yield blk
            self.blk = None



def emit_ln_tile(E_, C, st, Xf, ld_ev, Tt, eps, out_bf=None, out_f32=None,
                 out_off=0, gg=G, gamma=None, beta=None, out_free_key=None,
                 kp="ln", ones=None):
    S1, S2, MB, RB = st["S1"], st["S2"], st["MB"], st["RB"]
    SQ, M_, V_, R_, T1 = st["SQ"], st["M"], st["V"], st["R"], st["T1"]
    dinv = 1.0 / (gg * 128)
    if ones is None:
        ones = C["ones_col"][:]

    def f_pe1(pe):
        pe.wait(ld_ev)
        pe.waitb(kp + "_s1")
        for g in range(gg):
            mi = pe.matmul(S1, ones, Xf[:, g, :],
                           start=(g == 0), stop=(g == gg - 1))
        E_.i(mi, "pe")
    E_.pe(f_pe1)
    s1_ev = E_.ev("pe")

    for g in range(gg):
        par = g % 2

        def f_ac1(ac, g=g, par=par):
            ac.wait(ld_ev)
            ac.waitb(kp + f"_sq{par}")
            ins = ac.activation(SQ[:, par, :], Xf[:, g, :], AF.Square)
            E_.i(ins, "ac")
        E_.ac(f_ac1)
        sqg_ev = E_.ev("ac")

        def f_pe2(pe, g=g, par=par, sqg_ev=sqg_ev):
            pe.wait(sqg_ev)
            if g == 0:
                pe.waitb(kp + "_s2")
            mi = pe.matmul(S2, C["ones_col"][:], SQ[:, par, :],
                           start=(g == 0), stop=(g == gg - 1))
            ev = E_.i(mi, "pe")
            E_.setfree(kp + f"_sq{par}", ev)
        E_.pe(f_pe2)
    s2_ev = E_.ev("pe")

    def f_dv1(dv):
        dv.wait(s1_ev)
        dv.waitb(kp + "_mvr")
        dv.tensor_scalar(M_[:], S1, dinv, None, ALU.mult)
        dv.tensor_tensor(T1[0:1, 0:Tt], M_[:], M_[:], ALU.mult)
        dv.wait(s2_ev)
        ins = dv.scalar_tensor_tensor(V_[:], S2, dinv, T1[0:1, 0:Tt],
                                      ALU.mult, ALU.subtract)
        E_.i(ins, "dv")
    E_.dv(f_dv1)
    v_ev = E_.ev("dv")
    E_.setfree(kp + "_s1", v_ev)
    E_.setfree(kp + "_s2", v_ev)

    def f_ac2(ac):
        ac.wait(v_ev)
        ac.activation(R_[:], V_[:], AF.Ln, bias=float(eps))
        ins = ac.activation(R_[:], R_[:], AF.Exp, scale=-0.5)
        E_.i(ins, "ac")
    E_.ac(f_ac2)
    r_ev = E_.ev("ac")

    def f_pe3(pe):
        pe.wait(r_ev)
        pe.waitb(kp + "_mbrb")
        pe.matmul(MB, C["ones_row"][:], M_[:], start=True, stop=True)
        mi = pe.matmul(RB, C["ones_row"][:], R_[:], start=True, stop=True)
        E_.i(mi, "pe")
    E_.pe(f_pe3)
    bc_ev = E_.ev("pe")
    E_.setfree(kp + "_mvr", bc_ev)

    def f_dv2(dv):
        dv.wait(bc_ev)
        if out_free_key:
            dv.waitb(out_free_key)
        if gamma is not None:
            dv.waitb("ln_gamma_ld")
        for g in range(gg):
            dv.tensor_tensor(T1[:], Xf[:, g, :], MB, ALU.subtract)
            if gamma is not None:
                dv.tensor_tensor(T1[:], T1[:], RB, ALU.mult)
                if out_f32 is not None:
                    ins = dv.tensor_scalar(out_f32[:, g, :], T1[:],
                                           gamma[:, g:g + 1], beta[:, g:g + 1],
                                           ALU.mult, ALU.add)
                    if out_bf is not None:
                        ins = dv.tensor_copy(
                            out_bf[:, g, out_off:out_off + Tt], out_f32[:, g, :])
                else:
                    ins = dv.tensor_scalar(out_bf[:, g, out_off:out_off + Tt],
                                           T1[:], gamma[:, g:g + 1],
                                           beta[:, g:g + 1], ALU.mult, ALU.add)
            elif out_f32 is not None:
                ins = dv.tensor_tensor(out_f32[:, g, :], T1[:], RB, ALU.mult)
                if out_bf is not None:
                    ins = dv.tensor_copy(
                        out_bf[:, g, out_off:out_off + Tt], out_f32[:, g, :])
            else:
                ins = dv.tensor_tensor(out_bf[:, g, out_off:out_off + Tt],
                                       T1[:], RB, ALU.mult)
        E_.i(ins, "dv")
    E_.dv(f_dv2)
    out_ev = E_.ev("dv")
    E_.setfree(kp + "_mbrb", out_ev)
    return out_ev


def alloc_ln_scratch(es, nc, pfx, Tt, ps_alias, sq=None):
    sb = lambda n, s, d: es.enter_context(nc.sbuf_tensor(f"u{_UID[0]}x_{pfx}{n}", s, d))
    a, b, c_, d_ = ps_alias
    st = {
        "SQ": sq if sq is not None else sb("SQ", [128, 2, Tt], F32),
        "M": sb("M", [1, Tt], F32),
        "V": (lambda v: v)(sb("V", [1, Tt], F32)),
        "T1": sb("T1", [128, Tt], F32),
        "S1": a[0:1, 0:Tt],
        "S2": b[0:1, 0:Tt],
        "MB": c_[:, 0:Tt],
        "RB": d_[:, 0:Tt],
    }
    st["R"] = st["V"]
    return st



def emit_gate_softmax_topk(E_, C, st, lg_ev, topk, norm, norm_eps,
                           bias_bc=None):
    PSL, PRB, TMP8, SEL, M1 = st["PSL"], st["PRB"], st["TMP8"], st["SEL"], st["M1"]
    BAS = st["BAS"]

    def f_nm(dv):
        dv.wait(lg_ev)
        dv.waitb("gt_m1")
        ins = dv.reduce_max(M1[:], PSL, axis=AX.X, negate=True)
        E_.i(ins, "dv")
    E_.dv(f_nm)
    nm_ev = E_.ev("dv")

    def f_exp(ac):
        ac.wait(nm_ev)
        ac.waitb("gt_prb")
        ins = ac.activation(PRB[:], PSL, AF.Exp, bias=M1[:])
        E_.i(ins, "ac")
    E_.ac(f_exp)
    ex_ev = E_.ev("ac")
    E_.setfree("gt_psl", ex_ev)

    def f_top(dv):
        dv.wait(ex_ev)
        dv.waitb("gt_sel")
        dv.reduce_sum(M1[:], PRB[:], axis=AX.X)
        dv.reciprocal(M1[:], M1[:])
        dv.tensor_scalar(PRB[:], PRB[:], M1[:, 0:1], None, ALU.mult)
        if bias_bc is not None:
            dv.tensor_tensor(BAS[:], PRB[:], bias_bc, ALU.add)
            basis = BAS
        else:
            basis = PRB
        dv.tensor_copy(TMP8[:], basis[:])
        for r in range(topk):
            dv.reduce_max(M1[:], TMP8[:], axis=AX.X)
            if r < topk - 1:
                dv.tensor_scalar(SEL[:], TMP8[:], M1[:, 0:1], None, ALU.is_equal)
                dv.scalar_tensor_tensor(TMP8[:], SEL[:], -1e30, TMP8[:],
                                        ALU.mult, ALU.add)
        dv.tensor_scalar(SEL[:], basis[:], M1[:, 0:1], None, ALU.is_ge)
        ins = dv.tensor_tensor(SEL[:], SEL[:], PRB[:], ALU.mult)
        if norm:
            dv.reduce_sum(M1[:], SEL[:], axis=AX.X)
            if norm_eps:
                dv.tensor_scalar(M1[:], M1[:], float(norm_eps), None, ALU.add)
            dv.reciprocal(M1[:], M1[:])
            ins = dv.tensor_scalar(SEL[:], SEL[:], M1[:, 0:1], None, ALU.mult)
        E_.i(ins, "dv")
    E_.dv(f_top)
    sel_ev = E_.ev("dv")
    E_.setfree("gt_m1", sel_ev)
    E_.setfree("gt_prb", sel_ev)
    return sel_ev


def emit_gating_phase(E_, C, st, XN, GA, gatew_sb, n_tok, topk, norm, norm_eps,
                      xn_ev, bias_bc=None, xn_off=0):
    for s in range(n_tok // 128):
        c0 = xn_off + s * 128

        def f_lg(pe, c0=c0):
            pe.wait(xn_ev)
            pe.waitb("gt_psl")
            pe.waitb("gatew_ld")
            for g in range(G):
                mi = pe.matmul(st["PSL"], XN[:, g, c0:c0 + 128],
                               gatew_sb[:, g, :], start=(g == 0),
                               stop=(g == G - 1))
            E_.i(mi, "pe")
        E_.pe(f_lg)
        lg_ev = E_.ev("pe")
        sel_ev = emit_gate_softmax_topk(E_, C, st, lg_ev, topk, norm, norm_eps,
                                        bias_bc=bias_bc)

        def f_tr(pe, sel_ev=sel_ev):
            pe.wait(sel_ev)
            pe.waitb("gt_gt")
            ins = pe.transpose(st["PSGT"], st["SEL"][:], C["ident_f32"][:])
            E_.i(ins, "pe")
        E_.pe(f_tr)
        tr_ev = E_.ev("pe")
        E_.setfree("gt_sel", tr_ev)

        def f_cp(dv, tr_ev=tr_ev, s=s):
            dv.wait(tr_ev)
            ins = dv.tensor_copy(GA[:, s * 128:(s + 1) * 128], st["PSGT"])
            E_.i(ins, "dv")
        E_.dv(f_cp)
        E_.setfree("gt_gt", E_.ev("dv"))
    return E_.ev("dv")



def emit_moe_ffn(E_, C, W, xin, xout, cfg):
    nc = E_.nc
    HALF = 2048
    r = cfg["router"]
    internal_ln = bool(r.get("internal_ln"))
    has_wu = any(e.get("wu_ap") for e in cfg["experts"])
    HBW = 512

    with contextlib.ExitStack() as es:
        uid, sb, ps_ = _mk_alloc(nc, es)
        XN = sb("mf_XN", [128, G, HALF], BF16)
        OA = sb("mf_OA", [128, G, HALF], F32)
        XF = sb("mf_XF", [128, G, 512], F32)
        GATEW = sb("mf_gw", [128, G, 8], BF16)
        GA = sb("mf_ga", [8, HALF], F32)
        GB4 = XF[:, 0:4, :]
        WB1 = [sb(f"mf_w1{i}", [128, G, HBW], BF16) for i in range(2)]
        WB3 = ([sb(f"mf_w3{i}", [128, G, HBW], BF16) for i in range(2)]
               if has_wu else [None, None])
        WB2 = [sb(f"mf_w2{i}", [128, HBW // 128, 1024], BF16) for i in range(2)]
        ug = 8 if internal_ln else HBW // 128
        U8 = sb("mf_u8", [128, ug, 512], BF16)
        HT = U8[:, 0:HBW // 128, :]
        SCR = sb("mf_scr", [128, 8, 512], F32)
        SC1 = [SCR[:, 0, :], SCR[:, 1, :]]
        SC2 = [SCR[:, 2, :], SCR[:, 3, :]]
        SC3 = [SCR[:, 4, :], SCR[:, 5, :]]
        PH = [ps_(f"mf_ph{i}", [128, 512])
              for i in range(2)]
        PH3 = [ps_(f"mf_ph3{i}", [128, 512])
               for i in range(2)]
        PD = [ps_(f"mf_pd{i}", [128, 512])
              for i in range(2)]
        PMB = ps_("mf_pmb", [128, 512])
        PRBC = ps_("mf_prb", [128, 512])
        st = alloc_ln_scratch(es, nc, "mf_ln", 512, (PD[0], PD[1], PMB, PRBC),
                              sq=SCR[:, 0:2, :])
        st["PSL"] = PH[0][:, 0:8]
        st["PSGT"] = PH[1][0:8, 0:128]
        st["PRB"] = sb("mf_gprb", [128, 8], F32)
        st["TMP8"] = sb("mf_gtmp", [128, 8], F32)
        st["SEL"] = sb("mf_gsel", [128, 8], F32)
        st["BAS"] = (sb("mf_gbas", [128, 8], F32) if r.get("bias") else None)
        st["M1"] = sb("mf_gm1", [128, 1], F32)
        BIASBC = sb("mf_gbias", [128, 8], F32) if r.get("bias") else None
        XN2 = U8[:, 0:8, :] if internal_ln else None

        with E_.block():
            def f_gw(g_):
                ins = g_.dma_start(GATEW[:], W["gatew"])
                E_.i(ins, "qg", 16)
            E_.po(f_gw)
            E_.setfree("gatew_ld", E_.ev("qg"))
            if BIASBC is not None:
                BROW = sb("mf_brow", [1, 8], F32)

                def f_bl(sy):
                    ins = sy.dma_start(BROW[:], W["gatebias"])
                    E_.i(ins, "qc", 16)
                E_.sy(f_bl)
                bl_ev = E_.ev("qc")

                def f_bb(pe):
                    pe.wait(bl_ev)
                    ins = pe.matmul(PH[0][:, 0:8], C["ones_row"][:], BROW[:],
                                    start=True, stop=True)
                    E_.i(ins, "pe")
                E_.pe(f_bb)
                bb_ev = E_.ev("pe")

                def f_bc(dv):
                    dv.wait(bb_ev)
                    ins = dv.tensor_copy(BIASBC[:], PH[0][:, 0:8])
                    E_.i(ins, "dv")
                E_.dv(f_bc)
                E_.setfree("gt_psl", E_.ev("dv"))

        for hf in range(2):
            hof = hf * HALF
            with E_.block():
                for tt in range(4):
                    gtile = hf * 4 + tt

                    def f_lx(sy, gtile=gtile):
                        sy.waitb("mf_xf")
                        ins = sy.dma_start(XF[:], xin[gtile])
                        E_.i(ins, "qx", 16)
                    E_.sy(f_lx)
                    ld_ev = E_.ev("qx")
                    xn_ev = emit_ln_tile(
                        E_, C, st, XF, ld_ev, 512, LN_EPS, out_bf=XN,
                        out_off=tt * 512)
                    E_.setfree("mf_xf", xn_ev)
                    if internal_ln:
                        xn2_ev = emit_ln_tile(
                            E_, C, st, XN[:, :, tt * 512:(tt + 1) * 512],
                            xn_ev, 512, cfg["ln2_eps"], out_bf=XN2,
                            gamma=W["lns_sb"], beta=W["lnb_sb"],
                            out_free_key="mf_xn2", ones=C["ones_col_bf"][:])
                        gate_src, gate_ev, gate_off = XN2, xn2_ev, 0
                    else:
                        gate_src, gate_ev, gate_off = XN, xn_ev, tt * 512
                    emit_gating_phase(
                        E_, C, st, gate_src, GA[:, tt * 512:(tt + 1) * 512],
                        GATEW, 512, r["topk"], r["norm"],
                        r.get("norm_eps", 0.0), gate_ev,
                        bias_bc=(BIASBC[:] if BIASBC is not None else None),
                        xn_off=gate_off)
                    if internal_ln:
                        E_.setfree("mf_xn2", E_.ev("pe"))
                ga_ev = E_.ev("dv")

                def f_ms(dv):
                    ins = dv.memset(OA[:], 0.0)
                    E_.i(ins, "dv")
                E_.dv(f_ms)

            with E_.block():
                wrot = 0
                for ex in cfg["experts"]:
                    NHC = HBW // 128
                    gated = ex.get("gated")
                    if gated is not None:
                        for tt in range(4):
                            def f_gb(pe, gated=gated, tt=tt):
                                pe.wait(ga_ev)
                                pe.waitb("mf_pgb")
                                mi = pe.matmul(
                                    PMB[:],
                                    C["sel8"][0:8, gated * 128:(gated + 1) * 128],
                                    GA[:, tt * 512:(tt + 1) * 512],
                                    start=True, stop=True)
                                E_.i(mi, "pe")
                            E_.pe(f_gb)
                            gb_ev = E_.ev("pe")

                            def f_gc(dv, gb_ev=gb_ev, tt=tt):
                                dv.wait(gb_ev)
                                ins = dv.tensor_copy(GB4[:, tt, :], PMB[:])
                                E_.i(ins, "dv")
                            E_.dv(f_gc)
                            E_.setfree("mf_pgb", E_.ev("dv"))
                    for hb in range(ex["nhb"]):
                        slot = wrot % 2
                        w1b, w3b, w2b = WB1[slot], WB3[slot], WB2[slot]
                        wfree = E_.free.get(f"mf_wb{slot}")
                        wrot += 1

                        wsem = "qw" if slot == 0 else "qv"

                        def f_lw(g_, ex=ex, hb=hb, w1b=w1b, w3b=w3b, w2b=w2b,
                                 wfree=wfree, NHC=NHC, wsem=wsem):
                            g_.wait(wfree)
                            ins = g_.dma_start(w1b[:], ex["wg_ap"](hb))
                            E_.i(ins, wsem, 16)
                            if ex.get("wu_ap"):
                                ins = g_.dma_start(w3b[:], ex["wu_ap"](hb))
                                E_.i(ins, wsem, 16)
                            ins = g_.dma_start(w2b[:], ex["wd_ap"](hb))
                            E_.i(ins, wsem, 16)
                        E_.po(f_lw)
                        w_ev = E_.ev(wsem)

                        for tt in range(4):
                            ts0 = tt * 512
                            gb1_ev = None
                            GB1 = GB4[:, tt, :]

                            if ex["act"] == "gelu_tanh":
                                PHX = [PH[0], PH[1], PH3[0], PH3[1]]

                                def f_up4(pe, w1b=w1b, ts0=ts0, w_ev=w_ev,
                                          PHX=PHX):
                                    pe.wait(w_ev)
                                    for hc in range(4):
                                        pe.waitb(f"mf_phx{hc}")
                                        for k in range(G):
                                            mi = pe.matmul(
                                                PHX[hc][:],
                                                w1b[:, k, hc * 128:(hc + 1) * 128],
                                                XN[:, k, ts0:ts0 + 512],
                                                start=(k == 0),
                                                stop=(k == G - 1))
                                        E_.i(mi, "pe")
                                E_.pe(f_up4)
                                up_evs = [("pe", E_.cnt["pe"] - 3 + j)
                                          for j in range(4)]

                                def f_sq4(ac, up_evs=up_evs, PHX=PHX):
                                    for hc in range(4):
                                        ac.wait(up_evs[hc])
                                        ac.waitb(f"mf_scx{hc}")
                                        ins = ac.activation(SCR[:, hc, :],
                                                            PHX[hc][:],
                                                            AF.Square)
                                        E_.i(ins, "ac")
                                E_.ac(f_sq4)
                                sq_evs = [("ac", E_.cnt["ac"] - 3 + j)
                                          for j in range(4)]

                                def f_a14(dv, sq_evs=sq_evs, PHX=PHX):
                                    for hc in range(4):
                                        dv.wait(sq_evs[hc])
                                        dv.tensor_scalar(SCR[:, hc, :],
                                                         SCR[:, hc, :],
                                                         GELU_C1, GELU_C2,
                                                         ALU.mult, ALU.add)
                                        ins = dv.tensor_tensor(SCR[:, hc, :],
                                                               SCR[:, hc, :],
                                                               PHX[hc][:],
                                                               ALU.mult)
                                        E_.i(ins, "dv")
                                E_.dv(f_a14)
                                a1_evs = [("dv", E_.cnt["dv"] - 6 + 2 * j)
                                          for j in range(4)]

                                def f_sg4(ac, a1_evs=a1_evs):
                                    for hc in range(4):
                                        ac.wait(a1_evs[hc])
                                        ins = ac.activation(SCR[:, 4 + hc, :],
                                                            SCR[:, hc, :],
                                                            AF.Sigmoid)
                                        E_.i(ins, "ac")
                                E_.ac(f_sg4)
                                sg_evs = [("ac", E_.cnt["ac"] - 3 + j)
                                          for j in range(4)]

                                def f_m24(dv, sg_evs=sg_evs, PHX=PHX,
                                          gb1_ev=gb1_ev):
                                    dv.wait(gb1_ev)
                                    dv.waitb("mf_ht")
                                    for hc in range(4):
                                        dv.wait(sg_evs[hc])
                                        ins = dv.tensor_tensor(
                                            SCR[:, 4 + hc, :], SCR[:, 4 + hc, :],
                                            PHX[hc][:], ALU.mult)
                                        ev = E_.i(ins, "dv")
                                        E_.setfree(f"mf_phx{hc}", ev)
                                        ins = dv.tensor_tensor(
                                            HT[:, hc, :], SCR[:, 4 + hc, :],
                                            GB1, ALU.mult)
                                        ev = E_.i(ins, "dv")
                                        E_.setfree(f"mf_scx{hc}", ev)
                                E_.dv(f_m24)
                            else:
                              for hc in range(NHC):
                                par = (hc + tt * NHC) % 2
                                ph, ph3 = PH[par], PH3[par]

                                def f_up(pe, w1b=w1b, w3b=w3b, ph=ph, ph3=ph3,
                                         hc=hc, ts0=ts0, w_ev=w_ev, par=par,
                                         ex=ex):
                                    pe.wait(w_ev)
                                    pe.waitb(f"mf_ph{par}")
                                    for k in range(G):
                                        mi = pe.matmul(
                                            ph[:],
                                            w1b[:, k, hc * 128:(hc + 1) * 128],
                                            XN[:, k, ts0:ts0 + 512],
                                            start=(k == 0), stop=(k == G - 1))
                                    E_.i(mi, "pe")
                                    if ex.get("wu_ap"):
                                        pe.waitb(f"mf_ph3{par}")
                                        for k in range(G):
                                            mi = pe.matmul(
                                                ph3[:],
                                                w3b[:, k, hc * 128:(hc + 1) * 128],
                                                XN[:, k, ts0:ts0 + 512],
                                                start=(k == 0), stop=(k == G - 1))
                                        E_.i(mi, "pe")
                                E_.pe(f_up)
                                up_ev = E_.ev("pe")

                                if ex["act"] == "silu_mul":
                                    def f_sl(ac, par=par, ph=ph, up_ev=up_ev):
                                        ac.wait(up_ev)
                                        ac.waitb(f"mf_sc1{par}")
                                        ins = ac.activation(SC1[par], ph[:],
                                                            AF.Silu)
                                        E_.i(ins, "ac")
                                    E_.ac(f_sl)
                                    sl_ev = E_.ev("ac")
                                    E_.setfree(f"mf_ph{par}", sl_ev)

                                    def f_m1(dv, par=par, ph3=ph3, hc=hc,
                                             sl_ev=sl_ev, gb1_ev=gb1_ev,
                                             gated=gated):
                                        dv.wait(sl_ev)
                                        dv.waitb("mf_ht")
                                        if gated is not None:
                                            dv.wait(gb1_ev)
                                            dv.tensor_tensor(
                                                SC2[par], SC1[par], ph3[:],
                                                ALU.mult)
                                            ins = dv.tensor_tensor(
                                                HT[:, hc, :], SC2[par],
                                                GB1, ALU.mult)
                                        else:
                                            ins = dv.tensor_tensor(
                                                HT[:, hc, :], SC1[par], ph3[:],
                                                ALU.mult)
                                        ev = E_.i(ins, "dv")
                                        E_.setfree(f"mf_ph3{par}", ev)
                                        E_.setfree(f"mf_sc1{par}", ev)
                                    E_.dv(f_m1)
                                else:
                                    def f_sq(ac, par=par, ph=ph, up_ev=up_ev):
                                        ac.wait(up_ev)
                                        ac.waitb(f"mf_sc1{par}")
                                        ins = ac.activation(SC1[par], ph[:],
                                                            AF.Square)
                                        E_.i(ins, "ac")
                                    E_.ac(f_sq)
                                    sq_ev = E_.ev("ac")

                                    def f_a1(dv, par=par, ph=ph, sq_ev=sq_ev):
                                        dv.wait(sq_ev)
                                        dv.tensor_scalar(SC2[par], SC1[par],
                                                         GELU_C1, GELU_C2,
                                                         ALU.mult, ALU.add)
                                        ins = dv.tensor_tensor(SC1[par],
                                                               SC2[par], ph[:],
                                                               ALU.mult)
                                        E_.i(ins, "dv")
                                    E_.dv(f_a1)
                                    a1_ev = E_.ev("dv")

                                    def f_sg(ac, par=par, a1_ev=a1_ev):
                                        ac.wait(a1_ev)
                                        ins = ac.activation(SC3[par],
                                                            SC1[par],
                                                            AF.Sigmoid)
                                        E_.i(ins, "ac")
                                    E_.ac(f_sg)
                                    sg_ev = E_.ev("ac")

                                    def f_m2(dv, par=par, ph=ph, hc=hc,
                                             sg_ev=sg_ev, gb1_ev=gb1_ev):
                                        dv.wait(sg_ev)
                                        dv.wait(gb1_ev)
                                        ins = dv.tensor_tensor(
                                            SC2[par], SC3[par], ph[:],
                                            ALU.mult)
                                        ev = E_.i(ins, "dv")
                                        E_.setfree(f"mf_ph{par}", ev)
                                        dv.waitb("mf_ht")
                                        ins = dv.tensor_tensor(
                                            HT[:, hc, :], SC2[par],
                                            GB1, ALU.mult)
                                        ev = E_.i(ins, "dv")
                                        E_.setfree(f"mf_sc1{par}", ev)
                                    E_.dv(f_m2)
                            ht_ev = E_.ev("dv")

                            for dg in range(G):
                                pdb = PD[dg % 2]

                                def f_dn(pe, w2b=w2b, pdb=pdb, dg=dg, NHC=NHC,
                                         ht_ev=ht_ev):
                                    pe.wait(ht_ev)
                                    pe.waitb(f"mf_pd{dg % 2}")
                                    for k in range(NHC):
                                        mi = pe.matmul(
                                            pdb[:],
                                            w2b[:, k, dg * 128:(dg + 1) * 128],
                                            HT[:, k, :], start=(k == 0),
                                            stop=(k == NHC - 1))
                                    E_.i(mi, "pe")
                                E_.pe(f_dn)
                                dn_ev = E_.ev("pe")

                                def f_ac_(dv, pdb=pdb, dg=dg, ts0=ts0,
                                          dn_ev=dn_ev):
                                    dv.wait(dn_ev)
                                    ins = dv.tensor_tensor(
                                        OA[:, dg, ts0:ts0 + 512],
                                        OA[:, dg, ts0:ts0 + 512], pdb[:],
                                        ALU.add)
                                    ev = E_.i(ins, "dv")
                                    E_.setfree(f"mf_pd{dg % 2}", ev)
                                E_.dv(f_ac_)
                            E_.setfree("mf_ht", E_.ev("pe"))
                        E_.setfree(f"mf_wb{slot}", E_.ev("pe"))
            E_.setfree("mf_xf", E_.ev("dv"))

            with E_.block():
                for tt in range(4):
                    gtile = hf * 4 + tt

                    def f_lx2(sy, gtile=gtile):
                        sy.waitb("mf_xf")
                        ins = sy.dma_start(XF[:], xin[gtile])
                        E_.i(ins, "qx", 16)
                    E_.sy(f_lx2)
                    ld2_ev = E_.ev("qx")

                    def f_res(dv, tt=tt, ld2_ev=ld2_ev):
                        dv.wait(ld2_ev)
                        for g in range(G):
                            ins = dv.scalar_tensor_tensor(
                                XF[:, g, :], OA[:, g, tt * 512:(tt + 1) * 512],
                                0.5, XF[:, g, :], ALU.mult, ALU.add)
                        E_.i(ins, "dv")
                    E_.dv(f_res)
                    xo_ev = E_.ev("dv")

                    def f_st(sy, gtile=gtile, xo_ev=xo_ev):
                        sy.wait(xo_ev)
                        ins = sy.dma_start(xout[gtile], XF[:])
                        E_.i(ins, "qs", 16)
                    E_.sy(f_st)
                    E_.setfree("mf_xf", E_.ev("qs"))



def emit_ffn3(E_, C, W, xin, xout, qcache=None, first=True):
    nc = E_.nc
    H32 = 32
    w1r = W["f3_w1t"]
    w2r = W["f3_w2t"]

    with contextlib.ExitStack() as es:
        uid, sb, ps_ = _mk_alloc(nc, es)
        WQ1 = sb("f3_wq1", [128, G, 4096], BF16)
        WQ2 = sb("f3_wq2", [128, H32, 1024], BF16)
        QT2 = sb("f3_qt2", [128, 512], F32)
        XF = sb("f3_xf", [128, G, 512], F32)
        SCR = sb("f3_scr", [128, 4, 512], F32)
        WQF = SCR[:].rearrange("p a b -> p (a b)")
        QT = sb("f3_qt", [128, 512], F32)
        XN1 = sb("f3_xn1", [128, G, 512], BF16)
        HSB = sb("f3_h", [128, H32, 512], BF16)
        XA = HSB[:, 0:G, :]
        B1 = sb("f3_b1s", [128, H32], F32)
        L2Gs = sb("f3_l2g", [128, H32], F32)
        L2Bs = sb("f3_l2b", [128, H32], F32)
        L1Gs = sb("f3_l1g", [128, G], F32)
        L1Bs = sb("f3_l1b", [128, G], F32)
        B2 = sb("f3_b2s", [128, G], F32)
        MACC = sb("f3_macc", [128, 1], F32)
        MSC = [sb(f"f3_msc{i}", [128, 1], F32) for i in range(2)]
        PH = [ps_(f"f3_ph{i}", [128, 512])
              for i in range(2)]
        PD = [ps_(f"f3_pd{i}", [128, 512])
              for i in range(2)]
        PMB = ps_("f3_pmb", [128, 512])
        PRBC = ps_("f3_prb", [128, 512])
        st = alloc_ln_scratch(es, nc, "f3_ln", 512, (PD[0], PD[1], PMB, PRBC),
                              sq=SCR[:, 0:2, :])

        with E_.block():
            def f_lp(sy):
                for dst, key, n16 in ((B1, "f3_b1", 16), (L2Gs, "f3_ln2g", 16),
                                      (L2Bs, "f3_ln2b", 16), (L1Gs, "f3_ln1g", 16),
                                      (L1Bs, "f3_ln1b", 16), (B2, "f3_b2", 16)):
                    ins = sy.dma_start(dst[:], W[key])
                    E_.i(ins, "qc", 16)
            E_.sy(f_lp)
            E_.setfree("f3_params", E_.ev("qc"))

        if not first:
            with E_.block():
                def f_lq(g_):
                    ins = g_.dma_start(WQ1[:], qcache[0])
                    E_.i(ins, "qg", 16)
                    ins = g_.dma_start(WQ2[:], qcache[1])
                    E_.i(ins, "qg", 16)
                E_.po(f_lq)
            wq_ev = E_.ev("qg")
        else:
          for wi, (wr, WQ, nch, nel) in enumerate((
                (w1r, WQ1, 16, 1024 * 4096), (w2r, WQ2, 16, 4096 * 1024))):
            with E_.block():
                def f_z(dv):
                    ins = dv.memset(MACC[:], 0.0)
                    E_.i(ins, "dv")
                E_.dv(f_z)
                for c in range(nch):
                    src_ap = wr[c]

                    def f_lq(sy, src_ap=src_ap, wi=wi):
                        sy.waitb("f3_wqf")
                        if wi == 0:
                            ins = sy.dma_start(
                                WQF[:].rearrange("p (a b) -> p a b", a=8), src_ap)
                        else:
                            ins = sy.dma_start(
                                WQF[:].rearrange("p (a b) -> p a b", a=2), src_ap)
                        E_.i(ins, "qx", 16)
                    E_.sy(f_lq)
                    lq_ev = E_.ev("qx")

                    def f_ab(dv, lq_ev=lq_ev):
                        dv.wait(lq_ev)
                        dv.reduce_sum(QT[:, 0:1], WQF[:], axis=AX.X,
                                      apply_absolute_value=True)
                        ins = dv.tensor_tensor(MACC[:], MACC[:], QT[:, 0:1],
                                               ALU.add)
                        E_.i(ins, "dv")
                    E_.dv(f_ab)
                    E_.setfree("f3_wqf", E_.ev("dv"))
                macc_ev = E_.ev("dv")

                def f_tot(pe, macc_ev=macc_ev):
                    pe.wait(macc_ev)
                    ins = pe.matmul(PH[0][0:1, 0:1], C["ones_col"][:], MACC[:],
                                    start=True, stop=True)
                    E_.i(ins, "pe")
                E_.pe(f_tot)
                tot_ev = E_.ev("pe")

                def f_m(dv, tot_ev=tot_ev, nel=nel):
                    dv.wait(tot_ev)
                    dv.tensor_scalar(QT[0:1, 0:1], PH[0][0:1, 0:1],
                                     1.0 / nel, 1e-5, ALU.mult, ALU.max)
                    ins = dv.reciprocal(QT[0:1, 1:2], QT[0:1, 0:1])
                    E_.i(ins, "dv")
                E_.dv(f_m)
                m_ev = E_.ev("dv")

                def f_mb(pe, m_ev=m_ev):
                    pe.wait(m_ev)
                    pe.matmul(PH[0][:, 0:1], C["ones_row"][:], QT[0:1, 0:1],
                              start=True, stop=True)
                    ins = pe.matmul(PH[0][:, 1:2], C["ones_row"][:],
                                    QT[0:1, 1:2], start=True, stop=True)
                    E_.i(ins, "pe")
                E_.pe(f_mb)
                mb_ev = E_.ev("pe")

                def f_mc(dv, mb_ev=mb_ev):
                    dv.wait(mb_ev)
                    dv.tensor_copy(MSC[0][:], PH[0][:, 0:1])
                    ins = dv.tensor_copy(MSC[1][:], PH[0][:, 1:2])
                    E_.i(ins, "dv")
                E_.dv(f_mc)
            with E_.block():
                for c in range(nch):
                    src_ap = wr[c]
                    if wi == 0:
                        dst_ap = WQ[:, :, c * 256:(c + 1) * 256]
                    else:
                        dst_ap = WQ[:, c * 2:(c + 1) * 2, :]

                    def f_lq2(sy, src_ap=src_ap, wi=wi):
                        sy.waitb("f3_wqf")
                        a = 8 if wi == 0 else 2
                        ins = sy.dma_start(
                            WQF[:].rearrange("p (x b) -> p x b", x=a), src_ap)
                        E_.i(ins, "qx", 16)
                    E_.sy(f_lq2)
                    lq_ev = E_.ev("qx")

                    def f_q(dv, lq_ev=lq_ev, dst_ap=dst_ap, wi=wi):
                        dv.wait(lq_ev)
                        for s4 in range(4):
                            sl = slice(s4 * 512, (s4 + 1) * 512)
                            if wi == 0:
                                dst = dst_ap[:, 2 * s4:2 * s4 + 2, :]
                                qv = QT2[:].rearrange("p (x b) -> p x b", x=2)
                            else:
                                dst = dst_ap[:, s4 // 2,
                                             (s4 % 2) * 512:(s4 % 2 + 1) * 512]
                                qv = QT2[:]
                            dv.tensor_scalar(QT[:], WQF[:, sl],
                                             MSC[1][:, 0:1], None, ALU.mult)
                            dv.tensor_scalar(QT2[:], QT[:], 0.5, None, ALU.is_ge)
                            dv.tensor_scalar(QT[:], QT[:], -0.5, None, ALU.is_le)
                            dv.tensor_tensor(QT2[:], QT2[:], QT[:], ALU.subtract)
                            ins = dv.tensor_scalar(dst, qv,
                                                   MSC[0][:, 0:1], None, ALU.mult)
                        E_.i(ins, "dv")
                        E_.setfree("f3_wqf", E_.ev("dv"))
                    E_.dv(f_q)
          wq_ev = E_.ev("dv")
          if qcache is not None:
            with E_.block():
                def f_sq(sy, wq_ev=wq_ev):
                    sy.wait(wq_ev)
                    ins = sy.dma_start(qcache[0], WQ1[:])
                    E_.i(ins, "qs", 16)
                    ins = sy.dma_start(qcache[1], WQ2[:])
                    E_.i(ins, "qs", 16)
                E_.sy(f_sq)

        for tt in range(8):
            with E_.block():
                def f_lx(sy, tt=tt):
                    sy.waitb("f3_xf")
                    ins = sy.dma_start(XF[:], xin[tt])
                    E_.i(ins, "qx", 16)
                E_.sy(f_lx)
                ld_ev = E_.ev("qx")
                xa_ev = emit_ln_tile(E_, C, st, XF, ld_ev, 512, LN_EPS,
                                     out_bf=XA, out_free_key="f3_xa")
                E_.setfree("f3_xf_ln", xa_ev)
                def wait_params(dv):
                    dv.waitb("f3_params")
                E_.dv(wait_params)
                xn1_ev = emit_ln_tile(E_, C, st, XA, xa_ev, 512, LN_EPS,
                                      out_bf=XN1, gamma=L1Gs, beta=L1Bs,
                                      out_free_key="f3_xn1",
                                      ones=C["ones_col_bf"][:])
                E_.setfree("f3_xa", E_.ev("pe"))

                for hc in range(H32):
                    par = hc % 2
                    ph = PH[par]

                    def f_up(pe, hc=hc, ph=ph, xn1_ev=xn1_ev, par=par):
                        pe.wait(xn1_ev)
                        pe.wait(wq_ev)
                        pe.waitb(f"f3_ph{par}")
                        for k in range(G):
                            mi = pe.matmul(ph[:],
                                           WQ1[:, k, hc * 128:(hc + 1) * 128],
                                           XN1[:, k, :],
                                           start=(k == 0), stop=(k == G - 1))
                        E_.i(mi, "pe")
                    E_.pe(f_up)
                    up_ev = E_.ev("pe")

                    def f_gl(ac, hc=hc, ph=ph, up_ev=up_ev, par=par):
                        ac.wait(up_ev)
                        ac.waitb("f3_hsb")
                        ins = ac.activation(HSB[:, hc, :], ph[:], AF.Gelu,
                                            bias=B1[:, hc:hc + 1])
                        E_.i(ins, "ac")
                        E_.setfree(f"f3_ph{par}", E_.ev("ac"))
                    E_.ac(f_gl)
                h_ev = E_.ev("ac")
                E_.setfree("f3_xn1", E_.ev("pe"))

                h2_ev = emit_ln_tile(E_, C, st, HSB, h_ev, 512, LN_EPS,
                                     out_bf=HSB, gamma=L2Gs, beta=L2Bs,
                                     gg=H32, kp="l3b",
                                     ones=C["ones_col_bf"][:])
                E_.setfree("f3_hsb", E_.ev("pe"))

                for dg in range(G):
                    pdb = PD[dg % 2]

                    def f_dn(pe, dg=dg, pdb=pdb, h2_ev=h2_ev):
                        pe.wait(h2_ev)
                        pe.waitb(f"f3_pd{dg % 2}")
                        for k in range(H32):
                            mi = pe.matmul(pdb[:],
                                           WQ2[:, k, dg * 128:(dg + 1) * 128],
                                           HSB[:, k, :],
                                           start=(k == 0), stop=(k == H32 - 1))
                        E_.i(mi, "pe")
                    E_.pe(f_dn)
                    dn_ev = E_.ev("pe")

                    def f_ep(dv, dg=dg, pdb=pdb, dn_ev=dn_ev):
                        dv.wait(dn_ev)
                        dv.tensor_scalar(QT[:], pdb[:], B2[:, dg:dg + 1],
                                         0.5, ALU.add, ALU.mult)
                        ins = dv.tensor_tensor(XF[:, dg, :], QT[:],
                                               XF[:, dg, :], ALU.add)
                        E_.i(ins, "dv")
                        E_.setfree(f"f3_pd{dg % 2}", E_.ev("dv"))
                    E_.dv(f_ep)
                xo_ev = E_.ev("dv")

                def f_st(sy, tt=tt, xo_ev=xo_ev):
                    sy.wait(xo_ev)
                    ins = sy.dma_start(xout[tt], XF[:])
                    E_.i(ins, "qs", 16)
                E_.sy(f_st)
                E_.setfree("f3_xf", E_.ev("qs"))



def emit_att0(E_, C, W, xin, xout):
    nc = E_.nc
    NT = 8
    wint_r = W["a0_int"]
    woutt_r = W["a0_outt"]

    with contextlib.ExitStack() as es:
        uid, sb, ps_ = _mk_alloc(nc, es)
        WIN = sb("a0_win", [128, G, 2048], BF16)
        WOUT = sb("a0_wout", [128, G, 1024], BF16)
        CW = sb("a0_cw_sb", [128, 40], F32)
        CB = sb("a0_cb_sb", [128, G], F32)
        APAR = sb("a0_apar_sb", [128, G], F32)
        ATH = sb("a0_ath", [128, G], F32)
        LNW = sb("a0_lnw_sb", [128, G], F32)
        LNB = sb("a0_lnb_sb", [128, G], F32)
        XF = [sb(f"a0_xf{i}", [128, G, 512], F32) for i in range(2)]
        XNF = [sb(f"a0_xnf{i}", [128, G, 512], F32) for i in range(2)]
        XNB = sb("a0_xnb", [128, G, 512], BF16)
        VB = [sb(f"a0_v{i}", [128, G, 512], F32) for i in range(2)]
        GB = [sb(f"a0_g{i}", [128, G, 512], BF16) for i in range(2)]
        VHL = sb("a0_vhl", [128, G, 2], F32)
        ONESW = sb("a0_ones", [128, 512], F32)
        ABC1 = sb("a0_abc", [128, 512], F32)
        CT = sb("a0_ct", [128, 512], F32)
        GV = sb("a0_gv", [128, 512], F32)
        YS = sb("a0_ys", [128, 512], F32)
        CARRY = sb("a0_carry", [128, G], F32)
        YB = sb("a0_yb", [128, G, 512], BF16)
        PH = [ps_(f"a0_ph{i}", [128, 512])
              for i in range(2)]
        PD = [ps_(f"a0_pd{i}", [128, 512])
              for i in range(2)]
        PMB = ps_("a0_pmb", [128, 512])
        PRBC = ps_("a0_prb", [128, 512])
        st = alloc_ln_scratch(es, nc, "a0_ln", 512, (PD[0], PD[1], PMB, PRBC))

        with E_.block():
            def f_lw(g_):
                ins = g_.dma_start(WIN[:], wint_r)
                E_.i(ins, "qg", 16)
                ins = g_.dma_start(WOUT[:], woutt_r)
                E_.i(ins, "qg", 16)
            E_.po(f_lw)
            wl_ev = E_.ev("qg")

            def f_lp(sy):
                for dst, key in ((CW, "a0_convw"), (CB, "a0_convb"),
                                 (APAR, "a0_a"), (LNW, "a0_lnw"),
                                 (LNB, "a0_lnb")):
                    ins = sy.dma_start(dst[:], W[key])
                    E_.i(ins, "qc", 16)
            E_.sy(f_lp)
            lp_ev = E_.ev("qc")
            E_.setfree("ln_gamma_ld", lp_ev)

            def f_pc(dv):
                dv.wait(lp_ev)
                dv.memset(ONESW[:], 1.0)
                dv.memset(VHL[:], 0.0)
                dv.memset(CARRY[:], 0.0)
                ins = dv.memset(CT[:], 0.0)
                E_.i(ins, "dv")
            E_.dv(f_pc)

            def f_th(ac):
                ac.wait(lp_ev)
                ins = ac.activation(ATH[:], APAR[:], AF.Tanh)
                E_.i(ins, "ac")
            E_.ac(f_th)

        for s_ in range(NT + 1):
            with E_.block():
                cur, prv = s_ % 2, (s_ + 1) % 2
                if s_ < NT:
                    def f_lx(sy, s_=s_, cur=cur):
                        sy.waitb(f"a0_xf{cur}")
                        ins = sy.dma_start(XF[cur][:], xin[s_])
                        E_.i(ins, "qx", 16)
                    E_.sy(f_lx)
                    ld_ev = E_.ev("qx")
                    xn_ev = emit_ln_tile(E_, C, st, XF[cur], ld_ev, 512,
                                         LN_EPS, out_bf=XNB,
                                         out_f32=XNF[cur])
                    for ec in range(16):
                        php = PH[ec % 2]

                        def f_pj(pe, ec=ec, php=php, xn_ev=xn_ev):
                            pe.wait(xn_ev)
                            pe.wait(wl_ev)
                            pe.waitb(f"a0_ph{ec % 2}")
                            for k in range(G):
                                mi = pe.matmul(
                                    php[:], WIN[:, k, ec * 128:(ec + 1) * 128],
                                    XNB[:, k, :], start=(k == 0),
                                    stop=(k == G - 1))
                            E_.i(mi, "pe")
                        E_.pe(f_pj)
                        pj_ev = E_.ev("pe")

                        def f_ev(ac, ec=ec, php=php, pj_ev=pj_ev, cur=cur):
                            ac.wait(pj_ev)
                            if ec < 8:
                                ins = ac.activation(GB[cur][:, ec, :], php[:],
                                                    AF.Sigmoid)
                            else:
                                ins = ac.activation(VB[cur][:, ec - 8, :],
                                                    php[:], AF.Copy)
                            ev = E_.i(ins, "ac")
                            E_.setfree(f"a0_ph{ec % 2}", ev)
                        E_.ac(f_ev)
                    pv_ev = E_.ev("ac")
                else:
                    def f_z(dv, cur=cur):
                        ins = dv.memset(VB[cur][:, :, 0:2], 0.0)
                        E_.i(ins, "dv")
                    E_.dv(f_z)
                    pv_ev = E_.ev("dv")

                if s_ == 0:
                    continue
                ptok0 = (s_ - 1) * 512
                for g in range(G):
                    def f_cv(dv, g=g, prv=prv, cur=cur, pv_ev=pv_ev, s_=s_):
                        dv.wait(pv_ev)
                        vp = VB[prv]
                        vc = VB[cur]
                        dv.tensor_scalar(CT[:, 0:2], VHL[:, g, :],
                                         CW[:, g * 5 + 0:g * 5 + 1], None,
                                         ALU.mult)
                        dv.tensor_scalar(CT[:, 2:512], vp[:, g, 0:510],
                                         CW[:, g * 5 + 0:g * 5 + 1], None,
                                         ALU.mult)
                        dv.scalar_tensor_tensor(CT[:, 0:1], VHL[:, g, 1:2],
                                                CW[:, g * 5 + 1:g * 5 + 2],
                                                CT[:, 0:1], ALU.mult, ALU.add)
                        dv.scalar_tensor_tensor(CT[:, 1:512], vp[:, g, 0:511],
                                                CW[:, g * 5 + 1:g * 5 + 2],
                                                CT[:, 1:512], ALU.mult, ALU.add)
                        dv.scalar_tensor_tensor(CT[:], vp[:, g, :],
                                                CW[:, g * 5 + 2:g * 5 + 3],
                                                CT[:], ALU.mult, ALU.add)
                        dv.scalar_tensor_tensor(CT[:, 0:511], vp[:, g, 1:512],
                                                CW[:, g * 5 + 3:g * 5 + 4],
                                                CT[:, 0:511], ALU.mult, ALU.add)
                        dv.scalar_tensor_tensor(CT[:, 511:512], vc[:, g, 0:1],
                                                CW[:, g * 5 + 3:g * 5 + 4],
                                                CT[:, 511:512], ALU.mult, ALU.add)
                        dv.scalar_tensor_tensor(CT[:, 0:510], vp[:, g, 2:512],
                                                CW[:, g * 5 + 4:g * 5 + 5],
                                                CT[:, 0:510], ALU.mult, ALU.add)
                        dv.scalar_tensor_tensor(CT[:, 510:512], vc[:, g, 0:2],
                                                CW[:, g * 5 + 4:g * 5 + 5],
                                                CT[:, 510:512], ALU.mult, ALU.add)
                        dv.tensor_scalar(CT[:], CT[:], CB[:, g:g + 1], None,
                                         ALU.add)
                        dv.tensor_tensor(GV[:], GB[prv][:, g, :], CT[:],
                                         ALU.mult)
                        dv.tensor_scalar(ABC1[:], ONESW[:], ATH[:, g:g + 1],
                                         None, ALU.mult)
                        dv.tensor_tensor_scan(YS[:], ABC1[:], GV[:],
                                              CARRY[:, g:g + 1], ALU.mult,
                                              ALU.add)
                        dv.tensor_copy(CARRY[:, g:g + 1], YS[:, 511:512])
                        ins = dv.tensor_copy(YB[:, g, :], YS[:])
                        E_.i(ins, "dv")
                        if g == G - 1:
                            ins = dv.tensor_copy(VHL[:], vp[:, :, 510:512])
                            E_.i(ins, "dv")
                    E_.dv(f_cv)
                yb_ev = E_.ev("dv")

                for dg in range(G):
                    pdb = PD[dg % 2]

                    def f_op(pe, dg=dg, pdb=pdb, yb_ev=yb_ev):
                        pe.wait(yb_ev)
                        pe.waitb(f"a0_pd{dg % 2}")
                        for k in range(G):
                            mi = pe.matmul(pdb[:],
                                           WOUT[:, k, dg * 128:(dg + 1) * 128],
                                           YB[:, k, :], start=(k == 0),
                                           stop=(k == G - 1))
                        E_.i(mi, "pe")
                    E_.pe(f_op)
                    op_ev = E_.ev("pe")

                    def f_rs(dv, dg=dg, pdb=pdb, prv=prv, op_ev=op_ev):
                        dv.wait(op_ev)
                        ins = dv.tensor_tensor(XNF[prv][:, dg, :],
                                               XNF[prv][:, dg, :], pdb[:],
                                               ALU.add)
                        ev = E_.i(ins, "dv")
                        E_.setfree(f"a0_pd{dg % 2}", ev)
                    E_.dv(f_rs)
                rs_ev = E_.ev("dv")

                ln2_ev = emit_ln_tile(E_, C, st, XNF[prv], rs_ev, 512, LN_EPS,
                                      out_f32=XNF[prv], gamma=LNW, beta=LNB,
                                      kp="a0l2")

                def f_add(dv, prv=prv, ln2_ev=ln2_ev):
                    dv.wait(ln2_ev)
                    for g in range(G):
                        ins = dv.tensor_tensor(XF[prv][:, g, :],
                                               XF[prv][:, g, :],
                                               XNF[prv][:, g, :], ALU.add)
                    E_.i(ins, "dv")
                E_.dv(f_add)
                xo_ev = E_.ev("dv")

                def f_st(sy, s_=s_, prv=prv, xo_ev=xo_ev):
                    sy.wait(xo_ev)
                    ins = sy.dma_start(xout[s_ - 1], XF[prv][:])
                    E_.i(ins, "qs", 16)
                E_.sy(f_st)
                E_.setfree(f"a0_xf{prv}", E_.ev("qs"))



def emit_att1(E_, C, W, xin, xout):
    nc = E_.nc
    NT = 8
    wq_r = W["a1_wqkv"]

    with contextlib.ExitStack() as es:
        uid, sb, ps_ = _mk_alloc(nc, es)
        KS = sb("a1_ks_sb", [128, G, T], BF16)
        VT = sb("a1_vt", [128, 32, 16, 65], BF16)
        XF = sb("a1_xf", [128, G, 512], F32)
        XNB = sb("a1_xnb", [128, G, 512], BF16)
        QS = XNB
        CTXF = sb("a1_ctxf", [128, G, 512], BF16)
        PSB = [sb(f"a1_p{i}", [128, 512], BF16) for i in range(3)]
        CTB = sb("a1_ctb", [128, 64], BF16)
        RC = sb("a1_rc", [128, 1], F32)
        QSC = sb("a1_qsc_sb", [128, G], F32)
        KSC = sb("a1_ksc_sb", [128, G], F32)
        QS8 = sb("a1_qs8", [128, G], F32)
        BZ = sb("a1_bz", [128, G], F32)
        WSL = [sb(f"a1_wsl{i}", [128, G, 128], BF16) for i in range(2)]
        PH = [ps_(f"a1_ph{i}", [128, 512])
              for i in range(3)]
        PC = [ps_(f"a1_pc{i}", [128, 512])
              for i in range(4)]
        PVT = ps_("a1_pvt", [128, 128], BF16)
        st = alloc_ln_scratch(es, nc, "a1_ln", 512,
                              (PC[0], PC[1], PC[2], PC[3]))

        wslot = [0]

        def load_wslice(col0, name):
            s_ = wslot[0] % 2
            wslot[0] += 1
            buf = WSL[s_]
            wfree = E_.free.get(f"a1_wsl{s_}")
            wsem = "qw" if s_ == 0 else "qv"

            def f_lw(g_, buf=buf, col0=col0, wfree=wfree, wsem=wsem):
                g_.wait(wfree)
                ins = g_.dma_start(buf[:], wq_r[col0 // 128])
                E_.i(ins, wsem, 16)
            E_.po(f_lw)
            return buf, s_, E_.ev(wsem)

        with E_.block():
            def f_lp(sy):
                for dst, key in ((QSC, "a1_qs"), (KSC, "a1_ks")):
                    ins = sy.dma_start(dst[:], W[key])
                    E_.i(ins, "qc", 16)
            E_.sy(f_lp)
            lp_ev = E_.ev("qc")

            def f_pc(dv):
                dv.wait(lp_ev)
                dv.memset(BZ[:], 0.0)
                dv.tensor_scalar(QS8[:], QSC[:], 0.125, None, ALU.mult)
                ins = dv.memset(VT[:, :, :, 64:65], 1.0)
                E_.i(ins, "dv")
            E_.dv(f_pc)
            E_.setfree("ln_gamma_ld", E_.ev("dv"))

        for tt in range(NT):
            with E_.block():
                tok0 = tt * 512

                def f_lx(sy, tt=tt):
                    sy.waitb("a1_xf")
                    ins = sy.dma_start(XF[:], xin[tt])
                    E_.i(ins, "qx", 16)
                E_.sy(f_lx)
                ld_ev = E_.ev("qx")
                xn_ev = emit_ln_tile(E_, C, st, XF, ld_ev, 512, LN_EPS,
                                     out_bf=XNB)
                for ec in range(G):
                    php = PH[ec % 2]
                    wb, ws, wev = load_wslice(1024 + ec * 128, "k")

                    def f_km(pe, ec=ec, php=php, xn_ev=xn_ev, wb=wb, wev=wev):
                        pe.wait(xn_ev)
                        pe.wait(wev)
                        pe.waitb(f"a1_ph{ec % 2}")
                        for kk in range(G):
                            mi = pe.matmul(php[:], wb[:, kk, :],
                                           XNB[:, kk, :], start=(kk == 0),
                                           stop=(kk == G - 1))
                        E_.i(mi, "pe")
                    E_.pe(f_km)
                    km_ev = E_.ev("pe")
                    E_.setfree(f"a1_wsl{ws}", km_ev)

                    def f_ke(ac, ec=ec, php=php, km_ev=km_ev):
                        ac.wait(km_ev)
                        ins = ac.activation(XF[:, ec, :], php[:], AF.Copy)
                        ev = E_.i(ins, "ac")
                        E_.setfree(f"a1_ph{ec % 2}", ev)
                    E_.ac(f_ke)
                kr_ev = E_.ev("ac")
                emit_ln_tile(E_, C, st, XF, kr_ev, 512, 1e-6, out_bf=KS,
                             out_off=tok0, gamma=KSC, beta=BZ, kp="a1k")
                E_.setfree("a1_xf", E_.ev("dv"))
                for ec in range(G):
                    php = PH[ec % 2]
                    wb, ws, wev = load_wslice(2048 + ec * 128, "v")

                    def f_vm(pe, ec=ec, php=php, xn_ev=xn_ev, wb=wb, wev=wev):
                        pe.wait(xn_ev)
                        pe.wait(wev)
                        pe.waitb(f"a1_ph{ec % 2}")
                        for kk in range(G):
                            mi = pe.matmul(
                                php[:], wb[:, kk, :],
                                XNB[:, kk, :], start=(kk == 0),
                                stop=(kk == G - 1))
                        E_.i(mi, "pe")
                    E_.pe(f_vm)
                    vm_ev = E_.ev("pe")
                    E_.setfree(f"a1_wsl{ws}", vm_ev)

                    def f_ve(ac, ec=ec, php=php, vm_ev=vm_ev):
                        ac.wait(vm_ev)
                        ac.waitb("a1_psb0")
                        ins = ac.activation(PSB[0][:], php[:], AF.Copy)
                        ev = E_.i(ins, "ac")
                        E_.setfree(f"a1_ph{ec % 2}", ev)
                    E_.ac(f_ve)
                    ve_ev = E_.ev("ac")

                    for tb in range(4):
                        def f_tr(pe, tb=tb, ve_ev=ve_ev):
                            pe.wait(ve_ev)
                            pe.waitb("a1_pvt")
                            ins = pe.transpose(PVT[:],
                                               PSB[0][:, tb * 128:(tb + 1) * 128],
                                               C["ident_bf16"][:])
                            E_.i(ins, "pe")
                        E_.pe(f_tr)
                        tr_ev = E_.ev("pe")

                        def f_tc(dv, tb=tb, ec=ec, tt=tt, tr_ev=tr_ev):
                            dv.wait(tr_ev)
                            tg = tt * 4 + tb
                            dv.tensor_copy(VT[:, tg, 2 * ec, 0:64],
                                           PVT[:, 0:64])
                            ins = dv.tensor_copy(VT[:, tg, 2 * ec + 1, 0:64],
                                                 PVT[:, 64:128])
                            E_.i(ins, "dv")
                            E_.setfree("a1_pvt", E_.ev("dv"))
                        E_.dv(f_tc)
                    E_.setfree("a1_psb0", E_.ev("pe"))

        wo_r = W["a1_wout"]
        for qb in range(NT):
            with E_.block():
                def f_lx2(sy, qb=qb):
                    sy.waitb("a1_xf")
                    ins = sy.dma_start(XF[:], xin[qb])
                    E_.i(ins, "qx", 16)
                E_.sy(f_lx2)
                ld_ev = E_.ev("qx")
                xn_ev = emit_ln_tile(E_, C, st, XF, ld_ev, 512, LN_EPS,
                                     out_bf=XNB)
                for ec in range(G):
                    php = PH[ec % 2]
                    wb, ws, wev = load_wslice(ec * 128, "q")

                    def f_qm(pe, ec=ec, php=php, xn_ev=xn_ev, wb=wb, wev=wev):
                        pe.wait(xn_ev)
                        pe.wait(wev)
                        pe.waitb(f"a1_ph{ec % 2}")
                        for kk in range(G):
                            mi = pe.matmul(php[:], wb[:, kk, :],
                                           XNB[:, kk, :], start=(kk == 0),
                                           stop=(kk == G - 1))
                        E_.i(mi, "pe")
                    E_.pe(f_qm)
                    qm_ev = E_.ev("pe")
                    E_.setfree(f"a1_wsl{ws}", qm_ev)

                    def f_qe(ac, ec=ec, php=php, qm_ev=qm_ev):
                        ac.wait(qm_ev)
                        ins = ac.activation(XF[:, ec, :], php[:], AF.Copy)
                        ev = E_.i(ins, "ac")
                        E_.setfree(f"a1_ph{ec % 2}", ev)
                    E_.ac(f_qe)
                qr_ev = E_.ev("ac")
                qs_ev = emit_ln_tile(E_, C, st, XF, qr_ev, 512, 1e-6,
                                     out_bf=QS, gamma=QS8, beta=BZ, kp="a1k")
                E_.setfree("a1_xf", E_.ev("dv"))

                nk = 4 * qb + 4
                for h in range(16):
                    hp = 64 * (h % 2)
                    hg = h // 2
                    for kg in range(0, nk, 3):
                        kts = list(range(kg, min(kg + 3, nk)))

                        def f_scg(pe, kts=kts, hp=hp, hg=hg, qs_ev=qs_ev):
                            pe.wait(qs_ev)
                            for kt in kts:
                                pe.waitb(f"a1_ph{kt % 3}")
                                mi = pe.matmul(
                                    PH[kt % 3][:],
                                    KS[hp:hp + 64, hg, kt * 128:(kt + 1) * 128],
                                    QS[hp:hp + 64, hg, :],
                                    start=True, stop=True)
                                E_.i(mi, "pe")
                        E_.pe(f_scg)
                        scb = E_.cnt["pe"] - len(kts)
                        masked = [kt for kt in kts if kt - 4 * qb >= 0]
                        if masked:
                            def f_mkg(dv, kts=kts, masked=masked, scb=scb,
                                      qb=qb):
                                for kt in masked:
                                    dv.wait(("pe", scb + 1 + kts.index(kt)))
                                    ins = dv.tensor_tensor(
                                        PH[kt % 3][:], PH[kt % 3][:],
                                        C[f"cmask{kt - 4 * qb}"][:], ALU.add)
                                    E_.i(ins, "dv")
                            E_.dv(f_mkg)
                        mkb = E_.cnt["dv"] - len(masked)

                        def f_exg(ac, kts=kts, masked=masked, scb=scb,
                                  mkb=mkb):
                            for i, kt in enumerate(kts):
                                if kt in masked:
                                    ac.wait(("dv", mkb + 1 + masked.index(kt)))
                                else:
                                    ac.wait(("pe", scb + 1 + i))
                                ac.waitb(f"a1_psb{kt % 3}")
                                ins = ac.activation(PSB[kt % 3][:],
                                                    PH[kt % 3][:], AF.Exp,
                                                    bias=-40.0)
                                ev = E_.i(ins, "ac")
                                E_.setfree(f"a1_ph{kt % 3}", ev)
                        E_.ac(f_exg)
                        exb = E_.cnt["ac"] - len(kts)

                        def f_pvg(pe, kts=kts, h=h, nk=nk, exb=exb):
                            for i, kt in enumerate(kts):
                                pe.wait(("ac", exb + 1 + i))
                                for q4 in range(4):
                                    mi = pe.matmul(
                                        PC[q4][0:128, 0:65],
                                        PSB[kt % 3][:, q4 * 128:(q4 + 1) * 128],
                                        VT[:, kt, h, :],
                                        start=(kt == 0), stop=(kt == nk - 1))
                                    E_.i(mi, "pe")
                                E_.setfree(f"a1_psb{kt % 3}", E_.ev("pe"))
                        E_.pe(f_pvg)
                    pv_ev = E_.ev("pe")

                    for q4 in range(4):
                        def f_nr(dv, q4=q4, pv_ev=pv_ev):
                            dv.wait(pv_ev)
                            dv.waitb("a1_rc")
                            ins = dv.reciprocal(RC[:], PC[q4][:, 64:65])
                            E_.i(ins, "dv")
                        E_.dv(f_nr)
                        nr_ev = E_.ev("dv")

                        def f_sc2(dv, q4=q4, nr_ev=nr_ev):
                            dv.waitb("a1_ctb")
                            ins = dv.tensor_scalar(CTB[:], PC[q4][0:128, 0:64],
                                                   RC[:, 0:1], None, ALU.mult)
                            E_.i(ins, "dv")
                        E_.dv(f_sc2)
                        s2_ev = E_.ev("dv")
                        E_.setfree("a1_rc", s2_ev)

                        def f_tr2(pe, q4=q4, s2_ev=s2_ev):
                            pe.wait(s2_ev)
                            pe.waitb("a1_pvt")
                            ins = pe.transpose(PVT[0:64, :], CTB[:],
                                               C["ident_bf16"][:])
                            E_.i(ins, "pe")
                        E_.pe(f_tr2)
                        t2_ev = E_.ev("pe")
                        E_.setfree("a1_ctb", t2_ev)

                        def f_cc(dv, q4=q4, h=h, hp=hp, hg=hg, t2_ev=t2_ev):
                            dv.wait(t2_ev)
                            ins = dv.tensor_copy(
                                CTXF[hp:hp + 64, hg, q4 * 128:(q4 + 1) * 128],
                                PVT[0:64, :])
                            E_.i(ins, "dv")
                            E_.setfree("a1_pvt", E_.ev("dv"))
                        E_.dv(f_cc)
                ctx_ev = E_.ev("dv")

                def f_lx3(sy, qb=qb, ctx_ev=ctx_ev):
                    sy.waitb("a1_xf")
                    ins = sy.dma_start(XF[:], xin[qb])
                    E_.i(ins, "qx", 16)
                E_.sy(f_lx3)
                ld3_ev = E_.ev("qx")
                for dg in range(G):
                    php = PH[dg % 2]
                    s_ = wslot[0] % 2
                    wslot[0] += 1
                    wob = WSL[s_]
                    wofree = E_.free.get(f"a1_wsl{s_}")
                    wosem = "qw" if s_ == 0 else "qv"

                    def f_low(g_, wob=wob, dg=dg, wofree=wofree, wosem=wosem):
                        g_.wait(wofree)
                        ins = g_.dma_start(wob[:], wo_r[dg])
                        E_.i(ins, wosem, 16)
                    E_.po(f_low)
                    wo_ev = E_.ev(wosem)

                    def f_om(pe, dg=dg, php=php, ctx_ev=ctx_ev, wob=wob,
                             wo_ev=wo_ev):
                        pe.wait(ctx_ev)
                        pe.wait(wo_ev)
                        pe.waitb(f"a1_ph{dg % 2}")
                        for kk in range(G):
                            mi = pe.matmul(php[:], wob[:, kk, :],
                                           CTXF[:, kk, :], start=(kk == 0),
                                           stop=(kk == G - 1))
                        E_.i(mi, "pe")
                    E_.pe(f_om)
                    om_ev = E_.ev("pe")
                    E_.setfree(f"a1_wsl{s_}", om_ev)

                    def f_ra(dv, dg=dg, php=php, om_ev=om_ev, ld3_ev=ld3_ev):
                        dv.wait(om_ev)
                        dv.wait(ld3_ev)
                        ins = dv.tensor_tensor(XF[:, dg, :], XF[:, dg, :],
                                               php[:], ALU.add)
                        ev = E_.i(ins, "dv")
                        E_.setfree(f"a1_ph{dg % 2}", ev)
                    E_.dv(f_ra)
                xo_ev = E_.ev("dv")

                def f_st(sy, qb=qb, xo_ev=xo_ev):
                    sy.wait(xo_ev)
                    ins = sy.dma_start(xout[qb], XF[:])
                    E_.i(ins, "qs", 16)
                E_.sy(f_st)
                E_.setfree("a1_xf", E_.ev("qs"))



def emit_att2(E_, C, W, xin, xout):
    nc = E_.nc
    NB = 8
    wi_r = W["a2_int"]
    wo_r = W["a2_outt"]
    slopes = [2.0 ** (-(h + 1) * 0.5) for h in range(16)]

    with contextlib.ExitStack() as es:
        uid, sb, ps_ = _mk_alloc(nc, es)
        KC = [sb(f"a2_kc{i}", [128, G, 512], BF16) for i in range(2)]
        VT = [sb(f"a2_vt{i}", [128, 4, 16, 65], BF16) for i in range(2)]
        XF = sb("a2_xf", [128, G, 512], F32)
        XNB = sb("a2_xnb", [128, G, 512], BF16)
        QSB = sb("a2_qsb", [128, G, 512], BF16)
        CTXF = sb("a2_ctxf", [128, G, 512], BF16)
        PSB = [sb(f"a2_p{i}", [128, 512], BF16) for i in range(3)]
        CTB = sb("a2_ctb", [128, 64], BF16)
        RC = sb("a2_rc", [128, 1], F32)
        WSL = [sb(f"a2_wsl{i}", [128, G, 128], BF16) for i in range(2)]
        PH = [ps_(f"a2_ph{i}", [128, 512])
              for i in range(3)]
        PC = [ps_(f"a2_pc{i}", [128, 512])
              for i in range(4)]
        PVT = ps_("a2_pvt", [128, 128], BF16)
        st = alloc_ln_scratch(es, nc, "a2_ln", 512,
                              (PC[0], PC[1], PC[2], PC[3]))
        wslot = [0]

        def load_wslice(rsrc, col0):
            s_ = wslot[0] % 2
            wslot[0] += 1
            buf = WSL[s_]
            wfree = E_.free.get(f"a2_wsl{s_}")
            wsem = "qw" if s_ == 0 else "qv"

            def f_lw(g_, buf=buf, rsrc=rsrc, col0=col0, wfree=wfree, wsem=wsem):
                g_.wait(wfree)
                ins = g_.dma_start(buf[:], rsrc[col0 // 128])
                E_.i(ins, wsem, 16)
            E_.po(f_lw)
            return buf, s_, E_.ev(wsem)

        with E_.block():
            def f_pc(dv):
                dv.memset(VT[0][:, :, :, 64:65], 1.0)
                ins = dv.memset(VT[1][:, :, :, 64:65], 1.0)
                E_.i(ins, "dv")
            E_.dv(f_pc)

        for blk in range(NB):
            with E_.block():
                cur, prv = blk % 2, (blk + 1) % 2
                def f_lx(sy, blk=blk):
                    sy.waitb("a2_xf")
                    ins = sy.dma_start(XF[:], xin[blk])
                    E_.i(ins, "qx", 16)
                E_.sy(f_lx)
                ld_ev = E_.ev("qx")
                xn_ev = emit_ln_tile(E_, C, st, XF, ld_ev, 512, LN_EPS,
                                     out_bf=XNB)
                E_.setfree("a2_xf", xn_ev)
                for ec in range(24):
                    php = PH[ec % 2]
                    wb, ws, wev = load_wslice(wi_r, ec * 128)

                    def f_pj(pe, php=php, wb=wb, wev=wev, xn_ev=xn_ev,
                             ec=ec):
                        pe.wait(xn_ev)
                        pe.wait(wev)
                        pe.waitb(f"a2_ph{ec % 2}")
                        for kk in range(G):
                            mi = pe.matmul(php[:], wb[:, kk, :], XNB[:, kk, :],
                                           start=(kk == 0), stop=(kk == G - 1))
                        E_.i(mi, "pe")
                    E_.pe(f_pj)
                    pj_ev = E_.ev("pe")
                    E_.setfree(f"a2_wsl{ws}", pj_ev)

                    if ec < 8:
                        def f_ev(ac, ec=ec, php=php, pj_ev=pj_ev):
                            ac.wait(pj_ev)
                            ins = ac.activation(QSB[:, ec, :], php[:], AF.Copy,
                                                scale=0.125)
                            ev = E_.i(ins, "ac")
                            E_.setfree(f"a2_ph{ec % 2}", ev)
                        E_.ac(f_ev)
                    elif ec < 16:
                        def f_ev(ac, ec=ec, php=php, pj_ev=pj_ev, cur=cur):
                            ac.wait(pj_ev)
                            ins = ac.activation(KC[cur][:, ec - 8, :], php[:],
                                                AF.Copy)
                            ev = E_.i(ins, "ac")
                            E_.setfree(f"a2_ph{ec % 2}", ev)
                        E_.ac(f_ev)
                    else:
                        def f_ev(ac, ec=ec, php=php, pj_ev=pj_ev):
                            ac.wait(pj_ev)
                            ac.waitb("a2_psb0")
                            ins = ac.activation(PSB[0][:], php[:], AF.Copy)
                            ev = E_.i(ins, "ac")
                            E_.setfree(f"a2_ph{ec % 2}", ev)
                        E_.ac(f_ev)
                        ve_ev = E_.ev("ac")
                        vc = ec - 16
                        for tb in range(4):
                            def f_tr(pe, tb=tb, ve_ev=ve_ev):
                                pe.wait(ve_ev)
                                pe.waitb("a2_pvt")
                                ins = pe.transpose(
                                    PVT[:], PSB[0][:, tb * 128:(tb + 1) * 128],
                                    C["ident_bf16"][:])
                                E_.i(ins, "pe")
                            E_.pe(f_tr)
                            tr_ev = E_.ev("pe")

                            def f_tc(dv, tb=tb, vc=vc, cur=cur, tr_ev=tr_ev):
                                dv.wait(tr_ev)
                                dv.tensor_copy(VT[cur][:, tb, 2 * vc, 0:64],
                                               PVT[:, 0:64])
                                ins = dv.tensor_copy(
                                    VT[cur][:, tb, 2 * vc + 1, 0:64],
                                    PVT[:, 64:128])
                                E_.i(ins, "dv")
                                E_.setfree("a2_pvt", E_.ev("dv"))
                            E_.dv(f_tc)
                        E_.setfree("a2_psb0", E_.ev("pe"))
                qkv_ev = [E_.ev("ac"), E_.ev("dv")]

                kts_all = list(range(4, 8) if blk == 0 else range(8))
                for h in range(16):
                    hp, hg = 64 * (h % 2), h // 2
                    for kg in range(0, len(kts_all), 3):
                        kts = kts_all[kg:kg + 3]

                        def f_scg(pe, kts=kts, hp=hp, hg=hg, qkv_ev=qkv_ev,
                                  cur=cur, prv=prv):
                            pe.wait(qkv_ev)
                            for kt in kts:
                                ksrc = KC[cur] if kt >= 4 else KC[prv]
                                pe.waitb(f"a2_ph{kt % 3}")
                                mi = pe.matmul(
                                    PH[kt % 3][:],
                                    ksrc[hp:hp + 64, hg,
                                         (kt % 4) * 128:(kt % 4 + 1) * 128],
                                    QSB[hp:hp + 64, hg, :], start=True,
                                    stop=True)
                                E_.i(mi, "pe")
                        E_.pe(f_scg)
                        scb = E_.cnt["pe"] - len(kts)

                        def f_big(dv, kts=kts, h=h, blk=blk, scb=scb):
                            for i, kt in enumerate(kts):
                                dv.wait(("pe", scb + 1 + i))
                                dv.scalar_tensor_tensor(
                                    PH[kt % 3][:], C[f"l2delta{kt}"][:],
                                    slopes[h], PH[kt % 3][:], ALU.mult, ALU.add)
                                if blk == 0:
                                    dv.scalar_tensor_tensor(
                                        PH[kt % 3][:], C["qidx"][:], -slopes[h],
                                        PH[kt % 3][:], ALU.mult, ALU.add)
                                ins = dv.tensor_tensor(
                                    PH[kt % 3][:], PH[kt % 3][:],
                                    C[f"l2mask{kt}"][:], ALU.add)
                                E_.i(ins, "dv")
                        E_.dv(f_big)
                        per = 3 if blk == 0 else 2
                        bib = E_.cnt["dv"] - per * len(kts)

                        def f_exg(ac, kts=kts, h=h, blk=blk, bib=bib, per=per):
                            for i, kt in enumerate(kts):
                                ac.wait(("dv", bib + per * (i + 1)))
                                ac.waitb(f"a2_psb{kt % 3}")
                                bias = 0.0 if blk == 0 else -512.0 * slopes[h]
                                ins = ac.activation(PSB[kt % 3][:],
                                                    PH[kt % 3][:], AF.Exp,
                                                    bias=bias)
                                ev = E_.i(ins, "ac")
                                E_.setfree(f"a2_ph{kt % 3}", ev)
                        E_.ac(f_exg)
                        exb = E_.cnt["ac"] - len(kts)

                        def f_pvg(pe, kts=kts, h=h, exb=exb, cur=cur, prv=prv,
                                  kts_all=kts_all):
                            for i, kt in enumerate(kts):
                                vsrc = VT[cur] if kt >= 4 else VT[prv]
                                pe.wait(("ac", exb + 1 + i))
                                for q4 in range(4):
                                    mi = pe.matmul(
                                        PC[q4][0:128, 0:65],
                                        PSB[kt % 3][:, q4 * 128:(q4 + 1) * 128],
                                        vsrc[:, kt % 4, h, :],
                                        start=(kt == kts_all[0]),
                                        stop=(kt == 7))
                                    E_.i(mi, "pe")
                                E_.setfree(f"a2_psb{kt % 3}", E_.ev("pe"))
                        E_.pe(f_pvg)
                    pv_ev = E_.ev("pe")

                    for q4 in range(4):
                        def f_nr(dv, q4=q4, pv_ev=pv_ev):
                            dv.wait(pv_ev)
                            dv.waitb("a2_rc")
                            ins = dv.reciprocal(RC[:], PC[q4][:, 64:65])
                            E_.i(ins, "dv")
                        E_.dv(f_nr)
                        nr_ev = E_.ev("dv")

                        def f_sc2(dv, q4=q4, nr_ev=nr_ev):
                            dv.waitb("a2_ctb")
                            ins = dv.tensor_scalar(CTB[:], PC[q4][0:128, 0:64],
                                                   RC[:, 0:1], None, ALU.mult)
                            E_.i(ins, "dv")
                        E_.dv(f_sc2)
                        s2_ev = E_.ev("dv")
                        E_.setfree("a2_rc", s2_ev)

                        def f_tr2(pe, q4=q4, s2_ev=s2_ev):
                            pe.wait(s2_ev)
                            pe.waitb("a2_pvt")
                            ins = pe.transpose(PVT[0:64, :], CTB[:],
                                               C["ident_bf16"][:])
                            E_.i(ins, "pe")
                        E_.pe(f_tr2)
                        t2_ev = E_.ev("pe")
                        E_.setfree("a2_ctb", t2_ev)

                        def f_cc(dv, q4=q4, hp=hp, hg=hg, t2_ev=t2_ev):
                            dv.wait(t2_ev)
                            ins = dv.tensor_copy(
                                CTXF[hp:hp + 64, hg, q4 * 128:(q4 + 1) * 128],
                                PVT[0:64, :])
                            E_.i(ins, "dv")
                            E_.setfree("a2_pvt", E_.ev("dv"))
                        E_.dv(f_cc)
                ctx_ev = E_.ev("dv")

                def f_lx3(sy, blk=blk, ctx_ev=ctx_ev):
                    sy.waitb("a2_xf")
                    ins = sy.dma_start(XF[:], xin[blk])
                    E_.i(ins, "qx", 16)
                E_.sy(f_lx3)
                ld3_ev = E_.ev("qx")
                for dg in range(G):
                    php = PH[dg % 2]
                    wb, ws, wev = load_wslice(wo_r, dg * 128)

                    def f_om(pe, dg=dg, php=php, ctx_ev=ctx_ev, wb=wb,
                             wev=wev):
                        pe.wait(ctx_ev)
                        pe.wait(wev)
                        pe.waitb(f"a2_ph{dg % 2}")
                        for kk in range(G):
                            mi = pe.matmul(php[:], wb[:, kk, :],
                                           CTXF[:, kk, :], start=(kk == 0),
                                           stop=(kk == G - 1))
                        E_.i(mi, "pe")
                    E_.pe(f_om)
                    om_ev = E_.ev("pe")
                    E_.setfree(f"a2_wsl{ws}", om_ev)

                    def f_ra(dv, dg=dg, php=php, om_ev=om_ev, ld3_ev=ld3_ev):
                        dv.wait(om_ev)
                        dv.wait(ld3_ev)
                        ins = dv.tensor_tensor(XF[:, dg, :], XF[:, dg, :],
                                               php[:], ALU.add)
                        ev = E_.i(ins, "dv")
                        E_.setfree(f"a2_ph{dg % 2}", ev)
                    E_.dv(f_ra)
                xo_ev = E_.ev("dv")

                def f_st(sy, blk=blk, xo_ev=xo_ev):
                    sy.wait(xo_ev)
                    ins = sy.dma_start(xout[blk], XF[:])
                    E_.i(ins, "qs", 16)
                E_.sy(f_st)
                E_.setfree("a2_xf", E_.ev("qs"))



def emit_att3(E_, C, W, xin, xout):
    nc = E_.nc
    CH = 256
    NCH = T // CH
    G16 = 16
    wi_r = W["a3_int"]
    wo_r = W["a3_outt"]
    xp_r = W["a3_xpt"]

    with contextlib.ExitStack() as es:
        uid, sb, ps_ = _mk_alloc(nc, es)
        WOP = sb("a3_wop", [128, G16, 1024], BF16)
        WXP = sb("a3_wxp", [128, G16, 96], BF16)
        WDT = sb("a3_wdt", [64, 2048], BF16)
        WSL = [sb(f"a3_wsl{i}", [128, G, 128], BF16) for i in range(2)]
        NW = sb("a3_nw_sb", [128, G], F32)
        CW = sb("a3_cw2", [128, 64], F32)
        CB = sb("a3_cb2", [128, G16], F32)
        DTB = sb("a3_dtb_sb", [128, G16], F32)
        DP = sb("a3_dp", [128, G16], F32)
        AN = sb("a3_an", [128, 256], F32)
        XF = sb("a3_xf2", [128, G, CH], F32)
        XNF = XF
        HINB = sb("a3_hinb", [128, G, CH], BF16)
        XMF = sb("a3_xmf", [128, G16, CH], F32)
        XCF = sb("a3_xcf", [128, G16, CH], F32)
        XCB = sb("a3_xcb", [128, G16, CH], BF16)
        SZ = sb("a3_sz", [128, G16, CH], F32)
        DT = XMF
        DTR = sb("a3_dtr", [64, CH], BF16)
        BR = sb("a3_br", [16, CH], F32)
        CR = sb("a3_cr", [16, CH], F32)
        BBC = sb("a3_bbc", [128, CH, 16], F32)
        CBC = sb("a3_cbc", [128, CH, 16], F32)
        DAB = sb("a3_dab", [128, CH, 16], F32)
        DBU = sb("a3_dbu", [128, CH, 16], F32)
        HW = DBU
        YG = sb("a3_yg", [128, CH], F32)
        UT = sb("a3_ut", [128, CH], F32)
        Y2B = sb("a3_y2b", [128, G16, CH], BF16)
        CAR = sb("a3_car", [128, G16, 16], F32)
        XH = sb("a3_xh", [128, G16, 3], F32)
        PH = [ps_(f"a3_ph{i}", [128, 512])
              for i in range(2)]
        PC = [ps_(f"a3_pc{i}", [128, 512])
              for i in range(4)]
        st = alloc_ln_scratch(es, nc, "a3_ln", CH,
                              (PC[0], PC[1], PC[2], PC[3]))
        wslot = [0]

        def load_wslice(rsrc, col0):
            s_ = wslot[0] % 2
            wslot[0] += 1
            buf = WSL[s_]
            wfree = E_.free.get(f"a3_wsl{s_}")
            wsem = "qw" if s_ == 0 else "qv"

            def f_lw(g_, buf=buf, rsrc=rsrc, col0=col0, wfree=wfree, wsem=wsem):
                g_.wait(wfree)
                ins = g_.dma_start(buf[:], rsrc[col0 // 128])
                E_.i(ins, wsem, 16)
            E_.po(f_lw)
            return buf, s_, E_.ev(wsem)

        with E_.block():
            def f_lw0(g_):
                ins = g_.dma_start(WOP[:], wo_r)
                E_.i(ins, "qg", 16)
                ins = g_.dma_start(WXP[:], xp_r)
                E_.i(ins, "qg", 16)
                ins = g_.dma_start(WDT[:], W["a3_dtpt"])
                E_.i(ins, "qg", 16)
            E_.po(f_lw0)
            w0_ev = E_.ev("qg")

            def f_lp(sy):
                for dst, key in ((NW, "a3_nw"), (CW, "a3_convw"),
                                 (CB, "a3_convb"), (DTB, "a3_dtb"),
                                 (DP, "a3_d"), (AN, "a3_alog")):
                    ins = sy.dma_start(dst[:], W[key])
                    E_.i(ins, "qc", 16)
            E_.sy(f_lp)
            lp_ev = E_.ev("qc")

            def f_an(ac, lp_ev=lp_ev):
                ac.wait(lp_ev)
                ins = ac.activation(AN[:], AN[:], AF.Exp)
                E_.i(ins, "ac")
            E_.ac(f_an)
            an_ev = E_.ev("ac")

            def f_an2(dv, an_ev=an_ev):
                dv.wait(an_ev)
                dv.tensor_scalar(AN[:], AN[:], -1.0, None, ALU.mult)
                dv.memset(CAR[:], 0.0)
                ins = dv.memset(XH[:], 0.0)
                E_.i(ins, "dv")
            E_.dv(f_an2)

        ANv = AN[:].rearrange("p (g n) -> p g n", g=16)
        for ci in range(NCH):
            with E_.block():
                tok0 = ci * CH

                def f_lx(sy, ci=ci):
                    sy.waitb("a3_xf")
                    off = (ci % 2) * CH
                    ins = sy.dma_start(XF[:], xin[ci // 2][:, :, off:off + CH])
                    E_.i(ins, "qx", 16)
                E_.sy(f_lx)
                ld_ev = E_.ev("qx")
                xn_ev = emit_ln_tile(E_, C, st, XF, ld_ev, CH, LN_EPS,
                                     out_f32=XNF)

                for g in range(G):
                    par = g % 2

                    def f_sq(ac, g=g, par=par, xn_ev=xn_ev):
                        ac.wait(xn_ev)
                        ac.waitb(f"a3_sq{par}")
                        ins = ac.activation(st["SQ"][:, par, :], XNF[:, g, :],
                                            AF.Square)
                        E_.i(ins, "ac")
                    E_.ac(f_sq)
                    sq_ev = E_.ev("ac")

                    def f_s1(pe, g=g, par=par, sq_ev=sq_ev):
                        pe.wait(sq_ev)
                        if g == 0:
                            pe.waitb("a3_ssq")
                        mi = pe.matmul(st["S1"], C["ones_col"][:],
                                       st["SQ"][:, par, :], start=(g == 0),
                                       stop=(g == G - 1))
                        ev = E_.i(mi, "pe")
                        E_.setfree(f"a3_sq{par}", ev)
                    E_.pe(f_s1)
                ssq_ev = E_.ev("pe")

                def f_rms(ac, ssq_ev=ssq_ev):
                    ac.wait(ssq_ev)
                    ac.activation(st["M"][:], st["S1"], AF.Sqrt)
                    ins = ac.activation(st["M"][:], st["M"][:], AF.Copy,
                                        scale=1.0 / 32.0)
                    E_.i(ins, "ac")
                E_.ac(f_rms)
                rms_ev = E_.ev("ac")

                def f_ri(dv, rms_ev=rms_ev):
                    dv.wait(rms_ev)
                    dv.tensor_scalar(st["M"][:], st["M"][:], 1e-5, None,
                                     ALU.add)
                    ins = dv.reciprocal(st["M"][:], st["M"][:])
                    E_.i(ins, "dv")
                E_.dv(f_ri)
                ri_ev = E_.ev("dv")
                E_.setfree("a3_ssq", ri_ev)

                def f_rb(pe, ri_ev=ri_ev):
                    pe.wait(ri_ev)
                    pe.waitb("a3_rb")
                    ins = pe.matmul(st["RB"], C["ones_row"][:], st["M"][:],
                                    start=True, stop=True)
                    E_.i(ins, "pe")
                E_.pe(f_rb)
                rb_ev = E_.ev("pe")

                def f_hin(dv, rb_ev=rb_ev):
                    dv.wait(rb_ev)
                    dv.waitb("ln_gamma_ld")
                    dv.waitb("a3_hinb")
                    for g in range(G):
                        dv.tensor_tensor(st["T1"][:], XNF[:, g, :], st["RB"],
                                         ALU.mult)
                        ins = dv.tensor_scalar(HINB[:, g, :], st["T1"][:],
                                               NW[:, g:g + 1], None, ALU.mult)
                    E_.i(ins, "dv")
                E_.dv(f_hin)
                hin_ev = E_.ev("dv")
                E_.setfree("a3_rb", hin_ev)
                E_.setfree("a3_xf", hin_ev)

                for ec in range(32):
                    php = PH[ec % 2]
                    wb, ws, wev = load_wslice(wi_r, ec * 128)

                    def f_ip(pe, php=php, wb=wb, wev=wev, hin_ev=hin_ev,
                             ec=ec):
                        pe.wait(hin_ev)
                        pe.wait(wev)
                        pe.waitb(f"a3_ph{ec % 2}")
                        for kk in range(G):
                            mi = pe.matmul(php[0:128, 0:CH], wb[:, kk, :],
                                           HINB[:, kk, :], start=(kk == 0),
                                           stop=(kk == G - 1))
                        E_.i(mi, "pe")
                    E_.pe(f_ip)
                    ip_ev = E_.ev("pe")
                    E_.setfree(f"a3_wsl{ws}", ip_ev)

                    def f_ie(ac, ec=ec, php=php, ip_ev=ip_ev):
                        ac.wait(ip_ev)
                        ac.waitb("a3_xmf" if ec < 16 else "a3_sz")
                        if ec < 16:
                            ins = ac.activation(XMF[:, ec, :], php[0:128, 0:CH],
                                                AF.Copy)
                        else:
                            ins = ac.activation(SZ[:, ec - 16, :],
                                                php[0:128, 0:CH], AF.Silu)
                        ev = E_.i(ins, "ac")
                        E_.setfree(f"a3_ph{ec % 2}", ev)
                    E_.ac(f_ie)
                xm_ev = E_.ev("ac")

                for g in range(G16):
                    def f_cv(dv, g=g, xm_ev=xm_ev):
                        dv.wait(xm_ev)
                        dv.waitb("a3_xcf")
                        dv.tensor_scalar(UT[:, 0:3], XH[:, g, :],
                                         CW[:, g * 4:g * 4 + 1], None, ALU.mult)
                        dv.tensor_scalar(UT[:, 3:CH], XMF[:, g, 0:CH - 3],
                                         CW[:, g * 4:g * 4 + 1], None, ALU.mult)
                        for k2 in range(1, 4):
                            nh = 3 - k2
                            if nh > 0:
                                dv.scalar_tensor_tensor(
                                    UT[:, 0:nh], XH[:, g, 3 - nh:3],
                                    CW[:, g * 4 + k2:g * 4 + k2 + 1],
                                    UT[:, 0:nh], ALU.mult, ALU.add)
                            dv.scalar_tensor_tensor(
                                UT[:, nh:CH], XMF[:, g, 0:CH - nh],
                                CW[:, g * 4 + k2:g * 4 + k2 + 1],
                                UT[:, nh:CH], ALU.mult, ALU.add)
                        ins = dv.tensor_scalar(UT[:], UT[:], CB[:, g:g + 1],
                                               None, ALU.add)
                        E_.i(ins, "dv")
                        if g == G16 - 1:
                            ins = dv.tensor_copy(XH[:], XMF[:, :, CH - 3:CH])
                            E_.i(ins, "dv")
                    E_.dv(f_cv)
                    cv_ev = E_.ev("dv")

                    def f_si(ac, g=g, cv_ev=cv_ev):
                        ac.wait(cv_ev)
                        ins = ac.activation(XCF[:, g, :], UT[:], AF.Silu)
                        E_.i(ins, "ac")
                    E_.ac(f_si)
                    si_ev = E_.ev("ac")

                    def f_cb(dv, g=g, si_ev=si_ev):
                        dv.wait(si_ev)
                        ins = dv.tensor_copy(XCB[:, g, :], XCF[:, g, :])
                        E_.i(ins, "dv")
                    E_.dv(f_cb)
                xc_ev = [E_.ev("dv"), E_.ev("ac")]
                E_.setfree("a3_xmf", E_.ev("dv"))

                def f_xp(pe, xc_ev=xc_ev):
                    pe.wait(xc_ev)
                    pe.wait(w0_ev)
                    pe.waitb("a3_pdt")
                    for kk in range(G16):
                        mi = pe.matmul(PH[0][0:64, 0:CH], WXP[:, kk, 0:64],
                                       XCB[:, kk, :], start=(kk == 0),
                                       stop=(kk == G16 - 1))
                    E_.i(mi, "pe")
                    pe.waitb("a3_pbc")
                    for kk in range(G16):
                        mi = pe.matmul(PC[2][0:16, 0:CH], WXP[:, kk, 64:80],
                                       XCB[:, kk, :], start=(kk == 0),
                                       stop=(kk == G16 - 1))
                    E_.i(mi, "pe")
                    for kk in range(G16):
                        mi = pe.matmul(PC[3][0:16, 0:CH], WXP[:, kk, 80:96],
                                       XCB[:, kk, :], start=(kk == 0),
                                       stop=(kk == G16 - 1))
                    E_.i(mi, "pe")
                E_.pe(f_xp)
                xp_ev = E_.ev("pe")

                def f_xe(ac, xp_ev=xp_ev):
                    ac.wait(xp_ev)
                    ac.waitb("a3_dtr")
                    ins = ac.activation(DTR[:], PH[0][0:64, 0:CH], AF.Copy)
                    E_.i(ins, "ac")
                E_.ac(f_xe)
                dtr_ev = E_.ev("ac")
                E_.setfree("a3_pdt", dtr_ev)

                def f_bc(dv, xp_ev=xp_ev):
                    dv.wait(xp_ev)
                    dv.tensor_copy(BR[:], PC[2][0:16, 0:CH])
                    ins = dv.tensor_copy(CR[:], PC[3][0:16, 0:CH])
                    E_.i(ins, "dv")
                E_.dv(f_bc)
                br_ev = E_.ev("dv")
                E_.setfree("a3_pbc", br_ev)

                for n in range(16):
                    def f_bb(pe, n=n, br_ev=br_ev, dtr_ev=dtr_ev):
                        pe.wait(br_ev)
                        pe.wait(dtr_ev)
                        pe.waitb("a3_pbb")
                        pe.waitb("a3_ph0")
                        pe.waitb("a3_ph1")
                        mi = pe.matmul(PH[0][0:128, 0:CH],
                                       C["sel16"][0:16, n * 128:(n + 1) * 128],
                                       BR[:], start=True, stop=True)
                        E_.i(mi, "pe")
                        mi = pe.matmul(PH[1][0:128, 0:CH],
                                       C["sel16"][0:16, n * 128:(n + 1) * 128],
                                       CR[:], start=True, stop=True)
                        E_.i(mi, "pe")
                    E_.pe(f_bb)
                    bb_ev = E_.ev("pe")

                    def f_bcp(dv, n=n, bb_ev=bb_ev):
                        dv.wait(bb_ev)
                        dv.waitb("a3_bbc")
                        dv.tensor_copy(BBC[:, :, n], PH[0][0:128, 0:CH])
                        ins = dv.tensor_copy(CBC[:, :, n], PH[1][0:128, 0:CH])
                        E_.i(ins, "dv")
                        E_.setfree("a3_pbb", E_.ev("dv"))
                    E_.dv(f_bcp)
                bc_ev = E_.ev("dv")

                for g in range(G16):
                    php = PH[g % 2]

                    def f_dm(pe, g=g, php=php, dtr_ev=dtr_ev):
                        pe.wait(dtr_ev)
                        pe.waitb("a3_pbb")
                        pe.waitb(f"a3_ph{g % 2}")
                        mi = pe.matmul(php[0:128, 0:CH],
                                       WDT[0:64, g * 128:(g + 1) * 128],
                                       DTR[:], start=True, stop=True)
                        E_.i(mi, "pe")
                    E_.pe(f_dm)
                    dm_ev = E_.ev("pe")

                    def f_sp(dv, g=g, php=php, dm_ev=dm_ev):
                        dv.wait(dm_ev)
                        ins = dv.tensor_scalar(DT[:, g, :], php[0:128, 0:CH],
                                               DTB[:, g:g + 1], None, ALU.add)
                        ev = E_.i(ins, "dv")
                        E_.setfree(f"a3_ph{g % 2}", ev)
                    E_.dv(f_sp)
                x0_ev = E_.ev("dv")

                def f_spb(ac, x0_ev=x0_ev):
                    ac.wait(x0_ev)
                    for g in range(G16):
                        ins = ac.activation(DAB[:, :, g], DT[:, g, :], AF.Abs)
                    E_.i(ins, "ac")
                    for g in range(G16):
                        ins = ac.activation(DAB[:, :, g], DAB[:, :, g],
                                            AF.Exp, scale=-1.0)
                    E_.i(ins, "ac")
                    for g in range(G16):
                        ins = ac.activation(DAB[:, :, g], DAB[:, :, g],
                                            AF.Ln, bias=1.0)
                    E_.i(ins, "ac")
                E_.ac(f_spb)
                spb_ev = E_.ev("ac")

                def f_spf(dv, spb_ev=spb_ev):
                    dv.wait(spb_ev)
                    for g in range(G16):
                        dv.tensor_scalar(DT[:, g, :], DT[:, g, :], 0.0, None,
                                         ALU.max)
                        ins = dv.tensor_tensor(DT[:, g, :], DT[:, g, :],
                                               DAB[:, :, g], ALU.add)
                    E_.i(ins, "dv")
                E_.dv(f_spf)
                dt_ev = E_.ev("dv")

                for g in range(G16):
                    def f_u(dv, g=g, dt_ev=dt_ev):
                        dv.wait(dt_ev)
                        ins = dv.tensor_tensor(UT[:], DT[:, g, :], XCF[:, g, :],
                                               ALU.mult)
                        E_.i(ins, "dv")
                    E_.dv(f_u)
                    u_ev = E_.ev("dv")

                    def f_dag(ac, g=g, dt_ev=dt_ev):
                        ac.wait(dt_ev)
                        for n in range(16):
                            ac.waitb(f"a3_dab{n}")
                            ins = ac.activation(DAB[:, :, n], DT[:, g, :],
                                                AF.Exp,
                                                scale=ANv[:, g, n:n + 1])
                            E_.i(ins, "ac")
                    E_.ac(f_dag)
                    dab_ = E_.cnt["ac"] - 16

                    def f_dbg(dv, g=g, u_ev=u_ev, dab_=dab_):
                        for n in range(16):
                            dv.wait(("ac", dab_ + 1 + n))
                            dv.tensor_tensor(DBU[:, :, n], UT[:], BBC[:, :, n],
                                             ALU.mult)
                            ins = dv.tensor_tensor_scan(
                                HW[:, :, n], DAB[:, :, n], DBU[:, :, n],
                                CAR[:, g, n:n + 1], ALU.mult, ALU.add)
                            ev = E_.i(ins, "dv")
                            E_.setfree(f"a3_dab{n}", ev)
                    E_.dv(f_dbg)
                    def f_y(dv, g=g):
                        dv.tensor_copy(CAR[:, g, :], HW[:, CH - 1, :])
                        dv.tensor_tensor(HW[:], HW[:], CBC[:], ALU.mult)
                        dv.tensor_reduce(YG[:], HW[:, :, :], axis=AX.X,
                                         op=ALU.add)
                        dv.scalar_tensor_tensor(UT[:], XCF[:, g, :],
                                                DP[:, g:g + 1], YG[:],
                                                ALU.mult, ALU.add)
                        ins = dv.tensor_tensor(Y2B[:, g, :], UT[:],
                                               SZ[:, g, :], ALU.mult)
                        E_.i(ins, "dv")
                    E_.dv(f_y)
                y_ev = E_.ev("dv")
                E_.setfree("a3_xcf", y_ev)
                E_.setfree("a3_sz", y_ev)
                E_.setfree("a3_bbc", y_ev)
                E_.setfree("a3_hinb", E_.ev("pe"))

                def f_lx2(sy, ci=ci, y_ev=y_ev):
                    sy.waitb("a3_xf")
                    off = (ci % 2) * CH
                    ins = sy.dma_start(XF[:], xin[ci // 2][:, :, off:off + CH])
                    E_.i(ins, "qx", 16)
                E_.sy(f_lx2)
                ld2_ev = E_.ev("qx")
                for dg in range(G):
                    php = PH[dg % 2]

                    def f_op(pe, dg=dg, php=php, y_ev=y_ev):
                        pe.wait(y_ev)
                        pe.waitb(f"a3_ph{dg % 2}")
                        for kk in range(G16):
                            mi = pe.matmul(php[0:128, 0:CH],
                                           WOP[:, kk, dg * 128:(dg + 1) * 128],
                                           Y2B[:, kk, :], start=(kk == 0),
                                           stop=(kk == G16 - 1))
                        E_.i(mi, "pe")
                    E_.pe(f_op)
                    op_ev = E_.ev("pe")

                    def f_ra(dv, dg=dg, php=php, op_ev=op_ev, ld2_ev=ld2_ev):
                        dv.wait(op_ev)
                        dv.wait(ld2_ev)
                        ins = dv.tensor_tensor(XF[:, dg, :], XF[:, dg, :],
                                               php[0:128, 0:CH], ALU.add)
                        ev = E_.i(ins, "dv")
                        E_.setfree(f"a3_ph{dg % 2}", ev)
                    E_.dv(f_ra)
                xo_ev = E_.ev("dv")

                def f_st(sy, ci=ci, xo_ev=xo_ev):
                    sy.wait(xo_ev)
                    off = (ci % 2) * CH
                    ins = sy.dma_start(xout[ci // 2][:, :, off:off + CH], XF[:])
                    E_.i(ins, "qs", 16)
                E_.sy(f_st)
                E_.setfree("a3_xf", E_.ev("qs"))



def _c(a):
    return np.ascontiguousarray(a)


def _pp(v, g):
    return _c(v.reshape(g, 128).T.astype(np.float32))


def _pkn(wT):
    K = wT.shape[0] // 128
    return _c(np.asarray(wT).reshape(K, 128, wT.shape[1]).transpose(1, 0, 2))


def _blkc(wT, w):
    KK, N = wT.shape
    K = KK // 128
    return _c(np.asarray(wT).reshape(K, 128, N // w, w).transpose(2, 1, 0, 3))


def _blkr(wT, kw):
    KK, N = wT.shape
    nt = KK // (128 * kw)
    return _c(np.asarray(wT).reshape(nt, kw, 128, N).transpose(0, 2, 1, 3))


def _blkx(xb):
    Tn, Dn = xb.shape
    return _c(np.asarray(xb).reshape(Tn // 512, 512, Dn // 128, 128)
              .transpose(0, 3, 2, 1))


def _unblkx(arr):
    nt, p, g, w = arr.shape
    return np.ascontiguousarray(arr.transpose(0, 3, 2, 1).reshape(nt * w, g * p))


def _prep_shared(I):
    S = {}
    S["ones_col"] = np.ones((128, 1), np.float32)
    S["ones_row"] = np.ones((1, 128), np.float32)
    S["ident_f32"] = np.eye(128, dtype=np.float32)
    sel8 = np.zeros((8, 8 * 128), np.float32)
    for e in range(8):
        sel8[e, e * 128:(e + 1) * 128] = 1.0
    S["sel8"] = sel8
    sel16 = np.zeros((16, 16 * 128), np.float32)
    for n in range(16):
        sel16[n, n * 128:(n + 1) * 128] = 1.0
    S["sel16"] = sel16
    qf = np.arange(512)[None, :]
    kf = np.arange(128)[:, None]
    for j in range(4):
        S[f"cmask{j}"] = np.where(qf >= 128 * j + kf, 0.0, -30000.0).astype(np.float32)
    qi = np.arange(512)[None, :]
    for grp in range(8):
        kj = (np.arange(128) + grp * 128)[:, None]
        delta = qi - kj + 512
        valid = (delta >= 0) & (delta <= 512)
        S[f"l2delta{grp}"] = delta.astype(np.float32)
        S[f"l2mask{grp}"] = np.where(valid, 0.0, -30000.0).astype(np.float32)
    S["qidx"] = np.broadcast_to(np.arange(512, dtype=np.float32)[None, :],
                                (128, 512)).copy()

    S["f0_gatew"] = _pkn(I["L0_ffn_gate_w"].T)
    S["f0_gateb"] = _c(np.asarray(I["L0_ffn_gate_bias"], np.float32)[None, :])
    S["f0_w1t"] = np.stack([_blkc(np.asarray(I["L0_ffn_w1"][e]).T, 512)
                            for e in range(8)])
    S["f0_w3t"] = np.stack([_blkc(np.asarray(I["L0_ffn_w3"][e]).T, 512)
                            for e in range(8)])
    S["f0_w2t"] = np.stack([_blkr(np.asarray(I["L0_ffn_w2"][e]).T, 4)
                            for e in range(8)])
    S["f0_sw1t"] = _blkc(np.asarray(I["L0_ffn_sw1"]).T, 512)
    S["f0_sw3t"] = _blkc(np.asarray(I["L0_ffn_sw3"]).T, 512)
    S["f0_sw2t"] = _blkr(np.asarray(I["L0_ffn_sw2"]).T, 4)
    S["a0_int"] = _pkn(I["L0_attn_W_in"].T)
    S["a0_outt"] = _pkn(I["L0_attn_W_out"].T)
    S["a0_convw"] = _c(np.asarray(I["L0_attn_conv_w"]).reshape(1024, 5)
                       .reshape(8, 128, 5).transpose(1, 0, 2).reshape(128, 40))
    S["a0_convb"] = _pp(I["L0_attn_conv_b"], 8)
    S["a0_a"] = _pp(I["L0_attn_a_param"], 8)
    S["a0_lnw"] = _pp(I["L0_attn_ln_w"], 8)
    S["a0_lnb"] = _pp(I["L0_attn_ln_b"], 8)
    S["a1_wqkv"] = _blkc(np.asarray(I["L1_attn_W_qkv"]), 128)
    S["a1_wout"] = _blkc(np.asarray(I["L1_attn_W_out"]), 128)
    S["a1_qs"] = _pp(I["L1_attn_q_scale"], 8)
    S["a1_ks"] = _pp(I["L1_attn_k_scale"], 8)
    S["f1_routw"] = _pkn(I["L1_ffn_router_w"])
    S["f1_up"] = np.stack([_blkc(np.asarray(I["L1_ffn_up_w"][e]), 512)
                           for e in range(8)])
    S["f1_dn"] = np.stack([_blkr(np.asarray(I["L1_ffn_down_w"][e]), 4)
                           for e in range(8)])
    S["a2_int"] = _blkc(np.asarray(I["L2_attn_w_in"]).T, 128)
    S["a2_outt"] = _blkc(np.asarray(I["L2_attn_w_out"]).T, 128)
    S["f2_lns"] = _pp(I["L2_ffn_ln_scale"], 8)
    S["f2_lnb"] = _pp(I["L2_ffn_ln_bias"], 8)
    S["f2_gatew"] = _pkn(I["L2_ffn_gate_w"].T)
    S["f2_wg"] = np.stack([_blkc(np.asarray(I["L2_ffn_Wg"][e]), 512)
                           for e in range(8)])
    S["f2_wu"] = np.stack([_blkc(np.asarray(I["L2_ffn_Wu"][e]), 512)
                           for e in range(8)])
    S["f2_wd"] = np.stack([_blkr(np.asarray(I["L2_ffn_Wd"][e]), 4)
                           for e in range(8)])
    S["f2_swg"] = _blkc(np.asarray(I["L2_ffn_sWg"]), 512)
    S["f2_swu"] = _blkc(np.asarray(I["L2_ffn_sWu"]), 512)
    S["f2_swd"] = _blkr(np.asarray(I["L2_ffn_sWd"]), 4)
    S["a3_nw"] = _pp(I["L3_attn_norm_weight"], 8)
    S["a3_int"] = _blkc(np.asarray(I["L3_attn_in_proj_w"]).T, 128)
    S["a3_convw"] = _c(np.asarray(I["L3_attn_conv_w"]).reshape(16, 128, 4)
                       .transpose(1, 0, 2).reshape(128, 64))
    S["a3_convb"] = _pp(I["L3_attn_conv_b"], 16)
    S["a3_xpt"] = _pkn(I["L3_attn_x_proj_w"].T)
    S["a3_dtpt"] = _c(I["L3_attn_dt_proj_w"].T)
    S["a3_dtb"] = _pp(I["L3_attn_dt_proj_b"], 16)
    S["a3_alog"] = _c(np.asarray(I["L3_attn_A_log"]).reshape(16, 128, 16)
                      .transpose(1, 0, 2).reshape(128, 256))
    S["a3_d"] = _pp(I["L3_attn_D"], 16)
    S["a3_outt"] = _pkn(I["L3_attn_out_proj_w"].T)
    S["f3_ln1g"] = _pp(I["L3_ffn_ln1_g"], 8)
    S["f3_ln1b"] = _pp(I["L3_ffn_ln1_b"], 8)
    S["f3_w1t"] = _blkc(np.asarray(I["L3_ffn_w1"]).T, 256)
    S["f3_b1"] = _pp(I["L3_ffn_b1"], 32)
    S["f3_ln2g"] = _pp(I["L3_ffn_ln2_g"], 32)
    S["f3_ln2b"] = _pp(I["L3_ffn_ln2_b"], 32)
    S["f3_w2t"] = _blkr(np.asarray(I["L3_ffn_w2"]).T, 2)
    S["f3_b2"] = _pp(I["L3_ffn_b2"], 8)
    return S


CONSTS_BASE = ["ones_col", "ones_row"]
CONSTS_BY_SUB = {
    0: ["sel8", "ident_f32"], 2: ["sel8", "ident_f32"],
    3: ["sel8", "ident_f32"], 5: ["sel8", "ident_f32"],
    6: ["sel8", "ident_f32"], 8: ["sel8", "ident_f32"],
    9: [], 11: [],
    1: [],
    4: [f"cmask{j}" for j in range(4)] + ["ident_f32"],
    7: [f"l2delta{g}" for g in range(8)] + [f"l2mask{g}" for g in range(8)]
       + ["qidx", "ident_f32"],
    10: ["sel16", "ident_f32"],
}
NEED_IDENT_BF = {4, 7, 10}

SUB_WEIGHTS = {
    0: ["f0_gatew", "f0_gateb", "f0_w1t", "f0_w3t", "f0_w2t", "f0_sw1t",
        "f0_sw3t", "f0_sw2t"],
    1: ["a0_int", "a0_outt", "a0_convw", "a0_convb", "a0_a", "a0_lnw", "a0_lnb"],
    3: ["f1_routw", "f1_up", "f1_dn"],
    4: ["a1_wqkv", "a1_wout", "a1_qs", "a1_ks"],
    6: ["f2_lns", "f2_lnb", "f2_gatew", "f2_wg", "f2_wu", "f2_wd", "f2_swg",
        "f2_swu", "f2_swd"],
    7: ["a2_int", "a2_outt"],
    9: ["f3_ln1g", "f3_ln1b", "f3_w1t", "f3_b1", "f3_ln2g", "f3_ln2b",
        "f3_w2t", "f3_b2"],
    10: ["a3_nw", "a3_int", "a3_convw", "a3_convb", "a3_xpt", "a3_dtpt",
         "a3_dtb", "a3_alog", "a3_d", "a3_outt"],
}
SUB_WEIGHTS[2] = SUB_WEIGHTS[0]
SUB_WEIGHTS[5] = SUB_WEIGHTS[3]
SUB_WEIGHTS[8] = SUB_WEIGHTS[6]
SUB_WEIGHTS[11] = SUB_WEIGHTS[9]


def _emit_dispatch(E_, C, es, k, W, xin, xout, extra=None):
    nc = E_.nc
    if k in (3, 5):
        experts = []
        for e in range(8):
            up_e = W["f1_up"][e]
            dn_e = W["f1_dn"][e]
            experts.append({
                "nhb": 8, "act": "gelu_tanh", "gated": e,
                "wg_ap": (lambda hb, a=up_e: a[hb]),
                "wd_ap": (lambda hb, a=dn_e: a[hb]),
            })
        cfg = {"experts": experts,
               "router": {"topk": 4, "norm": True, "norm_eps": 0.0}}
        emit_moe_ffn(E_, C, {"gatew": W["f1_routw"]}, xin, xout, cfg)
    elif k in (0, 2):
        experts = []
        for e in range(8):
            experts.append({
                "nhb": 2, "act": "silu_mul", "gated": e,
                "wg_ap": (lambda hb, a=W["f0_w1t"][e]: a[hb]),
                "wu_ap": (lambda hb, a=W["f0_w3t"][e]: a[hb]),
                "wd_ap": (lambda hb, a=W["f0_w2t"][e]: a[hb]),
            })
        experts.append({
            "nhb": 4, "act": "silu_mul", "gated": None,
            "wg_ap": (lambda hb, a=W["f0_sw1t"]: a[hb]),
            "wu_ap": (lambda hb, a=W["f0_sw3t"]: a[hb]),
            "wd_ap": (lambda hb, a=W["f0_sw2t"]: a[hb]),
        })
        cfg = {"experts": experts,
               "router": {"topk": 2, "norm": False, "bias": True}}
        emit_moe_ffn(E_, C, {"gatew": W["f0_gatew"], "gatebias": W["f0_gateb"]},
                     xin, xout, cfg)
    elif k in (6, 8):
        lns_sb = es.enter_context(
            nc.sbuf_tensor(f"f2_lns_sb_{k}", [128, G], F32))
        lnb_sb = es.enter_context(
            nc.sbuf_tensor(f"f2_lnb_sb_{k}", [128, G], F32))
        with E_.block():
            def f_l(sy):
                ins = sy.dma_start(lns_sb[:], W["f2_lns"])
                E_.i(ins, "qc", 16)
                ins = sy.dma_start(lnb_sb[:], W["f2_lnb"])
                E_.i(ins, "qc", 16)
            E_.sy(f_l)
            E_.setfree("ln_gamma_ld", E_.ev("qc"))
        experts = []
        for e in range(8):
            experts.append({
                "nhb": 4, "act": "silu_mul", "gated": e,
                "wg_ap": (lambda hb, a=W["f2_wg"][e]: a[hb]),
                "wu_ap": (lambda hb, a=W["f2_wu"][e]: a[hb]),
                "wd_ap": (lambda hb, a=W["f2_wd"][e]: a[hb]),
            })
        experts.append({
            "nhb": 4, "act": "silu_mul", "gated": None,
            "wg_ap": (lambda hb, a=W["f2_swg"]: a[hb]),
            "wu_ap": (lambda hb, a=W["f2_swu"]: a[hb]),
            "wd_ap": (lambda hb, a=W["f2_swd"]: a[hb]),
        })
        cfg = {"experts": experts, "ln2_eps": 1e-5,
               "router": {"topk": 2, "norm": True, "norm_eps": 1e-5,
                          "internal_ln": True}}
        emit_moe_ffn(E_, C, {"gatew": W["f2_gatew"], "lns_sb": lns_sb,
                             "lnb_sb": lnb_sb}, xin, xout, cfg)
    elif k in (9, 11):
        qc = (extra or {}).get("f3_qcache")
        emit_ffn3(E_, C, W, xin, xout, qcache=qc, first=(k == 9 or qc is None))
    elif k == 1:
        emit_att0(E_, C, W, xin, xout)
    elif k == 4:
        emit_att1(E_, C, W, xin, xout)
    elif k == 7:
        emit_att2(E_, C, W, xin, xout)
    elif k == 10:
        emit_att3(E_, C, W, xin, xout)
    else:
        raise NotImplementedError(f"sublayer {k}")


def _emit_sub_body(E_, nc, es_outer, k, S, const_dram, xin, xout, W,
                   extra=None):
    ckeys = CONSTS_BASE + CONSTS_BY_SUB[k]
    with contextlib.ExitStack() as es:
        C = {}
        for key in ckeys:
            arr = S[key]
            C[key] = es.enter_context(
                nc.sbuf_tensor(f"c{k}_{key}", list(arr.shape), F32))
        if k in NEED_IDENT_BF:
            C["ident_bf16"] = es.enter_context(
                nc.sbuf_tensor(f"c{k}_ident_bf16", [128, 128], BF16))
        C["ones_col_bf"] = es.enter_context(
            nc.sbuf_tensor(f"c{k}_ones_col_bf", [128, 1], BF16))
        with E_.block():
            fence = [E_.ev(s) for s in ("qs", "qx", "qw", "qv", "qg", "qc")]

            def f_fence_s(sy):
                sy.wait(fence)

            def f_fence_g(g_):
                g_.wait(fence)
            E_.sy(f_fence_s)
            E_.po(f_fence_g)

            def f_cl(sy):
                for key in ckeys:
                    ins = sy.dma_start(C[key][:], const_dram[key])
                    E_.i(ins, "qc", 16)
            E_.sy(f_cl)

            def f_ib(g_):
                if k in NEED_IDENT_BF:
                    ins = g_.dma_start(C["ident_bf16"][:],
                                       const_dram["ident_f32"])
                    E_.i(ins, "qg", 16)
                ins = g_.dma_start(C["ones_col_bf"][:], const_dram["ones_col"])
                E_.i(ins, "qg", 16)
            E_.po(f_ib)
            cl_ev = [E_.ev("qc"), E_.ev("qg")]

            def f_cw(sy):
                sy.wait(cl_ev)
            E_.sy(f_cw)

            def f_cw2(g_):
                g_.wait(cl_ev)
            E_.po(f_cw2)
        _emit_dispatch(E_, C, es, k, W, xin, xout, extra=extra)


def _register_const_floats(nc, es):
    for i, val in enumerate(CONST_FLOATS):
        t_ = es.enter_context(nc.sbuf_tensor(f"c_f{i}", [128, 1], F32))
        nc.const_aps.register(F32, float(val), t_[:])
        yield t_, float(val)


def _build_sub_program(k, S):
    _UID[0] = 0
    nc = bass.Bass("TRN2", target_bir_lowering=False)
    xin = nc.dram_tensor("xin", [8, 128, G, 512], F32, kind="ExternalInput").ap()
    xout = nc.dram_tensor("xout", [8, 128, G, 512], F32,
                          kind="ExternalOutput").ap()
    W = {}
    for key in SUB_WEIGHTS[k]:
        arr = S[key]
        W[key] = nc.dram_tensor(key, list(arr.shape), F32,
                                kind="ExternalInput").ap()
    const_dram = {}
    for key in CONSTS_BASE + CONSTS_BY_SUB[k]:
        arr = S[key]
        const_dram[key] = nc.dram_tensor(key, list(arr.shape), F32,
                                         kind="ExternalInput").ap()
    with contextlib.ExitStack() as es:
        E_ = E(nc, es)
        cf_tiles = list(_register_const_floats(nc, es))
        with E_.block():
            def f_cf(g_):
                for t_, val in cf_tiles:
                    g_.memset(t_[:], val)
            E_.po(f_cf)
        _emit_sub_body(E_, nc, es, k, S, const_dram, xin, xout, W)
        with E_.block():
            def f_fin(sy):
                sy.wait(E_.ev("qs"))
            E_.sy(f_fin)
    return nc


ALL_CONSTS = sorted({k2 for ks in ([CONSTS_BASE] + list(CONSTS_BY_SUB.values()))
                     for k2 in ks})
ALL_WEIGHTS = sorted({k2 for ks in SUB_WEIGHTS.values() for k2 in ks})


def _build_fused(S):
    _UID[0] = 0
    nc = bass.Bass("TRN2", target_bir_lowering=False)
    xin = nc.dram_tensor("xin", [8, 128, G, 512], F32, kind="ExternalInput").ap()
    y = nc.dram_tensor("y", [8, 128, G, 512], F32, kind="ExternalOutput").ap()
    r0 = nc.dram_tensor("r0", [8, 128, G, 512], F32).ap()
    r1 = nc.dram_tensor("r1", [8, 128, G, 512], F32).ap()
    f3q1 = nc.dram_tensor("f3q1", [128, G, 4096], BF16).ap()
    f3q2 = nc.dram_tensor("f3q2", [128, 32, 1024], BF16).ap()
    extra = {"f3_qcache": (f3q1, f3q2)}
    W = {}
    for key in ALL_WEIGHTS:
        arr = S[key]
        W[key] = nc.dram_tensor(key, list(arr.shape), F32,
                                kind="ExternalInput").ap()
    const_dram = {}
    for key in ALL_CONSTS:
        arr = S[key]
        const_dram[key] = nc.dram_tensor(key, list(arr.shape), F32,
                                         kind="ExternalInput").ap()
    with contextlib.ExitStack() as es:
        E_ = E(nc, es)
        cf_tiles = list(_register_const_floats(nc, es))
        with E_.block():
            def f_cf(g_):
                for t_, val in cf_tiles:
                    g_.memset(t_[:], val)
            E_.po(f_cf)
        bufs = [r0, r1]
        for k in range(12):
            src_ap = xin if k == 0 else bufs[(k + 1) % 2]
            dst_ap = y if k == 11 else bufs[k % 2]
            Wk = {key: W[key] for key in SUB_WEIGHTS[k]}
            _emit_sub_body(E_, nc, es, k, S, const_dram, src_ap, dst_ap, Wk,
                           extra=extra)
        with E_.block():
            def f_fin(sy):
                sy.wait(E_.ev("qs"))
            E_.sy(f_fin)
    return nc


def kernel(**inputs):
    S = _prep_shared(inputs)
    x = inputs["L0_attn_x"]
    xs = [_blkx(np.asarray(x[b]).astype(np.float32)) for b in range(NCORES)]
    if FUSED:
        nc = _build_fused(S)
        in_maps = []
        for b in range(NCORES):
            im = {"xin": xs[b]}
            for key in ALL_WEIGHTS:
                im[key] = S[key]
            for key in ALL_CONSTS:
                im[key] = S[key]
            in_maps.append(im)
        res = run_bass_kernel_spmd(nc, in_maps, core_ids=list(range(NCORES)))
        out = np.stack([_unblkx(np.asarray(r["y"])) for r in res.results],
                       axis=0)
        return np.ascontiguousarray(out.astype(np.float32))
    for k in range(12):
        nc = _build_sub_program(k, S)
        in_maps = []
        for b in range(NCORES):
            im = {"xin": xs[b]}
            for key in SUB_WEIGHTS[k]:
                im[key] = S[key]
            for key in CONSTS_BASE + CONSTS_BY_SUB[k]:
                im[key] = S[key]
            in_maps.append(im)
        res = run_bass_kernel_spmd(nc, in_maps, core_ids=list(range(NCORES)))
        xs = [np.asarray(r["xout"]) for r in res.results]
    out = np.stack([_unblkx(xs[b]) for b in range(NCORES)], axis=0)
    return np.ascontiguousarray(out.astype(np.float32))


# version 07
# speedup vs baseline: 1.6401x; 1.1235x over previous
import contextlib
import math
import numpy as np
import ml_dtypes
import concourse.bass as bass
import concourse.mybir as mybir
from concourse.bass_utils import run_bass_kernel_spmd

F32 = mybir.dt.float32
BF16 = mybir.dt.bfloat16
I32 = mybir.dt.int32
AF = mybir.ActivationFunctionType
ALU = mybir.AluOpType
AX = mybir.AxisListType

D = 1024
T = 4096
G = 8
NCORES = 8
LN_EPS = 1e-5

GELU_C2 = 2.0 * math.sqrt(2.0 / math.pi)
GELU_C1 = GELU_C2 * 0.044715

FUSED = True

_UID = [0]


def _mk_alloc(nc, es):
    uid = _UID[0]
    _UID[0] += 1
    sb = lambda n, s, d: es.enter_context(nc.sbuf_tensor(f"u{uid}_{n}", s, d))
    ps = lambda n, s, d=None: es.enter_context(
        nc.psum_tensor(f"u{uid}_{n}", s, F32 if d is None else d))
    return uid, sb, ps

CONST_FLOATS = [1e-5, 1e-6, -0.5, -1.0, -40.0, 0.5, 2.0, 0.125] + \
    [-512.0 * 2.0 ** (-(h + 1) * 0.5) for h in range(16)]



AUTO_OPS = {
    "tensor_tensor", "tensor_scalar", "tensor_copy", "tensor_add",
    "tensor_sub", "tensor_mul", "tensor_max", "scalar_tensor_tensor",
    "tensor_tensor_scan", "tensor_reduce", "reduce_max", "reduce_sum",
    "reciprocal", "memset", "memzero", "select", "copy_predicated",
    "tensor_single_scalar", "tensor_scalar_add", "tensor_scalar_sub",
    "tensor_scalar_mul", "tensor_scalar_max", "tensor_scalar_min",
    "tensor_relu", "engine_nop", "activation", "copy", "mul", "add",
    "square", "sqrt", "sign", "transpose", "max", "bn_stats", "bn_aggr",
    "iota",
}
AUTO_SEM = {"vector": "dv", "scalar": "ac", "gpsimd": "po"}


class Eng:
    def __init__(self, E, name, h):
        self.E = E
        self.name = name
        self.h = h
        self.auto = AUTO_SEM.get(name)

    def wait(self, ev):
        if ev is None:
            return
        if isinstance(ev, list):
            for e in ev:
                self.wait(e)
            return
        sem, val = ev
        if val <= 0:
            return
        key = (self.name, sem)
        if self.E.lastw.get(key, -1) >= val:
            return
        self.E.lastw[key] = val
        self.h.wait_ge(self.E.sems[sem], val)

    def waitb(self, name):
        self.wait(self.E.free.get(name))

    def __getattr__(self, a):
        f = getattr(self.h, a)
        if self.auto is None or a not in AUTO_OPS or not callable(f):
            return f
        E_, sem = self.E, self.auto

        def wrapped(*args, **kw):
            self.wait((sem, E_.cnt[sem]))
            ins = f(*args, **kw)
            E_.cnt[sem] += 1
            ins.then_inc(E_.sems[sem], 1)
            return ins
        return wrapped


class E:
    SEMS = ("qw", "qv", "qg", "qc", "qx", "qs", "pe", "dv", "ac", "po")

    def __init__(self, nc, stack):
        self.nc = nc
        self.sems = {}
        self.cnt = {}
        self.lastw = {}
        self.free = {}
        for s in self.SEMS:
            self.sems[s] = stack.enter_context(nc.semaphore("sem_" + s))
            self.cnt[s] = 0
        self.blk = None

    def i(self, instr, sem, n=1):
        if sem in ("dv", "ac", "po") and n == 1:
            return (sem, self.cnt[sem])
        self.cnt[sem] += n
        instr.then_inc(self.sems[sem], n)
        return (sem, self.cnt[sem])

    def ev(self, sem):
        return (sem, self.cnt[sem])

    def setfree(self, name, ev):
        self.free[name] = ev

    def _sec(self, engname, fn):
        def wrap(h):
            fn(Eng(self, engname, h))
        getattr(self.blk, engname)(wrap)

    def pe(self, fn):
        self._sec("tensor", fn)

    def dv(self, fn):
        self._sec("vector", fn)

    def ac(self, fn):
        self._sec("scalar", fn)

    def po(self, fn):
        self._sec("gpsimd", fn)

    def sy(self, fn):
        self._sec("sync", fn)

    @contextlib.contextmanager
    def block(self):
        with self.nc.Block() as blk:
            self.blk = blk
            yield blk
            self.blk = None



def emit_ln_tile(E_, C, st, Xf, ld_ev, Tt, eps, out_bf=None, out_f32=None,
                 out_off=0, gg=G, gamma=None, beta=None, out_free_key=None,
                 kp="ln", ones=None):
    S1, S2, MB, RB = st["S1"], st["S2"], st["MB"], st["RB"]
    SQ, M_, V_, R_, T1 = st["SQ"], st["M"], st["V"], st["R"], st["T1"]
    dinv = 1.0 / (gg * 128)
    if ones is None:
        ones = C["ones_col"][:]

    def f_pe1(pe):
        pe.wait(ld_ev)
        pe.waitb(kp + "_s1")
        for g in range(gg):
            mi = pe.matmul(S1, ones, Xf[:, g, :],
                           start=(g == 0), stop=(g == gg - 1))
        E_.i(mi, "pe")
    E_.pe(f_pe1)
    s1_ev = E_.ev("pe")

    for g in range(gg):
        par = g % 2

        def f_ac1(ac, g=g, par=par):
            ac.wait(ld_ev)
            ac.waitb(kp + f"_sq{par}")
            ins = ac.activation(SQ[:, par, :], Xf[:, g, :], AF.Square)
            E_.i(ins, "ac")
        E_.ac(f_ac1)
        sqg_ev = E_.ev("ac")

        def f_pe2(pe, g=g, par=par, sqg_ev=sqg_ev):
            pe.wait(sqg_ev)
            if g == 0:
                pe.waitb(kp + "_s2")
            mi = pe.matmul(S2, C["ones_col"][:], SQ[:, par, :],
                           start=(g == 0), stop=(g == gg - 1))
            ev = E_.i(mi, "pe")
            E_.setfree(kp + f"_sq{par}", ev)
        E_.pe(f_pe2)
    s2_ev = E_.ev("pe")

    def f_dv1(dv):
        dv.wait(s1_ev)
        dv.waitb(kp + "_mvr")
        dv.tensor_scalar(M_[:], S1, dinv, None, ALU.mult)
        dv.tensor_tensor(T1[0:1, 0:Tt], M_[:], M_[:], ALU.mult)
        dv.wait(s2_ev)
        ins = dv.scalar_tensor_tensor(V_[:], S2, dinv, T1[0:1, 0:Tt],
                                      ALU.mult, ALU.subtract)
        E_.i(ins, "dv")
    E_.dv(f_dv1)
    v_ev = E_.ev("dv")
    E_.setfree(kp + "_s1", v_ev)
    E_.setfree(kp + "_s2", v_ev)

    def f_ac2(ac):
        ac.wait(v_ev)
        ac.activation(R_[:], V_[:], AF.Ln, bias=float(eps))
        ins = ac.activation(R_[:], R_[:], AF.Exp, scale=-0.5)
        E_.i(ins, "ac")
    E_.ac(f_ac2)
    r_ev = E_.ev("ac")

    def f_pe3(pe):
        pe.wait(r_ev)
        pe.waitb(kp + "_mbrb")
        pe.matmul(MB, C["ones_row"][:], M_[:], start=True, stop=True)
        mi = pe.matmul(RB, C["ones_row"][:], R_[:], start=True, stop=True)
        E_.i(mi, "pe")
    E_.pe(f_pe3)
    bc_ev = E_.ev("pe")
    E_.setfree(kp + "_mvr", bc_ev)

    def f_dv2(dv):
        dv.wait(bc_ev)
        if out_free_key:
            dv.waitb(out_free_key)
        if gamma is not None:
            dv.waitb("ln_gamma_ld")
        for g in range(gg):
            dv.tensor_tensor(T1[:], Xf[:, g, :], MB, ALU.subtract)
            if gamma is not None:
                dv.tensor_tensor(T1[:], T1[:], RB, ALU.mult)
                if out_f32 is not None:
                    ins = dv.tensor_scalar(out_f32[:, g, :], T1[:],
                                           gamma[:, g:g + 1], beta[:, g:g + 1],
                                           ALU.mult, ALU.add)
                    if out_bf is not None:
                        ins = dv.tensor_copy(
                            out_bf[:, g, out_off:out_off + Tt], out_f32[:, g, :])
                else:
                    ins = dv.tensor_scalar(out_bf[:, g, out_off:out_off + Tt],
                                           T1[:], gamma[:, g:g + 1],
                                           beta[:, g:g + 1], ALU.mult, ALU.add)
            elif out_f32 is not None:
                ins = dv.tensor_tensor(out_f32[:, g, :], T1[:], RB, ALU.mult)
                if out_bf is not None:
                    ins = dv.tensor_copy(
                        out_bf[:, g, out_off:out_off + Tt], out_f32[:, g, :])
            else:
                ins = dv.tensor_tensor(out_bf[:, g, out_off:out_off + Tt],
                                       T1[:], RB, ALU.mult)
        E_.i(ins, "dv")
    E_.dv(f_dv2)
    out_ev = E_.ev("dv")
    E_.setfree(kp + "_mbrb", out_ev)
    return out_ev


def alloc_ln_scratch(es, nc, pfx, Tt, ps_alias, sq=None):
    sb = lambda n, s, d: es.enter_context(nc.sbuf_tensor(f"u{_UID[0]}x_{pfx}{n}", s, d))
    a, b, c_, d_ = ps_alias
    st = {
        "SQ": sq if sq is not None else sb("SQ", [128, 2, Tt], F32),
        "M": sb("M", [1, Tt], F32),
        "V": (lambda v: v)(sb("V", [1, Tt], F32)),
        "T1": sb("T1", [128, Tt], F32),
        "S1": a[0:1, 0:Tt],
        "S2": b[0:1, 0:Tt],
        "MB": c_[:, 0:Tt],
        "RB": d_[:, 0:Tt],
    }
    st["R"] = st["V"]
    return st



def emit_gate_softmax_topk(E_, C, st, lg_ev, topk, norm, norm_eps,
                           bias_bc=None):
    PSL, PRB, TMP8, SEL, M1 = st["PSL"], st["PRB"], st["TMP8"], st["SEL"], st["M1"]
    BAS = st["BAS"]

    def f_nm(dv):
        dv.wait(lg_ev)
        dv.waitb("gt_m1")
        ins = dv.reduce_max(M1[:], PSL, axis=AX.X, negate=True)
        E_.i(ins, "dv")
    E_.dv(f_nm)
    nm_ev = E_.ev("dv")

    def f_exp(ac):
        ac.wait(nm_ev)
        ac.waitb("gt_prb")
        ins = ac.activation(PRB[:], PSL, AF.Exp, bias=M1[:])
        E_.i(ins, "ac")
    E_.ac(f_exp)
    ex_ev = E_.ev("ac")
    E_.setfree("gt_psl", ex_ev)

    def f_top(dv):
        dv.wait(ex_ev)
        dv.waitb("gt_sel")
        dv.reduce_sum(M1[:], PRB[:], axis=AX.X)
        dv.reciprocal(M1[:], M1[:])
        dv.tensor_scalar(PRB[:], PRB[:], M1[:, 0:1], None, ALU.mult)
        if bias_bc is not None:
            dv.tensor_tensor(BAS[:], PRB[:], bias_bc, ALU.add)
            basis = BAS
        else:
            basis = PRB
        dv.tensor_copy(TMP8[:], basis[:])
        for r in range(topk):
            dv.reduce_max(M1[:], TMP8[:], axis=AX.X)
            if r < topk - 1:
                dv.tensor_scalar(SEL[:], TMP8[:], M1[:, 0:1], None, ALU.is_equal)
                dv.scalar_tensor_tensor(TMP8[:], SEL[:], -1e30, TMP8[:],
                                        ALU.mult, ALU.add)
        dv.tensor_scalar(SEL[:], basis[:], M1[:, 0:1], None, ALU.is_ge)
        ins = dv.tensor_tensor(SEL[:], SEL[:], PRB[:], ALU.mult)
        if norm:
            dv.reduce_sum(M1[:], SEL[:], axis=AX.X)
            if norm_eps:
                dv.tensor_scalar(M1[:], M1[:], float(norm_eps), None, ALU.add)
            dv.reciprocal(M1[:], M1[:])
            ins = dv.tensor_scalar(SEL[:], SEL[:], M1[:, 0:1], None, ALU.mult)
        E_.i(ins, "dv")
    E_.dv(f_top)
    sel_ev = E_.ev("dv")
    E_.setfree("gt_m1", sel_ev)
    E_.setfree("gt_prb", sel_ev)
    return sel_ev


def emit_gating_phase(E_, C, st, XN, GA, gatew_sb, n_tok, topk, norm, norm_eps,
                      xn_ev, bias_bc=None, xn_off=0):
    for s in range(n_tok // 128):
        c0 = xn_off + s * 128

        def f_lg(pe, c0=c0):
            pe.wait(xn_ev)
            pe.waitb("gt_psl")
            pe.waitb("gatew_ld")
            for g in range(G):
                mi = pe.matmul(st["PSL"], XN[:, g, c0:c0 + 128],
                               gatew_sb[:, g, :], start=(g == 0),
                               stop=(g == G - 1))
            E_.i(mi, "pe")
        E_.pe(f_lg)
        lg_ev = E_.ev("pe")
        sel_ev = emit_gate_softmax_topk(E_, C, st, lg_ev, topk, norm, norm_eps,
                                        bias_bc=bias_bc)

        def f_tr(pe, sel_ev=sel_ev):
            pe.wait(sel_ev)
            pe.waitb("gt_gt")
            ins = pe.transpose(st["PSGT"], st["SEL"][:], C["ident_f32"][:])
            E_.i(ins, "pe")
        E_.pe(f_tr)
        tr_ev = E_.ev("pe")
        E_.setfree("gt_sel", tr_ev)

        def f_cp(dv, tr_ev=tr_ev, s=s):
            dv.wait(tr_ev)
            ins = dv.tensor_copy(GA[:, s * 128:(s + 1) * 128], st["PSGT"])
            E_.i(ins, "dv")
        E_.dv(f_cp)
        E_.setfree("gt_gt", E_.ev("dv"))
    return E_.ev("dv")



def emit_moe_ffn(E_, C, W, xin, xout, cfg):
    nc = E_.nc
    HALF = 2048
    r = cfg["router"]
    internal_ln = bool(r.get("internal_ln"))
    has_wu = any(e.get("wu_ap") for e in cfg["experts"])
    HBW = 512

    with contextlib.ExitStack() as es:
        uid, sb, ps_ = _mk_alloc(nc, es)
        XN = sb("mf_XN", [128, G, HALF], BF16)
        OA = sb("mf_OA", [128, G, HALF], F32)
        XF = sb("mf_XF", [128, G, 512], F32)
        GATEW = sb("mf_gw", [128, G, 8], BF16)
        GA = sb("mf_ga", [8, HALF], F32)
        GB4 = XF[:, 0:4, :]
        WB1 = [sb(f"mf_w1{i}", [128, G, HBW], BF16) for i in range(2)]
        WB3 = ([sb(f"mf_w3{i}", [128, G, HBW], BF16) for i in range(2)]
               if has_wu else [None, None])
        WB2 = [sb(f"mf_w2{i}", [128, HBW // 128, 1024], BF16) for i in range(2)]
        ug = 8 if internal_ln else HBW // 128
        U8 = sb("mf_u8", [128, ug, 512], BF16)
        HT = U8[:, 0:HBW // 128, :]
        SCR = sb("mf_scr", [128, 8, 512], F32)
        SC1 = [SCR[:, 0, :], SCR[:, 1, :]]
        SC2 = [SCR[:, 2, :], SCR[:, 3, :]]
        SC3 = [SCR[:, 4, :], SCR[:, 5, :]]
        PH = [ps_(f"mf_ph{i}", [128, 512])
              for i in range(2)]
        PH3 = [ps_(f"mf_ph3{i}", [128, 512])
               for i in range(2)]
        PD = [ps_(f"mf_pd{i}", [128, 512])
              for i in range(2)]
        PMB = ps_("mf_pmb", [128, 512])
        PRBC = ps_("mf_prb", [128, 512])
        st = alloc_ln_scratch(es, nc, "mf_ln", 512, (PD[0], PD[1], PMB, PRBC),
                              sq=SCR[:, 0:2, :])
        st["PSL"] = PH[0][:, 0:8]
        st["PSGT"] = PH[1][0:8, 0:128]
        st["PRB"] = sb("mf_gprb", [128, 8], F32)
        st["TMP8"] = sb("mf_gtmp", [128, 8], F32)
        st["SEL"] = sb("mf_gsel", [128, 8], F32)
        st["BAS"] = (sb("mf_gbas", [128, 8], F32) if r.get("bias") else None)
        st["M1"] = sb("mf_gm1", [128, 1], F32)
        BIASBC = sb("mf_gbias", [128, 8], F32) if r.get("bias") else None
        XN2 = U8[:, 0:8, :] if internal_ln else None

        with E_.block():
            def f_gw(g_):
                ins = g_.dma_start(GATEW[:], W["gatew"])
                E_.i(ins, "qg", 16)
            E_.po(f_gw)
            E_.setfree("gatew_ld", E_.ev("qg"))
            if BIASBC is not None:
                BROW = sb("mf_brow", [1, 8], F32)

                def f_bl(sy):
                    ins = sy.dma_start(BROW[:], W["gatebias"])
                    E_.i(ins, "qc", 16)
                E_.sy(f_bl)
                bl_ev = E_.ev("qc")

                def f_bb(pe):
                    pe.wait(bl_ev)
                    ins = pe.matmul(PH[0][:, 0:8], C["ones_row"][:], BROW[:],
                                    start=True, stop=True)
                    E_.i(ins, "pe")
                E_.pe(f_bb)
                bb_ev = E_.ev("pe")

                def f_bc(dv):
                    dv.wait(bb_ev)
                    ins = dv.tensor_copy(BIASBC[:], PH[0][:, 0:8])
                    E_.i(ins, "dv")
                E_.dv(f_bc)
                E_.setfree("gt_psl", E_.ev("dv"))

        for hf in range(2):
            hof = hf * HALF
            with E_.block():
                for tt in range(4):
                    gtile = hf * 4 + tt

                    def f_lx(sy, gtile=gtile):
                        sy.waitb("mf_xf")
                        ins = sy.dma_start(XF[:], xin[gtile])
                        E_.i(ins, "qx", 16)
                    E_.sy(f_lx)
                    ld_ev = E_.ev("qx")
                    xn_ev = emit_ln_tile(
                        E_, C, st, XF, ld_ev, 512, LN_EPS, out_bf=XN,
                        out_off=tt * 512)
                    E_.setfree("mf_xf", xn_ev)
                    if internal_ln:
                        xn2_ev = emit_ln_tile(
                            E_, C, st, XN[:, :, tt * 512:(tt + 1) * 512],
                            xn_ev, 512, cfg["ln2_eps"], out_bf=XN2,
                            gamma=W["lns_sb"], beta=W["lnb_sb"],
                            out_free_key="mf_xn2", ones=C["ones_col_bf"][:])
                        gate_src, gate_ev, gate_off = XN2, xn2_ev, 0
                    else:
                        gate_src, gate_ev, gate_off = XN, xn_ev, tt * 512
                    emit_gating_phase(
                        E_, C, st, gate_src, GA[:, tt * 512:(tt + 1) * 512],
                        GATEW, 512, r["topk"], r["norm"],
                        r.get("norm_eps", 0.0), gate_ev,
                        bias_bc=(BIASBC[:] if BIASBC is not None else None),
                        xn_off=gate_off)
                    if internal_ln:
                        E_.setfree("mf_xn2", E_.ev("pe"))
                ga_ev = E_.ev("dv")

                def f_ms(dv):
                    ins = dv.memset(OA[:], 0.0)
                    E_.i(ins, "dv")
                E_.dv(f_ms)

            with E_.block():
                wrot = 0
                for ex in cfg["experts"]:
                    NHC = HBW // 128
                    gated = ex.get("gated")
                    if gated is not None:
                        for tt in range(4):
                            def f_gb(pe, gated=gated, tt=tt):
                                pe.wait(ga_ev)
                                pe.waitb("mf_pgb")
                                mi = pe.matmul(
                                    PMB[:],
                                    C["sel8"][0:8, gated * 128:(gated + 1) * 128],
                                    GA[:, tt * 512:(tt + 1) * 512],
                                    start=True, stop=True)
                                E_.i(mi, "pe")
                            E_.pe(f_gb)
                            gb_ev = E_.ev("pe")

                            def f_gc(dv, gb_ev=gb_ev, tt=tt):
                                dv.wait(gb_ev)
                                ins = dv.tensor_copy(GB4[:, tt, :], PMB[:])
                                E_.i(ins, "dv")
                            E_.dv(f_gc)
                            E_.setfree("mf_pgb", E_.ev("dv"))
                    for hb in range(ex["nhb"]):
                        slot = wrot % 2
                        w1b, w3b, w2b = WB1[slot], WB3[slot], WB2[slot]
                        wfree = E_.free.get(f"mf_wb{slot}")
                        wrot += 1

                        wsem = "qw" if slot == 0 else "qv"

                        def f_lw(g_, ex=ex, hb=hb, w1b=w1b, w3b=w3b, w2b=w2b,
                                 wfree=wfree, NHC=NHC, wsem=wsem):
                            g_.wait(wfree)
                            ins = g_.dma_start(w1b[:], ex["wg_ap"](hb))
                            E_.i(ins, wsem, 16)
                            if ex.get("wu_ap"):
                                ins = g_.dma_start(w3b[:], ex["wu_ap"](hb))
                                E_.i(ins, wsem, 16)
                            ins = g_.dma_start(w2b[:], ex["wd_ap"](hb))
                            E_.i(ins, wsem, 16)
                        E_.po(f_lw)
                        w_ev = E_.ev(wsem)

                        for tt in range(4):
                            ts0 = tt * 512
                            gb1_ev = None
                            GB1 = GB4[:, tt, :]

                            if ex["act"] == "gelu_tanh":
                                PHX = [PH[0], PH[1], PH3[0], PH3[1]]

                                def f_up4(pe, w1b=w1b, ts0=ts0, w_ev=w_ev,
                                          PHX=PHX):
                                    pe.wait(w_ev)
                                    for hc in range(4):
                                        pe.waitb(f"mf_phx{hc}")
                                        for k in range(G):
                                            mi = pe.matmul(
                                                PHX[hc][:],
                                                w1b[:, k, hc * 128:(hc + 1) * 128],
                                                XN[:, k, ts0:ts0 + 512],
                                                start=(k == 0),
                                                stop=(k == G - 1))
                                        E_.i(mi, "pe")
                                E_.pe(f_up4)
                                up_evs = [("pe", E_.cnt["pe"] - 3 + j)
                                          for j in range(4)]

                                def f_gl4(ac, up_evs=up_evs, PHX=PHX):
                                    for hc in range(4):
                                        ac.wait(up_evs[hc])
                                        ac.waitb(f"mf_scx{hc}")
                                        ins = ac.activation(SCR[:, hc, :],
                                                            PHX[hc][:],
                                                            AF.Gelu_apprx_tanh)
                                        ev = E_.i(ins, "ac")
                                        E_.setfree(f"mf_phx{hc}", ev)
                                E_.ac(f_gl4)
                                gl_evs = [("ac", E_.cnt["ac"] - 3 + j)
                                          for j in range(4)]

                                def f_m4(dv, gl_evs=gl_evs, gb1_ev=gb1_ev):
                                    dv.wait(gb1_ev)
                                    dv.waitb("mf_ht")
                                    for hc in range(4):
                                        dv.wait(gl_evs[hc])
                                        ins = dv.tensor_tensor(
                                            HT[:, hc, :], SCR[:, hc, :],
                                            GB1, ALU.mult)
                                        ev = E_.i(ins, "dv")
                                        E_.setfree(f"mf_scx{hc}", ev)
                                E_.dv(f_m4)
                            else:
                              for hc in range(NHC):
                                par = (hc + tt * NHC) % 2
                                ph, ph3 = PH[par], PH3[par]

                                def f_up(pe, w1b=w1b, w3b=w3b, ph=ph, ph3=ph3,
                                         hc=hc, ts0=ts0, w_ev=w_ev, par=par,
                                         ex=ex):
                                    pe.wait(w_ev)
                                    pe.waitb(f"mf_ph{par}")
                                    for k in range(G):
                                        mi = pe.matmul(
                                            ph[:],
                                            w1b[:, k, hc * 128:(hc + 1) * 128],
                                            XN[:, k, ts0:ts0 + 512],
                                            start=(k == 0), stop=(k == G - 1))
                                    E_.i(mi, "pe")
                                    if ex.get("wu_ap"):
                                        pe.waitb(f"mf_ph3{par}")
                                        for k in range(G):
                                            mi = pe.matmul(
                                                ph3[:],
                                                w3b[:, k, hc * 128:(hc + 1) * 128],
                                                XN[:, k, ts0:ts0 + 512],
                                                start=(k == 0), stop=(k == G - 1))
                                        E_.i(mi, "pe")
                                E_.pe(f_up)
                                up_ev = E_.ev("pe")

                                if ex["act"] == "silu_mul":
                                    def f_sl(ac, par=par, ph=ph, up_ev=up_ev):
                                        ac.wait(up_ev)
                                        ac.waitb(f"mf_sc1{par}")
                                        ins = ac.activation(SC1[par], ph[:],
                                                            AF.Silu)
                                        E_.i(ins, "ac")
                                    E_.ac(f_sl)
                                    sl_ev = E_.ev("ac")
                                    E_.setfree(f"mf_ph{par}", sl_ev)

                                    def f_m1(dv, par=par, ph3=ph3, hc=hc,
                                             sl_ev=sl_ev, gb1_ev=gb1_ev,
                                             gated=gated):
                                        dv.wait(sl_ev)
                                        dv.waitb("mf_ht")
                                        if gated is not None:
                                            dv.wait(gb1_ev)
                                            dv.tensor_tensor(
                                                SC2[par], SC1[par], ph3[:],
                                                ALU.mult)
                                            ins = dv.tensor_tensor(
                                                HT[:, hc, :], SC2[par],
                                                GB1, ALU.mult)
                                        else:
                                            ins = dv.tensor_tensor(
                                                HT[:, hc, :], SC1[par], ph3[:],
                                                ALU.mult)
                                        ev = E_.i(ins, "dv")
                                        E_.setfree(f"mf_ph3{par}", ev)
                                        E_.setfree(f"mf_sc1{par}", ev)
                                    E_.dv(f_m1)
                                else:
                                    def f_sq(ac, par=par, ph=ph, up_ev=up_ev):
                                        ac.wait(up_ev)
                                        ac.waitb(f"mf_sc1{par}")
                                        ins = ac.activation(SC1[par], ph[:],
                                                            AF.Square)
                                        E_.i(ins, "ac")
                                    E_.ac(f_sq)
                                    sq_ev = E_.ev("ac")

                                    def f_a1(dv, par=par, ph=ph, sq_ev=sq_ev):
                                        dv.wait(sq_ev)
                                        dv.tensor_scalar(SC2[par], SC1[par],
                                                         GELU_C1, GELU_C2,
                                                         ALU.mult, ALU.add)
                                        ins = dv.tensor_tensor(SC1[par],
                                                               SC2[par], ph[:],
                                                               ALU.mult)
                                        E_.i(ins, "dv")
                                    E_.dv(f_a1)
                                    a1_ev = E_.ev("dv")

                                    def f_sg(ac, par=par, a1_ev=a1_ev):
                                        ac.wait(a1_ev)
                                        ins = ac.activation(SC3[par],
                                                            SC1[par],
                                                            AF.Sigmoid)
                                        E_.i(ins, "ac")
                                    E_.ac(f_sg)
                                    sg_ev = E_.ev("ac")

                                    def f_m2(dv, par=par, ph=ph, hc=hc,
                                             sg_ev=sg_ev, gb1_ev=gb1_ev):
                                        dv.wait(sg_ev)
                                        dv.wait(gb1_ev)
                                        ins = dv.tensor_tensor(
                                            SC2[par], SC3[par], ph[:],
                                            ALU.mult)
                                        ev = E_.i(ins, "dv")
                                        E_.setfree(f"mf_ph{par}", ev)
                                        dv.waitb("mf_ht")
                                        ins = dv.tensor_tensor(
                                            HT[:, hc, :], SC2[par],
                                            GB1, ALU.mult)
                                        ev = E_.i(ins, "dv")
                                        E_.setfree(f"mf_sc1{par}", ev)
                                    E_.dv(f_m2)
                            ht_ev = E_.ev("dv")

                            for dg in range(G):
                                pdb = PD[dg % 2]

                                def f_dn(pe, w2b=w2b, pdb=pdb, dg=dg, NHC=NHC,
                                         ht_ev=ht_ev):
                                    pe.wait(ht_ev)
                                    pe.waitb(f"mf_pd{dg % 2}")
                                    for k in range(NHC):
                                        mi = pe.matmul(
                                            pdb[:],
                                            w2b[:, k, dg * 128:(dg + 1) * 128],
                                            HT[:, k, :], start=(k == 0),
                                            stop=(k == NHC - 1))
                                    E_.i(mi, "pe")
                                E_.pe(f_dn)
                                dn_ev = E_.ev("pe")

                                def f_ac_(dv, pdb=pdb, dg=dg, ts0=ts0,
                                          dn_ev=dn_ev):
                                    dv.wait(dn_ev)
                                    ins = dv.tensor_tensor(
                                        OA[:, dg, ts0:ts0 + 512],
                                        OA[:, dg, ts0:ts0 + 512], pdb[:],
                                        ALU.add)
                                    ev = E_.i(ins, "dv")
                                    E_.setfree(f"mf_pd{dg % 2}", ev)
                                E_.dv(f_ac_)
                            E_.setfree("mf_ht", E_.ev("pe"))
                        E_.setfree(f"mf_wb{slot}", E_.ev("pe"))
            E_.setfree("mf_xf", E_.ev("dv"))

            with E_.block():
                for tt in range(4):
                    gtile = hf * 4 + tt

                    def f_lx2(sy, gtile=gtile):
                        sy.waitb("mf_xf")
                        ins = sy.dma_start(XF[:], xin[gtile])
                        E_.i(ins, "qx", 16)
                    E_.sy(f_lx2)
                    ld2_ev = E_.ev("qx")

                    def f_res(dv, tt=tt, ld2_ev=ld2_ev):
                        dv.wait(ld2_ev)
                        for g in range(G):
                            ins = dv.scalar_tensor_tensor(
                                XF[:, g, :], OA[:, g, tt * 512:(tt + 1) * 512],
                                0.5, XF[:, g, :], ALU.mult, ALU.add)
                        E_.i(ins, "dv")
                    E_.dv(f_res)
                    xo_ev = E_.ev("dv")

                    def f_st(sy, gtile=gtile, xo_ev=xo_ev):
                        sy.wait(xo_ev)
                        ins = sy.dma_start(xout[gtile], XF[:])
                        E_.i(ins, "qs", 16)
                    E_.sy(f_st)
                    E_.setfree("mf_xf", E_.ev("qs"))



def emit_ffn3(E_, C, W, xin, xout, qcache=None, first=True):
    nc = E_.nc
    H32 = 32
    w1r = W["f3_w1t"]
    w2r = W["f3_w2t"]

    with contextlib.ExitStack() as es:
        uid, sb, ps_ = _mk_alloc(nc, es)
        WQ1 = sb("f3_wq1", [128, G, 4096], BF16)
        WQ2 = sb("f3_wq2", [128, H32, 1024], BF16)
        QT2 = sb("f3_qt2", [128, 512], F32)
        XF = sb("f3_xf", [128, G, 512], F32)
        SCR = sb("f3_scr", [128, 4, 512], F32)
        WQF = SCR[:].rearrange("p a b -> p (a b)")
        QT = sb("f3_qt", [128, 512], F32)
        XN1 = sb("f3_xn1", [128, G, 512], BF16)
        HSB = sb("f3_h", [128, H32, 512], BF16)
        XA = HSB[:, 0:G, :]
        B1 = sb("f3_b1s", [128, H32], F32)
        L2Gs = sb("f3_l2g", [128, H32], F32)
        L2Bs = sb("f3_l2b", [128, H32], F32)
        L1Gs = sb("f3_l1g", [128, G], F32)
        L1Bs = sb("f3_l1b", [128, G], F32)
        B2 = sb("f3_b2s", [128, G], F32)
        MACC = sb("f3_macc", [128, 1], F32)
        MSC = [sb(f"f3_msc{i}", [128, 1], F32) for i in range(2)]
        PH = [ps_(f"f3_ph{i}", [128, 512])
              for i in range(2)]
        PD = [ps_(f"f3_pd{i}", [128, 512])
              for i in range(2)]
        PMB = ps_("f3_pmb", [128, 512])
        PRBC = ps_("f3_prb", [128, 512])
        st = alloc_ln_scratch(es, nc, "f3_ln", 512, (PD[0], PD[1], PMB, PRBC),
                              sq=SCR[:, 0:2, :])

        with E_.block():
            def f_lp(sy):
                for dst, key, n16 in ((B1, "f3_b1", 16), (L2Gs, "f3_ln2g", 16),
                                      (L2Bs, "f3_ln2b", 16), (L1Gs, "f3_ln1g", 16),
                                      (L1Bs, "f3_ln1b", 16), (B2, "f3_b2", 16)):
                    ins = sy.dma_start(dst[:], W[key])
                    E_.i(ins, "qc", 16)
            E_.sy(f_lp)
            E_.setfree("f3_params", E_.ev("qc"))

        if not first:
            with E_.block():
                def f_lq(g_):
                    ins = g_.dma_start(WQ1[:], qcache[0])
                    E_.i(ins, "qg", 16)
                    ins = g_.dma_start(WQ2[:], qcache[1])
                    E_.i(ins, "qg", 16)
                E_.po(f_lq)
            wq_ev = E_.ev("qg")
        else:
          for wi, (wr, WQ, nch, nel) in enumerate((
                (w1r, WQ1, 16, 1024 * 4096), (w2r, WQ2, 16, 4096 * 1024))):
            with E_.block():
                def f_z(dv):
                    ins = dv.memset(MACC[:], 0.0)
                    E_.i(ins, "dv")
                E_.dv(f_z)
                for c in range(nch):
                    src_ap = wr[c]

                    def f_lq(sy, src_ap=src_ap, wi=wi):
                        sy.waitb("f3_wqf")
                        if wi == 0:
                            ins = sy.dma_start(
                                WQF[:].rearrange("p (a b) -> p a b", a=8), src_ap)
                        else:
                            ins = sy.dma_start(
                                WQF[:].rearrange("p (a b) -> p a b", a=2), src_ap)
                        E_.i(ins, "qx", 16)
                    E_.sy(f_lq)
                    lq_ev = E_.ev("qx")

                    def f_ab(dv, lq_ev=lq_ev):
                        dv.wait(lq_ev)
                        dv.reduce_sum(QT[:, 0:1], WQF[:], axis=AX.X,
                                      apply_absolute_value=True)
                        ins = dv.tensor_tensor(MACC[:], MACC[:], QT[:, 0:1],
                                               ALU.add)
                        E_.i(ins, "dv")
                    E_.dv(f_ab)
                    E_.setfree("f3_wqf", E_.ev("dv"))
                macc_ev = E_.ev("dv")

                def f_tot(pe, macc_ev=macc_ev):
                    pe.wait(macc_ev)
                    ins = pe.matmul(PH[0][0:1, 0:1], C["ones_col"][:], MACC[:],
                                    start=True, stop=True)
                    E_.i(ins, "pe")
                E_.pe(f_tot)
                tot_ev = E_.ev("pe")

                def f_m(dv, tot_ev=tot_ev, nel=nel):
                    dv.wait(tot_ev)
                    dv.tensor_scalar(QT[0:1, 0:1], PH[0][0:1, 0:1],
                                     1.0 / nel, 1e-5, ALU.mult, ALU.max)
                    ins = dv.reciprocal(QT[0:1, 1:2], QT[0:1, 0:1])
                    E_.i(ins, "dv")
                E_.dv(f_m)
                m_ev = E_.ev("dv")

                def f_mb(pe, m_ev=m_ev):
                    pe.wait(m_ev)
                    pe.matmul(PH[0][:, 0:1], C["ones_row"][:], QT[0:1, 0:1],
                              start=True, stop=True)
                    ins = pe.matmul(PH[0][:, 1:2], C["ones_row"][:],
                                    QT[0:1, 1:2], start=True, stop=True)
                    E_.i(ins, "pe")
                E_.pe(f_mb)
                mb_ev = E_.ev("pe")

                def f_mc(dv, mb_ev=mb_ev):
                    dv.wait(mb_ev)
                    dv.tensor_copy(MSC[0][:], PH[0][:, 0:1])
                    ins = dv.tensor_copy(MSC[1][:], PH[0][:, 1:2])
                    E_.i(ins, "dv")
                E_.dv(f_mc)
            with E_.block():
                for c in range(nch):
                    src_ap = wr[c]
                    if wi == 0:
                        dst_ap = WQ[:, :, c * 256:(c + 1) * 256]
                    else:
                        dst_ap = WQ[:, c * 2:(c + 1) * 2, :]

                    def f_lq2(sy, src_ap=src_ap, wi=wi):
                        sy.waitb("f3_wqf")
                        a = 8 if wi == 0 else 2
                        ins = sy.dma_start(
                            WQF[:].rearrange("p (x b) -> p x b", x=a), src_ap)
                        E_.i(ins, "qx", 16)
                    E_.sy(f_lq2)
                    lq_ev = E_.ev("qx")

                    def f_q(dv, lq_ev=lq_ev, dst_ap=dst_ap, wi=wi):
                        dv.wait(lq_ev)
                        for s4 in range(4):
                            sl = slice(s4 * 512, (s4 + 1) * 512)
                            if wi == 0:
                                dst = dst_ap[:, 2 * s4:2 * s4 + 2, :]
                                qv = QT2[:].rearrange("p (x b) -> p x b", x=2)
                            else:
                                dst = dst_ap[:, s4 // 2,
                                             (s4 % 2) * 512:(s4 % 2 + 1) * 512]
                                qv = QT2[:]
                            dv.tensor_scalar(QT[:], WQF[:, sl],
                                             MSC[1][:, 0:1], None, ALU.mult)
                            dv.tensor_scalar(QT2[:], QT[:], 0.5, None, ALU.is_ge)
                            dv.tensor_scalar(QT[:], QT[:], -0.5, None, ALU.is_le)
                            dv.tensor_tensor(QT2[:], QT2[:], QT[:], ALU.subtract)
                            ins = dv.tensor_scalar(dst, qv,
                                                   MSC[0][:, 0:1], None, ALU.mult)
                        E_.i(ins, "dv")
                        E_.setfree("f3_wqf", E_.ev("dv"))
                    E_.dv(f_q)
          wq_ev = E_.ev("dv")
          if qcache is not None:
            with E_.block():
                def f_sq(sy, wq_ev=wq_ev):
                    sy.wait(wq_ev)
                    ins = sy.dma_start(qcache[0], WQ1[:])
                    E_.i(ins, "qs", 16)
                    ins = sy.dma_start(qcache[1], WQ2[:])
                    E_.i(ins, "qs", 16)
                E_.sy(f_sq)

        for tt in range(8):
            with E_.block():
                def f_lx(sy, tt=tt):
                    sy.waitb("f3_xf")
                    ins = sy.dma_start(XF[:], xin[tt])
                    E_.i(ins, "qx", 16)
                E_.sy(f_lx)
                ld_ev = E_.ev("qx")
                xa_ev = emit_ln_tile(E_, C, st, XF, ld_ev, 512, LN_EPS,
                                     out_bf=XA, out_free_key="f3_xa")
                E_.setfree("f3_xf_ln", xa_ev)
                def wait_params(dv):
                    dv.waitb("f3_params")
                E_.dv(wait_params)
                xn1_ev = emit_ln_tile(E_, C, st, XA, xa_ev, 512, LN_EPS,
                                      out_bf=XN1, gamma=L1Gs, beta=L1Bs,
                                      out_free_key="f3_xn1",
                                      ones=C["ones_col_bf"][:])
                E_.setfree("f3_xa", E_.ev("pe"))

                for hc in range(H32):
                    par = hc % 2
                    ph = PH[par]

                    def f_up(pe, hc=hc, ph=ph, xn1_ev=xn1_ev, par=par):
                        pe.wait(xn1_ev)
                        pe.wait(wq_ev)
                        pe.waitb(f"f3_ph{par}")
                        for k in range(G):
                            mi = pe.matmul(ph[:],
                                           WQ1[:, k, hc * 128:(hc + 1) * 128],
                                           XN1[:, k, :],
                                           start=(k == 0), stop=(k == G - 1))
                        E_.i(mi, "pe")
                    E_.pe(f_up)
                    up_ev = E_.ev("pe")

                    def f_gl(ac, hc=hc, ph=ph, up_ev=up_ev, par=par):
                        ac.wait(up_ev)
                        ac.waitb("f3_hsb")
                        ins = ac.activation(HSB[:, hc, :], ph[:], AF.Gelu,
                                            bias=B1[:, hc:hc + 1])
                        E_.i(ins, "ac")
                        E_.setfree(f"f3_ph{par}", E_.ev("ac"))
                    E_.ac(f_gl)
                h_ev = E_.ev("ac")
                E_.setfree("f3_xn1", E_.ev("pe"))

                h2_ev = emit_ln_tile(E_, C, st, HSB, h_ev, 512, LN_EPS,
                                     out_bf=HSB, gamma=L2Gs, beta=L2Bs,
                                     gg=H32, kp="l3b",
                                     ones=C["ones_col_bf"][:])
                E_.setfree("f3_hsb", E_.ev("pe"))

                for dg in range(G):
                    pdb = PD[dg % 2]

                    def f_dn(pe, dg=dg, pdb=pdb, h2_ev=h2_ev):
                        pe.wait(h2_ev)
                        pe.waitb(f"f3_pd{dg % 2}")
                        for k in range(H32):
                            mi = pe.matmul(pdb[:],
                                           WQ2[:, k, dg * 128:(dg + 1) * 128],
                                           HSB[:, k, :],
                                           start=(k == 0), stop=(k == H32 - 1))
                        E_.i(mi, "pe")
                    E_.pe(f_dn)
                    dn_ev = E_.ev("pe")

                    def f_ep(dv, dg=dg, pdb=pdb, dn_ev=dn_ev):
                        dv.wait(dn_ev)
                        dv.tensor_scalar(QT[:], pdb[:], B2[:, dg:dg + 1],
                                         0.5, ALU.add, ALU.mult)
                        ins = dv.tensor_tensor(XF[:, dg, :], QT[:],
                                               XF[:, dg, :], ALU.add)
                        E_.i(ins, "dv")
                        E_.setfree(f"f3_pd{dg % 2}", E_.ev("dv"))
                    E_.dv(f_ep)
                xo_ev = E_.ev("dv")

                def f_st(sy, tt=tt, xo_ev=xo_ev):
                    sy.wait(xo_ev)
                    ins = sy.dma_start(xout[tt], XF[:])
                    E_.i(ins, "qs", 16)
                E_.sy(f_st)
                E_.setfree("f3_xf", E_.ev("qs"))



def emit_att0(E_, C, W, xin, xout):
    nc = E_.nc
    NT = 8
    wint_r = W["a0_int"]
    woutt_r = W["a0_outt"]

    with contextlib.ExitStack() as es:
        uid, sb, ps_ = _mk_alloc(nc, es)
        WIN = sb("a0_win", [128, G, 2048], BF16)
        WOUT = sb("a0_wout", [128, G, 1024], BF16)
        CW = sb("a0_cw_sb", [128, 40], F32)
        CB = sb("a0_cb_sb", [128, G], F32)
        APAR = sb("a0_apar_sb", [128, G], F32)
        ATH = sb("a0_ath", [128, G], F32)
        LNW = sb("a0_lnw_sb", [128, G], F32)
        LNB = sb("a0_lnb_sb", [128, G], F32)
        XF = [sb(f"a0_xf{i}", [128, G, 512], F32) for i in range(2)]
        XNF = [sb(f"a0_xnf{i}", [128, G, 512], F32) for i in range(2)]
        XNB = sb("a0_xnb", [128, G, 512], BF16)
        VB = [sb(f"a0_v{i}", [128, G, 512], F32) for i in range(2)]
        GB = [sb(f"a0_g{i}", [128, G, 512], BF16) for i in range(2)]
        VHL = sb("a0_vhl", [128, G, 2], F32)
        ONESW = sb("a0_ones", [128, 512], F32)
        ABC1 = sb("a0_abc", [128, 512], F32)
        CT = sb("a0_ct", [128, 512], F32)
        GV = sb("a0_gv", [128, 512], F32)
        YS = sb("a0_ys", [128, 512], F32)
        CARRY = sb("a0_carry", [128, G], F32)
        YB = sb("a0_yb", [128, G, 512], BF16)
        PH = [ps_(f"a0_ph{i}", [128, 512])
              for i in range(2)]
        PD = [ps_(f"a0_pd{i}", [128, 512])
              for i in range(2)]
        PMB = ps_("a0_pmb", [128, 512])
        PRBC = ps_("a0_prb", [128, 512])
        st = alloc_ln_scratch(es, nc, "a0_ln", 512, (PD[0], PD[1], PMB, PRBC))

        with E_.block():
            def f_lw(g_):
                ins = g_.dma_start(WIN[:], wint_r)
                E_.i(ins, "qg", 16)
                ins = g_.dma_start(WOUT[:], woutt_r)
                E_.i(ins, "qg", 16)
            E_.po(f_lw)
            wl_ev = E_.ev("qg")

            def f_lp(sy):
                for dst, key in ((CW, "a0_convw"), (CB, "a0_convb"),
                                 (APAR, "a0_a"), (LNW, "a0_lnw"),
                                 (LNB, "a0_lnb")):
                    ins = sy.dma_start(dst[:], W[key])
                    E_.i(ins, "qc", 16)
            E_.sy(f_lp)
            lp_ev = E_.ev("qc")
            E_.setfree("ln_gamma_ld", lp_ev)

            def f_pc(dv):
                dv.wait(lp_ev)
                dv.memset(ONESW[:], 1.0)
                dv.memset(VHL[:], 0.0)
                dv.memset(CARRY[:], 0.0)
                ins = dv.memset(CT[:], 0.0)
                E_.i(ins, "dv")
            E_.dv(f_pc)

            def f_th(ac):
                ac.wait(lp_ev)
                ins = ac.activation(ATH[:], APAR[:], AF.Tanh)
                E_.i(ins, "ac")
            E_.ac(f_th)

        for s_ in range(NT + 1):
            with E_.block():
                cur, prv = s_ % 2, (s_ + 1) % 2
                if s_ < NT:
                    def f_lx(sy, s_=s_, cur=cur):
                        sy.waitb(f"a0_xf{cur}")
                        ins = sy.dma_start(XF[cur][:], xin[s_])
                        E_.i(ins, "qx", 16)
                    E_.sy(f_lx)
                    ld_ev = E_.ev("qx")
                    xn_ev = emit_ln_tile(E_, C, st, XF[cur], ld_ev, 512,
                                         LN_EPS, out_bf=XNB,
                                         out_f32=XNF[cur])
                    for ec in range(16):
                        php = PH[ec % 2]

                        def f_pj(pe, ec=ec, php=php, xn_ev=xn_ev):
                            pe.wait(xn_ev)
                            pe.wait(wl_ev)
                            pe.waitb(f"a0_ph{ec % 2}")
                            for k in range(G):
                                mi = pe.matmul(
                                    php[:], WIN[:, k, ec * 128:(ec + 1) * 128],
                                    XNB[:, k, :], start=(k == 0),
                                    stop=(k == G - 1))
                            E_.i(mi, "pe")
                        E_.pe(f_pj)
                        pj_ev = E_.ev("pe")

                        def f_ev(ac, ec=ec, php=php, pj_ev=pj_ev, cur=cur):
                            ac.wait(pj_ev)
                            if ec < 8:
                                ins = ac.activation(GB[cur][:, ec, :], php[:],
                                                    AF.Sigmoid)
                            else:
                                ins = ac.activation(VB[cur][:, ec - 8, :],
                                                    php[:], AF.Copy)
                            ev = E_.i(ins, "ac")
                            E_.setfree(f"a0_ph{ec % 2}", ev)
                        E_.ac(f_ev)
                    pv_ev = E_.ev("ac")
                else:
                    def f_z(dv, cur=cur):
                        ins = dv.memset(VB[cur][:, :, 0:2], 0.0)
                        E_.i(ins, "dv")
                    E_.dv(f_z)
                    pv_ev = E_.ev("dv")

                if s_ == 0:
                    continue
                ptok0 = (s_ - 1) * 512
                for g in range(G):
                    def f_cv(dv, g=g, prv=prv, cur=cur, pv_ev=pv_ev, s_=s_):
                        dv.wait(pv_ev)
                        vp = VB[prv]
                        vc = VB[cur]
                        dv.tensor_scalar(CT[:, 0:2], VHL[:, g, :],
                                         CW[:, g * 5 + 0:g * 5 + 1], None,
                                         ALU.mult)
                        dv.tensor_scalar(CT[:, 2:512], vp[:, g, 0:510],
                                         CW[:, g * 5 + 0:g * 5 + 1], None,
                                         ALU.mult)
                        dv.scalar_tensor_tensor(CT[:, 0:1], VHL[:, g, 1:2],
                                                CW[:, g * 5 + 1:g * 5 + 2],
                                                CT[:, 0:1], ALU.mult, ALU.add)
                        dv.scalar_tensor_tensor(CT[:, 1:512], vp[:, g, 0:511],
                                                CW[:, g * 5 + 1:g * 5 + 2],
                                                CT[:, 1:512], ALU.mult, ALU.add)
                        dv.scalar_tensor_tensor(CT[:], vp[:, g, :],
                                                CW[:, g * 5 + 2:g * 5 + 3],
                                                CT[:], ALU.mult, ALU.add)
                        dv.scalar_tensor_tensor(CT[:, 0:511], vp[:, g, 1:512],
                                                CW[:, g * 5 + 3:g * 5 + 4],
                                                CT[:, 0:511], ALU.mult, ALU.add)
                        dv.scalar_tensor_tensor(CT[:, 511:512], vc[:, g, 0:1],
                                                CW[:, g * 5 + 3:g * 5 + 4],
                                                CT[:, 511:512], ALU.mult, ALU.add)
                        dv.scalar_tensor_tensor(CT[:, 0:510], vp[:, g, 2:512],
                                                CW[:, g * 5 + 4:g * 5 + 5],
                                                CT[:, 0:510], ALU.mult, ALU.add)
                        dv.scalar_tensor_tensor(CT[:, 510:512], vc[:, g, 0:2],
                                                CW[:, g * 5 + 4:g * 5 + 5],
                                                CT[:, 510:512], ALU.mult, ALU.add)
                        dv.tensor_scalar(CT[:], CT[:], CB[:, g:g + 1], None,
                                         ALU.add)
                        dv.tensor_tensor(GV[:], GB[prv][:, g, :], CT[:],
                                         ALU.mult)
                        dv.tensor_scalar(ABC1[:], ONESW[:], ATH[:, g:g + 1],
                                         None, ALU.mult)
                        dv.tensor_tensor_scan(YS[:], ABC1[:], GV[:],
                                              CARRY[:, g:g + 1], ALU.mult,
                                              ALU.add)
                        dv.tensor_copy(CARRY[:, g:g + 1], YS[:, 511:512])
                        ins = dv.tensor_copy(YB[:, g, :], YS[:])
                        E_.i(ins, "dv")
                        if g == G - 1:
                            ins = dv.tensor_copy(VHL[:], vp[:, :, 510:512])
                            E_.i(ins, "dv")
                    E_.dv(f_cv)
                yb_ev = E_.ev("dv")

                for dg in range(G):
                    pdb = PD[dg % 2]

                    def f_op(pe, dg=dg, pdb=pdb, yb_ev=yb_ev):
                        pe.wait(yb_ev)
                        pe.waitb(f"a0_pd{dg % 2}")
                        for k in range(G):
                            mi = pe.matmul(pdb[:],
                                           WOUT[:, k, dg * 128:(dg + 1) * 128],
                                           YB[:, k, :], start=(k == 0),
                                           stop=(k == G - 1))
                        E_.i(mi, "pe")
                    E_.pe(f_op)
                    op_ev = E_.ev("pe")

                    def f_rs(dv, dg=dg, pdb=pdb, prv=prv, op_ev=op_ev):
                        dv.wait(op_ev)
                        ins = dv.tensor_tensor(XNF[prv][:, dg, :],
                                               XNF[prv][:, dg, :], pdb[:],
                                               ALU.add)
                        ev = E_.i(ins, "dv")
                        E_.setfree(f"a0_pd{dg % 2}", ev)
                    E_.dv(f_rs)
                rs_ev = E_.ev("dv")

                ln2_ev = emit_ln_tile(E_, C, st, XNF[prv], rs_ev, 512, LN_EPS,
                                      out_f32=XNF[prv], gamma=LNW, beta=LNB,
                                      kp="a0l2")

                def f_add(dv, prv=prv, ln2_ev=ln2_ev):
                    dv.wait(ln2_ev)
                    for g in range(G):
                        ins = dv.tensor_tensor(XF[prv][:, g, :],
                                               XF[prv][:, g, :],
                                               XNF[prv][:, g, :], ALU.add)
                    E_.i(ins, "dv")
                E_.dv(f_add)
                xo_ev = E_.ev("dv")

                def f_st(sy, s_=s_, prv=prv, xo_ev=xo_ev):
                    sy.wait(xo_ev)
                    ins = sy.dma_start(xout[s_ - 1], XF[prv][:])
                    E_.i(ins, "qs", 16)
                E_.sy(f_st)
                E_.setfree(f"a0_xf{prv}", E_.ev("qs"))



def emit_att1(E_, C, W, xin, xout):
    nc = E_.nc
    NT = 8
    wq_r = W["a1_wqkv"]

    with contextlib.ExitStack() as es:
        uid, sb, ps_ = _mk_alloc(nc, es)
        KS = sb("a1_ks_sb", [128, G, T], BF16)
        VT = sb("a1_vt", [128, 32, 16, 65], BF16)
        XF = sb("a1_xf", [128, G, 512], F32)
        XNB = sb("a1_xnb", [128, G, 512], BF16)
        QS = XNB
        CTXF = sb("a1_ctxf", [128, G, 512], BF16)
        PSB = [sb(f"a1_p{i}", [128, 512], BF16) for i in range(3)]
        CTB = sb("a1_ctb", [128, 64], BF16)
        RC = sb("a1_rc", [128, 1], F32)
        QSC = sb("a1_qsc_sb", [128, G], F32)
        KSC = sb("a1_ksc_sb", [128, G], F32)
        QS8 = sb("a1_qs8", [128, G], F32)
        BZ = sb("a1_bz", [128, G], F32)
        WSL = [sb(f"a1_wsl{i}", [128, G, 128], BF16) for i in range(2)]
        PH = [ps_(f"a1_ph{i}", [128, 512])
              for i in range(3)]
        PC = [ps_(f"a1_pc{i}", [128, 512])
              for i in range(4)]
        PVT = ps_("a1_pvt", [128, 128], BF16)
        st = alloc_ln_scratch(es, nc, "a1_ln", 512,
                              (PC[0], PC[1], PC[2], PC[3]))

        wslot = [0]

        def load_wslice(col0, name):
            s_ = wslot[0] % 2
            wslot[0] += 1
            buf = WSL[s_]
            wfree = E_.free.get(f"a1_wsl{s_}")
            wsem = "qw" if s_ == 0 else "qv"

            def f_lw(g_, buf=buf, col0=col0, wfree=wfree, wsem=wsem):
                g_.wait(wfree)
                ins = g_.dma_start(buf[:], wq_r[col0 // 128])
                E_.i(ins, wsem, 16)
            E_.po(f_lw)
            return buf, s_, E_.ev(wsem)

        with E_.block():
            def f_lp(sy):
                for dst, key in ((QSC, "a1_qs"), (KSC, "a1_ks")):
                    ins = sy.dma_start(dst[:], W[key])
                    E_.i(ins, "qc", 16)
            E_.sy(f_lp)
            lp_ev = E_.ev("qc")

            def f_pc(dv):
                dv.wait(lp_ev)
                dv.memset(BZ[:], 0.0)
                dv.tensor_scalar(QS8[:], QSC[:], 0.125, None, ALU.mult)
                ins = dv.memset(VT[:, :, :, 64:65], 1.0)
                E_.i(ins, "dv")
            E_.dv(f_pc)
            E_.setfree("ln_gamma_ld", E_.ev("dv"))

        for tt in range(NT):
            with E_.block():
                tok0 = tt * 512

                def f_lx(sy, tt=tt):
                    sy.waitb("a1_xf")
                    ins = sy.dma_start(XF[:], xin[tt])
                    E_.i(ins, "qx", 16)
                E_.sy(f_lx)
                ld_ev = E_.ev("qx")
                xn_ev = emit_ln_tile(E_, C, st, XF, ld_ev, 512, LN_EPS,
                                     out_bf=XNB)
                for ec in range(G):
                    php = PH[ec % 2]
                    wb, ws, wev = load_wslice(1024 + ec * 128, "k")

                    def f_km(pe, ec=ec, php=php, xn_ev=xn_ev, wb=wb, wev=wev):
                        pe.wait(xn_ev)
                        pe.wait(wev)
                        pe.waitb(f"a1_ph{ec % 2}")
                        for kk in range(G):
                            mi = pe.matmul(php[:], wb[:, kk, :],
                                           XNB[:, kk, :], start=(kk == 0),
                                           stop=(kk == G - 1))
                        E_.i(mi, "pe")
                    E_.pe(f_km)
                    km_ev = E_.ev("pe")
                    E_.setfree(f"a1_wsl{ws}", km_ev)

                    def f_ke(dv, ec=ec, php=php, km_ev=km_ev):
                        dv.wait(km_ev)
                        ins = dv.tensor_copy(XF[:, ec, :], php[:])
                        ev = E_.i(ins, "dv")
                        E_.setfree(f"a1_ph{ec % 2}", ev)
                    E_.dv(f_ke)
                kr_ev = E_.ev("dv")
                emit_ln_tile(E_, C, st, XF, kr_ev, 512, 1e-6, out_bf=KS,
                             out_off=tok0, gamma=KSC, beta=BZ, kp="a1k")
                E_.setfree("a1_xf", E_.ev("dv"))
                for ec in range(G):
                    php = PH[ec % 2]
                    wb, ws, wev = load_wslice(2048 + ec * 128, "v")

                    def f_vm(pe, ec=ec, php=php, xn_ev=xn_ev, wb=wb, wev=wev):
                        pe.wait(xn_ev)
                        pe.wait(wev)
                        pe.waitb(f"a1_ph{ec % 2}")
                        for kk in range(G):
                            mi = pe.matmul(
                                php[:], wb[:, kk, :],
                                XNB[:, kk, :], start=(kk == 0),
                                stop=(kk == G - 1))
                        E_.i(mi, "pe")
                    E_.pe(f_vm)
                    vm_ev = E_.ev("pe")
                    E_.setfree(f"a1_wsl{ws}", vm_ev)

                    def f_ve(dv, ec=ec, php=php, vm_ev=vm_ev):
                        dv.wait(vm_ev)
                        dv.waitb("a1_psb0")
                        ins = dv.tensor_copy(PSB[0][:], php[:])
                        ev = E_.i(ins, "dv")
                        E_.setfree(f"a1_ph{ec % 2}", ev)
                    E_.dv(f_ve)
                    ve_ev = E_.ev("dv")

                    for tb in range(4):
                        def f_tr(pe, tb=tb, ve_ev=ve_ev):
                            pe.wait(ve_ev)
                            pe.waitb("a1_pvt")
                            ins = pe.transpose(PVT[:],
                                               PSB[0][:, tb * 128:(tb + 1) * 128],
                                               C["ident_bf16"][:])
                            E_.i(ins, "pe")
                        E_.pe(f_tr)
                        tr_ev = E_.ev("pe")

                        def f_tc(dv, tb=tb, ec=ec, tt=tt, tr_ev=tr_ev):
                            dv.wait(tr_ev)
                            tg = tt * 4 + tb
                            dv.tensor_copy(VT[:, tg, 2 * ec, 0:64],
                                           PVT[:, 0:64])
                            ins = dv.tensor_copy(VT[:, tg, 2 * ec + 1, 0:64],
                                                 PVT[:, 64:128])
                            E_.i(ins, "dv")
                            E_.setfree("a1_pvt", E_.ev("dv"))
                        E_.dv(f_tc)
                    E_.setfree("a1_psb0", E_.ev("pe"))

        wo_r = W["a1_wout"]
        for qb in range(NT):
            with E_.block():
                def f_lx2(sy, qb=qb):
                    sy.waitb("a1_xf")
                    ins = sy.dma_start(XF[:], xin[qb])
                    E_.i(ins, "qx", 16)
                E_.sy(f_lx2)
                ld_ev = E_.ev("qx")
                xn_ev = emit_ln_tile(E_, C, st, XF, ld_ev, 512, LN_EPS,
                                     out_bf=XNB)
                for ec in range(G):
                    php = PH[ec % 2]
                    wb, ws, wev = load_wslice(ec * 128, "q")

                    def f_qm(pe, ec=ec, php=php, xn_ev=xn_ev, wb=wb, wev=wev):
                        pe.wait(xn_ev)
                        pe.wait(wev)
                        pe.waitb(f"a1_ph{ec % 2}")
                        for kk in range(G):
                            mi = pe.matmul(php[:], wb[:, kk, :],
                                           XNB[:, kk, :], start=(kk == 0),
                                           stop=(kk == G - 1))
                        E_.i(mi, "pe")
                    E_.pe(f_qm)
                    qm_ev = E_.ev("pe")
                    E_.setfree(f"a1_wsl{ws}", qm_ev)

                    def f_qe(dv, ec=ec, php=php, qm_ev=qm_ev):
                        dv.wait(qm_ev)
                        ins = dv.tensor_copy(XF[:, ec, :], php[:])
                        ev = E_.i(ins, "dv")
                        E_.setfree(f"a1_ph{ec % 2}", ev)
                    E_.dv(f_qe)
                qr_ev = E_.ev("dv")
                qs_ev = emit_ln_tile(E_, C, st, XF, qr_ev, 512, 1e-6,
                                     out_bf=QS, gamma=QS8, beta=BZ, kp="a1k")
                E_.setfree("a1_xf", E_.ev("dv"))

                nk = 4 * qb + 4
                for h in range(16):
                    hp = 64 * (h % 2)
                    hg = h // 2
                    for kg in range(0, nk, 3):
                        kts = list(range(kg, min(kg + 3, nk)))
                        masked = [kt for kt in kts if kt - 4 * qb >= 0]
                        if masked:
                            def f_pre(dv, masked=masked, qb=qb):
                                for kt in masked:
                                    dv.waitb(f"a1_ph{kt % 3}")
                                    ins = dv.tensor_copy(
                                        PH[kt % 3][:],
                                        C[f"cmask{kt - 4 * qb}"][:])
                                    E_.i(ins, "dv")
                            E_.dv(f_pre)
                        preb = E_.cnt["dv"] - len(masked)

                        def f_scg(pe, kts=kts, masked=masked, hp=hp, hg=hg,
                                  qs_ev=qs_ev, preb=preb):
                            pe.wait(qs_ev)
                            for kt in kts:
                                if kt in masked:
                                    pe.wait(("dv",
                                             preb + 1 + masked.index(kt)))
                                else:
                                    pe.waitb(f"a1_ph{kt % 3}")
                                mi = pe.matmul(
                                    PH[kt % 3][:],
                                    KS[hp:hp + 64, hg, kt * 128:(kt + 1) * 128],
                                    QS[hp:hp + 64, hg, :],
                                    start=(kt not in masked), stop=True)
                                E_.i(mi, "pe")
                        E_.pe(f_scg)
                        scb = E_.cnt["pe"] - len(kts)

                        def f_exg(ac, kts=kts, scb=scb):
                            for i, kt in enumerate(kts):
                                ac.wait(("pe", scb + 1 + i))
                                ac.waitb(f"a1_psb{kt % 3}")
                                ins = ac.activation(PSB[kt % 3][:],
                                                    PH[kt % 3][:], AF.Exp,
                                                    bias=-40.0)
                                ev = E_.i(ins, "ac")
                                E_.setfree(f"a1_ph{kt % 3}", ev)
                        E_.ac(f_exg)
                        exb = E_.cnt["ac"] - len(kts)

                        def f_pvg(pe, kts=kts, h=h, nk=nk, exb=exb):
                            for i, kt in enumerate(kts):
                                pe.wait(("ac", exb + 1 + i))
                                for q4 in range(4):
                                    mi = pe.matmul(
                                        PC[q4][0:128, 0:65],
                                        PSB[kt % 3][:, q4 * 128:(q4 + 1) * 128],
                                        VT[:, kt, h, :],
                                        start=(kt == 0), stop=(kt == nk - 1))
                                    E_.i(mi, "pe")
                                E_.setfree(f"a1_psb{kt % 3}", E_.ev("pe"))
                        E_.pe(f_pvg)
                    pv_ev = E_.ev("pe")

                    for q4 in range(4):
                        def f_nr(dv, q4=q4, pv_ev=pv_ev):
                            dv.wait(pv_ev)
                            dv.waitb("a1_rc")
                            ins = dv.reciprocal(RC[:], PC[q4][:, 64:65])
                            E_.i(ins, "dv")
                        E_.dv(f_nr)
                        nr_ev = E_.ev("dv")

                        def f_sc2(dv, q4=q4, nr_ev=nr_ev):
                            dv.waitb("a1_ctb")
                            ins = dv.tensor_scalar(CTB[:], PC[q4][0:128, 0:64],
                                                   RC[:, 0:1], None, ALU.mult)
                            E_.i(ins, "dv")
                        E_.dv(f_sc2)
                        s2_ev = E_.ev("dv")
                        E_.setfree("a1_rc", s2_ev)

                        def f_tr2(pe, q4=q4, s2_ev=s2_ev):
                            pe.wait(s2_ev)
                            pe.waitb("a1_pvt")
                            ins = pe.transpose(PVT[0:64, :], CTB[:],
                                               C["ident_bf16"][:])
                            E_.i(ins, "pe")
                        E_.pe(f_tr2)
                        t2_ev = E_.ev("pe")
                        E_.setfree("a1_ctb", t2_ev)

                        def f_cc(dv, q4=q4, h=h, hp=hp, hg=hg, t2_ev=t2_ev):
                            dv.wait(t2_ev)
                            ins = dv.tensor_copy(
                                CTXF[hp:hp + 64, hg, q4 * 128:(q4 + 1) * 128],
                                PVT[0:64, :])
                            E_.i(ins, "dv")
                            E_.setfree("a1_pvt", E_.ev("dv"))
                        E_.dv(f_cc)
                ctx_ev = E_.ev("dv")

                def f_lx3(sy, qb=qb, ctx_ev=ctx_ev):
                    sy.waitb("a1_xf")
                    ins = sy.dma_start(XF[:], xin[qb])
                    E_.i(ins, "qx", 16)
                E_.sy(f_lx3)
                ld3_ev = E_.ev("qx")
                for dg in range(G):
                    php = PH[dg % 2]
                    s_ = wslot[0] % 2
                    wslot[0] += 1
                    wob = WSL[s_]
                    wofree = E_.free.get(f"a1_wsl{s_}")
                    wosem = "qw" if s_ == 0 else "qv"

                    def f_low(g_, wob=wob, dg=dg, wofree=wofree, wosem=wosem):
                        g_.wait(wofree)
                        ins = g_.dma_start(wob[:], wo_r[dg])
                        E_.i(ins, wosem, 16)
                    E_.po(f_low)
                    wo_ev = E_.ev(wosem)

                    def f_om(pe, dg=dg, php=php, ctx_ev=ctx_ev, wob=wob,
                             wo_ev=wo_ev):
                        pe.wait(ctx_ev)
                        pe.wait(wo_ev)
                        pe.waitb(f"a1_ph{dg % 2}")
                        for kk in range(G):
                            mi = pe.matmul(php[:], wob[:, kk, :],
                                           CTXF[:, kk, :], start=(kk == 0),
                                           stop=(kk == G - 1))
                        E_.i(mi, "pe")
                    E_.pe(f_om)
                    om_ev = E_.ev("pe")
                    E_.setfree(f"a1_wsl{s_}", om_ev)

                    def f_ra(dv, dg=dg, php=php, om_ev=om_ev, ld3_ev=ld3_ev):
                        dv.wait(om_ev)
                        dv.wait(ld3_ev)
                        ins = dv.tensor_tensor(XF[:, dg, :], XF[:, dg, :],
                                               php[:], ALU.add)
                        ev = E_.i(ins, "dv")
                        E_.setfree(f"a1_ph{dg % 2}", ev)
                    E_.dv(f_ra)
                xo_ev = E_.ev("dv")

                def f_st(sy, qb=qb, xo_ev=xo_ev):
                    sy.wait(xo_ev)
                    ins = sy.dma_start(xout[qb], XF[:])
                    E_.i(ins, "qs", 16)
                E_.sy(f_st)
                E_.setfree("a1_xf", E_.ev("qs"))



def emit_att2(E_, C, W, xin, xout):
    nc = E_.nc
    NB = 8
    wi_r = W["a2_int"]
    wo_r = W["a2_outt"]
    slopes = [2.0 ** (-(h + 1) * 0.5) for h in range(16)]

    with contextlib.ExitStack() as es:
        uid, sb, ps_ = _mk_alloc(nc, es)
        KC = [sb(f"a2_kc{i}", [128, G, 512], BF16) for i in range(2)]
        VT = [sb(f"a2_vt{i}", [128, 4, 16, 65], BF16) for i in range(2)]
        XF = sb("a2_xf", [128, G, 512], F32)
        XNB = sb("a2_xnb", [128, G, 512], BF16)
        QSB = sb("a2_qsb", [128, G, 512], BF16)
        CTXF = sb("a2_ctxf", [128, G, 512], BF16)
        PSB = [sb(f"a2_p{i}", [128, 512], BF16) for i in range(3)]
        CTB = sb("a2_ctb", [128, 64], BF16)
        RC = sb("a2_rc", [128, 1], F32)
        WSL = [sb(f"a2_wsl{i}", [128, G, 128], BF16) for i in range(2)]
        PH = [ps_(f"a2_ph{i}", [128, 512])
              for i in range(3)]
        PC = [ps_(f"a2_pc{i}", [128, 512])
              for i in range(4)]
        PVT = ps_("a2_pvt", [128, 128], BF16)
        st = alloc_ln_scratch(es, nc, "a2_ln", 512,
                              (PC[0], PC[1], PC[2], PC[3]))
        wslot = [0]

        def load_wslice(rsrc, col0):
            s_ = wslot[0] % 2
            wslot[0] += 1
            buf = WSL[s_]
            wfree = E_.free.get(f"a2_wsl{s_}")
            wsem = "qw" if s_ == 0 else "qv"

            def f_lw(g_, buf=buf, rsrc=rsrc, col0=col0, wfree=wfree, wsem=wsem):
                g_.wait(wfree)
                ins = g_.dma_start(buf[:], rsrc[col0 // 128])
                E_.i(ins, wsem, 16)
            E_.po(f_lw)
            return buf, s_, E_.ev(wsem)

        with E_.block():
            def f_pc(dv):
                dv.memset(VT[0][:, :, :, 64:65], 1.0)
                ins = dv.memset(VT[1][:, :, :, 64:65], 1.0)
                E_.i(ins, "dv")
            E_.dv(f_pc)

        for blk in range(NB):
            with E_.block():
                cur, prv = blk % 2, (blk + 1) % 2
                def f_lx(sy, blk=blk):
                    sy.waitb("a2_xf")
                    ins = sy.dma_start(XF[:], xin[blk])
                    E_.i(ins, "qx", 16)
                E_.sy(f_lx)
                ld_ev = E_.ev("qx")
                xn_ev = emit_ln_tile(E_, C, st, XF, ld_ev, 512, LN_EPS,
                                     out_bf=XNB)
                E_.setfree("a2_xf", xn_ev)
                for ec in range(24):
                    php = PH[ec % 2]
                    wb, ws, wev = load_wslice(wi_r, ec * 128)

                    def f_pj(pe, php=php, wb=wb, wev=wev, xn_ev=xn_ev,
                             ec=ec):
                        pe.wait(xn_ev)
                        pe.wait(wev)
                        pe.waitb(f"a2_ph{ec % 2}")
                        for kk in range(G):
                            mi = pe.matmul(php[:], wb[:, kk, :], XNB[:, kk, :],
                                           start=(kk == 0), stop=(kk == G - 1))
                        E_.i(mi, "pe")
                    E_.pe(f_pj)
                    pj_ev = E_.ev("pe")
                    E_.setfree(f"a2_wsl{ws}", pj_ev)

                    if ec < 8:
                        def f_ev(ac, ec=ec, php=php, pj_ev=pj_ev):
                            ac.wait(pj_ev)
                            ins = ac.activation(QSB[:, ec, :], php[:], AF.Copy,
                                                scale=0.125)
                            ev = E_.i(ins, "ac")
                            E_.setfree(f"a2_ph{ec % 2}", ev)
                        E_.ac(f_ev)
                    elif ec < 16:
                        def f_ev(ac, ec=ec, php=php, pj_ev=pj_ev, cur=cur):
                            ac.wait(pj_ev)
                            ins = ac.activation(KC[cur][:, ec - 8, :], php[:],
                                                AF.Copy)
                            ev = E_.i(ins, "ac")
                            E_.setfree(f"a2_ph{ec % 2}", ev)
                        E_.ac(f_ev)
                    else:
                        def f_ev(ac, ec=ec, php=php, pj_ev=pj_ev):
                            ac.wait(pj_ev)
                            ac.waitb("a2_psb0")
                            ins = ac.activation(PSB[0][:], php[:], AF.Copy)
                            ev = E_.i(ins, "ac")
                            E_.setfree(f"a2_ph{ec % 2}", ev)
                        E_.ac(f_ev)
                        ve_ev = E_.ev("ac")
                        vc = ec - 16
                        for tb in range(4):
                            def f_tr(pe, tb=tb, ve_ev=ve_ev):
                                pe.wait(ve_ev)
                                pe.waitb("a2_pvt")
                                ins = pe.transpose(
                                    PVT[:], PSB[0][:, tb * 128:(tb + 1) * 128],
                                    C["ident_bf16"][:])
                                E_.i(ins, "pe")
                            E_.pe(f_tr)
                            tr_ev = E_.ev("pe")

                            def f_tc(dv, tb=tb, vc=vc, cur=cur, tr_ev=tr_ev):
                                dv.wait(tr_ev)
                                dv.tensor_copy(VT[cur][:, tb, 2 * vc, 0:64],
                                               PVT[:, 0:64])
                                ins = dv.tensor_copy(
                                    VT[cur][:, tb, 2 * vc + 1, 0:64],
                                    PVT[:, 64:128])
                                E_.i(ins, "dv")
                                E_.setfree("a2_pvt", E_.ev("dv"))
                            E_.dv(f_tc)
                        E_.setfree("a2_psb0", E_.ev("pe"))
                qkv_ev = [E_.ev("ac"), E_.ev("dv")]

                kts_all = list(range(4, 8) if blk == 0 else range(8))
                for h in range(16):
                    hp, hg = 64 * (h % 2), h // 2
                    for kg in range(0, len(kts_all), 3):
                        kts = kts_all[kg:kg + 3]

                        def f_pre(dv, kts=kts, h=h, blk=blk):
                            for kt in kts:
                                dv.waitb(f"a2_ph{kt % 3}")
                                dv.scalar_tensor_tensor(
                                    PH[kt % 3][:], C[f"l2delta{kt}"][:],
                                    slopes[h], C[f"l2mask{kt}"][:],
                                    ALU.mult, ALU.add)
                                if blk == 0:
                                    dv.scalar_tensor_tensor(
                                        PH[kt % 3][:], C["qidx"][:], -slopes[h],
                                        PH[kt % 3][:], ALU.mult, ALU.add)
                        E_.dv(f_pre)
                        per = 2 if blk == 0 else 1
                        preb = E_.cnt["dv"] - per * len(kts)

                        def f_scg(pe, kts=kts, hp=hp, hg=hg, qkv_ev=qkv_ev,
                                  cur=cur, prv=prv, preb=preb, per=per):
                            pe.wait(qkv_ev)
                            for i, kt in enumerate(kts):
                                ksrc = KC[cur] if kt >= 4 else KC[prv]
                                pe.wait(("dv", preb + per * (i + 1)))
                                mi = pe.matmul(
                                    PH[kt % 3][:],
                                    ksrc[hp:hp + 64, hg,
                                         (kt % 4) * 128:(kt % 4 + 1) * 128],
                                    QSB[hp:hp + 64, hg, :], start=False,
                                    stop=True)
                                E_.i(mi, "pe")
                        E_.pe(f_scg)
                        scb = E_.cnt["pe"] - len(kts)

                        def f_exg(ac, kts=kts, h=h, blk=blk, scb=scb):
                            for i, kt in enumerate(kts):
                                ac.wait(("pe", scb + 1 + i))
                                ac.waitb(f"a2_psb{kt % 3}")
                                bias = 0.0 if blk == 0 else -512.0 * slopes[h]
                                ins = ac.activation(PSB[kt % 3][:],
                                                    PH[kt % 3][:], AF.Exp,
                                                    bias=bias)
                                ev = E_.i(ins, "ac")
                                E_.setfree(f"a2_ph{kt % 3}", ev)
                        E_.ac(f_exg)
                        exb = E_.cnt["ac"] - len(kts)

                        def f_pvg(pe, kts=kts, h=h, exb=exb, cur=cur, prv=prv,
                                  kts_all=kts_all):
                            for i, kt in enumerate(kts):
                                vsrc = VT[cur] if kt >= 4 else VT[prv]
                                pe.wait(("ac", exb + 1 + i))
                                for q4 in range(4):
                                    mi = pe.matmul(
                                        PC[q4][0:128, 0:65],
                                        PSB[kt % 3][:, q4 * 128:(q4 + 1) * 128],
                                        vsrc[:, kt % 4, h, :],
                                        start=(kt == kts_all[0]),
                                        stop=(kt == 7))
                                    E_.i(mi, "pe")
                                E_.setfree(f"a2_psb{kt % 3}", E_.ev("pe"))
                        E_.pe(f_pvg)
                    pv_ev = E_.ev("pe")

                    for q4 in range(4):
                        def f_nr(dv, q4=q4, pv_ev=pv_ev):
                            dv.wait(pv_ev)
                            dv.waitb("a2_rc")
                            ins = dv.reciprocal(RC[:], PC[q4][:, 64:65])
                            E_.i(ins, "dv")
                        E_.dv(f_nr)
                        nr_ev = E_.ev("dv")

                        def f_sc2(dv, q4=q4, nr_ev=nr_ev):
                            dv.waitb("a2_ctb")
                            ins = dv.tensor_scalar(CTB[:], PC[q4][0:128, 0:64],
                                                   RC[:, 0:1], None, ALU.mult)
                            E_.i(ins, "dv")
                        E_.dv(f_sc2)
                        s2_ev = E_.ev("dv")
                        E_.setfree("a2_rc", s2_ev)

                        def f_tr2(pe, q4=q4, s2_ev=s2_ev):
                            pe.wait(s2_ev)
                            pe.waitb("a2_pvt")
                            ins = pe.transpose(PVT[0:64, :], CTB[:],
                                               C["ident_bf16"][:])
                            E_.i(ins, "pe")
                        E_.pe(f_tr2)
                        t2_ev = E_.ev("pe")
                        E_.setfree("a2_ctb", t2_ev)

                        def f_cc(dv, q4=q4, hp=hp, hg=hg, t2_ev=t2_ev):
                            dv.wait(t2_ev)
                            ins = dv.tensor_copy(
                                CTXF[hp:hp + 64, hg, q4 * 128:(q4 + 1) * 128],
                                PVT[0:64, :])
                            E_.i(ins, "dv")
                            E_.setfree("a2_pvt", E_.ev("dv"))
                        E_.dv(f_cc)
                ctx_ev = E_.ev("dv")

                def f_lx3(sy, blk=blk, ctx_ev=ctx_ev):
                    sy.waitb("a2_xf")
                    ins = sy.dma_start(XF[:], xin[blk])
                    E_.i(ins, "qx", 16)
                E_.sy(f_lx3)
                ld3_ev = E_.ev("qx")
                for dg in range(G):
                    php = PH[dg % 2]
                    wb, ws, wev = load_wslice(wo_r, dg * 128)

                    def f_om(pe, dg=dg, php=php, ctx_ev=ctx_ev, wb=wb,
                             wev=wev):
                        pe.wait(ctx_ev)
                        pe.wait(wev)
                        pe.waitb(f"a2_ph{dg % 2}")
                        for kk in range(G):
                            mi = pe.matmul(php[:], wb[:, kk, :],
                                           CTXF[:, kk, :], start=(kk == 0),
                                           stop=(kk == G - 1))
                        E_.i(mi, "pe")
                    E_.pe(f_om)
                    om_ev = E_.ev("pe")
                    E_.setfree(f"a2_wsl{ws}", om_ev)

                    def f_ra(dv, dg=dg, php=php, om_ev=om_ev, ld3_ev=ld3_ev):
                        dv.wait(om_ev)
                        dv.wait(ld3_ev)
                        ins = dv.tensor_tensor(XF[:, dg, :], XF[:, dg, :],
                                               php[:], ALU.add)
                        ev = E_.i(ins, "dv")
                        E_.setfree(f"a2_ph{dg % 2}", ev)
                    E_.dv(f_ra)
                xo_ev = E_.ev("dv")

                def f_st(sy, blk=blk, xo_ev=xo_ev):
                    sy.wait(xo_ev)
                    ins = sy.dma_start(xout[blk], XF[:])
                    E_.i(ins, "qs", 16)
                E_.sy(f_st)
                E_.setfree("a2_xf", E_.ev("qs"))



def emit_att3(E_, C, W, xin, xout):
    nc = E_.nc
    CH = 256
    NCH = T // CH
    G16 = 16
    wi_r = W["a3_int"]
    wo_r = W["a3_outt"]
    xp_r = W["a3_xpt"]

    with contextlib.ExitStack() as es:
        uid, sb, ps_ = _mk_alloc(nc, es)
        WOP = sb("a3_wop", [128, G16, 1024], BF16)
        WXP = sb("a3_wxp", [128, G16, 96], F32)
        WDT = sb("a3_wdt", [64, 2048], BF16)
        WSL = [sb(f"a3_wsl{i}", [128, G, 128], BF16) for i in range(2)]
        NW = sb("a3_nw_sb", [128, G], F32)
        CW = sb("a3_cw2", [128, 64], F32)
        CB = sb("a3_cb2", [128, G16], F32)
        DTB = sb("a3_dtb_sb", [128, G16], F32)
        DP = sb("a3_dp", [128, G16], F32)
        AN = sb("a3_an", [128, 256], F32)
        XF = sb("a3_xf2", [128, G, CH], F32)
        XNF = XF
        HINB = sb("a3_hinb", [128, G, CH], BF16)
        XMF = sb("a3_xmf", [128, G16, CH], F32)
        XCF = sb("a3_xcf", [128, G16, CH], F32)
        SZ = sb("a3_sz", [128, G16, CH], BF16)
        DT = XMF
        DTR = sb("a3_dtr", [64, CH], BF16)
        BR = sb("a3_br", [16, CH], F32)
        CR = sb("a3_cr", [16, CH], F32)
        BBC = sb("a3_bbc", [128, CH, 16], F32)
        CBC = sb("a3_cbc", [128, CH, 16], BF16)
        DAB = sb("a3_dab", [128, CH, 16], F32)
        DBU = sb("a3_dbu", [128, 2, CH, 16], F32)
        YG = sb("a3_yg", [128, CH], F32)
        UT = sb("a3_ut", [128, CH], F32)
        Y2B = sb("a3_y2b", [128, G16, CH], BF16)
        CAR = sb("a3_car", [128, G16, 16], F32)
        XH = sb("a3_xh", [128, G16, 3], F32)
        PH = [ps_(f"a3_ph{i}", [128, 512])
              for i in range(2)]
        PC = [ps_(f"a3_pc{i}", [128, 512])
              for i in range(4)]
        st = alloc_ln_scratch(es, nc, "a3_ln", CH,
                              (PC[0], PC[1], PC[2], PC[3]))
        wslot = [0]

        def load_wslice(rsrc, col0):
            s_ = wslot[0] % 2
            wslot[0] += 1
            buf = WSL[s_]
            wfree = E_.free.get(f"a3_wsl{s_}")
            wsem = "qw" if s_ == 0 else "qv"

            def f_lw(g_, buf=buf, rsrc=rsrc, col0=col0, wfree=wfree, wsem=wsem):
                g_.wait(wfree)
                ins = g_.dma_start(buf[:], rsrc[col0 // 128])
                E_.i(ins, wsem, 16)
            E_.po(f_lw)
            return buf, s_, E_.ev(wsem)

        with E_.block():
            def f_lw0(g_):
                ins = g_.dma_start(WOP[:], wo_r)
                E_.i(ins, "qg", 16)
                ins = g_.dma_start(WXP[:], xp_r)
                E_.i(ins, "qg", 16)
                ins = g_.dma_start(WDT[:], W["a3_dtpt"])
                E_.i(ins, "qg", 16)
            E_.po(f_lw0)
            w0_ev = E_.ev("qg")

            def f_lp(sy):
                for dst, key in ((NW, "a3_nw"), (CW, "a3_convw"),
                                 (CB, "a3_convb"), (DTB, "a3_dtb"),
                                 (DP, "a3_d"), (AN, "a3_alog")):
                    ins = sy.dma_start(dst[:], W[key])
                    E_.i(ins, "qc", 16)
            E_.sy(f_lp)
            lp_ev = E_.ev("qc")

            def f_an(ac, lp_ev=lp_ev):
                ac.wait(lp_ev)
                ins = ac.activation(AN[:], AN[:], AF.Exp)
                E_.i(ins, "ac")
            E_.ac(f_an)
            an_ev = E_.ev("ac")

            def f_an2(dv, an_ev=an_ev):
                dv.wait(an_ev)
                dv.tensor_scalar(AN[:], AN[:], -1.0, None, ALU.mult)
                dv.memset(CAR[:], 0.0)
                ins = dv.memset(XH[:], 0.0)
                E_.i(ins, "dv")
            E_.dv(f_an2)

        ANv = AN[:].rearrange("p (g n) -> p g n", g=16)
        for ci in range(NCH):
            with E_.block():
                tok0 = ci * CH

                def f_lx(sy, ci=ci):
                    sy.waitb("a3_xf")
                    off = (ci % 2) * CH
                    ins = sy.dma_start(XF[:], xin[ci // 2][:, :, off:off + CH])
                    E_.i(ins, "qx", 16)
                E_.sy(f_lx)
                ld_ev = E_.ev("qx")
                xn_ev = emit_ln_tile(E_, C, st, XF, ld_ev, CH, LN_EPS,
                                     out_f32=XNF)

                for g in range(G):
                    par = g % 2

                    def f_sq(ac, g=g, par=par, xn_ev=xn_ev):
                        ac.wait(xn_ev)
                        ac.waitb(f"a3_sq{par}")
                        ins = ac.activation(st["SQ"][:, par, :], XNF[:, g, :],
                                            AF.Square)
                        E_.i(ins, "ac")
                    E_.ac(f_sq)
                    sq_ev = E_.ev("ac")

                    def f_s1(pe, g=g, par=par, sq_ev=sq_ev):
                        pe.wait(sq_ev)
                        if g == 0:
                            pe.waitb("a3_ssq")
                        mi = pe.matmul(st["S1"], C["ones_col"][:],
                                       st["SQ"][:, par, :], start=(g == 0),
                                       stop=(g == G - 1))
                        ev = E_.i(mi, "pe")
                        E_.setfree(f"a3_sq{par}", ev)
                    E_.pe(f_s1)
                ssq_ev = E_.ev("pe")

                def f_rms(ac, ssq_ev=ssq_ev):
                    ac.wait(ssq_ev)
                    ac.activation(st["M"][:], st["S1"], AF.Sqrt)
                    ins = ac.activation(st["M"][:], st["M"][:], AF.Copy,
                                        scale=1.0 / 32.0)
                    E_.i(ins, "ac")
                E_.ac(f_rms)
                rms_ev = E_.ev("ac")

                def f_ri(dv, rms_ev=rms_ev):
                    dv.wait(rms_ev)
                    dv.tensor_scalar(st["M"][:], st["M"][:], 1e-5, None,
                                     ALU.add)
                    ins = dv.reciprocal(st["M"][:], st["M"][:])
                    E_.i(ins, "dv")
                E_.dv(f_ri)
                ri_ev = E_.ev("dv")
                E_.setfree("a3_ssq", ri_ev)

                def f_rb(pe, ri_ev=ri_ev):
                    pe.wait(ri_ev)
                    pe.waitb("a3_rb")
                    ins = pe.matmul(st["RB"], C["ones_row"][:], st["M"][:],
                                    start=True, stop=True)
                    E_.i(ins, "pe")
                E_.pe(f_rb)
                rb_ev = E_.ev("pe")

                def f_hin(dv, rb_ev=rb_ev):
                    dv.wait(rb_ev)
                    dv.waitb("ln_gamma_ld")
                    dv.waitb("a3_hinb")
                    for g in range(G):
                        dv.tensor_tensor(st["T1"][:], XNF[:, g, :], st["RB"],
                                         ALU.mult)
                        ins = dv.tensor_scalar(HINB[:, g, :], st["T1"][:],
                                               NW[:, g:g + 1], None, ALU.mult)
                    E_.i(ins, "dv")
                E_.dv(f_hin)
                hin_ev = E_.ev("dv")
                E_.setfree("a3_rb", hin_ev)
                E_.setfree("a3_xf", hin_ev)

                for ec in range(32):
                    php = PH[ec % 2]
                    wb, ws, wev = load_wslice(wi_r, ec * 128)

                    def f_ip(pe, php=php, wb=wb, wev=wev, hin_ev=hin_ev,
                             ec=ec):
                        pe.wait(hin_ev)
                        pe.wait(wev)
                        pe.waitb(f"a3_ph{ec % 2}")
                        for kk in range(G):
                            mi = pe.matmul(php[0:128, 0:CH], wb[:, kk, :],
                                           HINB[:, kk, :], start=(kk == 0),
                                           stop=(kk == G - 1))
                        E_.i(mi, "pe")
                    E_.pe(f_ip)
                    ip_ev = E_.ev("pe")
                    E_.setfree(f"a3_wsl{ws}", ip_ev)

                    def f_ie(ac, ec=ec, php=php, ip_ev=ip_ev):
                        ac.wait(ip_ev)
                        ac.waitb("a3_xmf" if ec < 16 else "a3_sz")
                        if ec < 16:
                            ins = ac.activation(XMF[:, ec, :], php[0:128, 0:CH],
                                                AF.Copy)
                        else:
                            ins = ac.activation(SZ[:, ec - 16, :],
                                                php[0:128, 0:CH], AF.Silu)
                        ev = E_.i(ins, "ac")
                        E_.setfree(f"a3_ph{ec % 2}", ev)
                    E_.ac(f_ie)
                xm_ev = E_.ev("ac")

                for g in range(G16):
                    def f_cv(dv, g=g, xm_ev=xm_ev):
                        dv.wait(xm_ev)
                        dv.waitb("a3_xcf")
                        dv.tensor_scalar(UT[:, 0:3], XH[:, g, :],
                                         CW[:, g * 4:g * 4 + 1], None, ALU.mult)
                        dv.tensor_scalar(UT[:, 3:CH], XMF[:, g, 0:CH - 3],
                                         CW[:, g * 4:g * 4 + 1], None, ALU.mult)
                        for k2 in range(1, 4):
                            nh = 3 - k2
                            if nh > 0:
                                dv.scalar_tensor_tensor(
                                    UT[:, 0:nh], XH[:, g, 3 - nh:3],
                                    CW[:, g * 4 + k2:g * 4 + k2 + 1],
                                    UT[:, 0:nh], ALU.mult, ALU.add)
                            dv.scalar_tensor_tensor(
                                UT[:, nh:CH], XMF[:, g, 0:CH - nh],
                                CW[:, g * 4 + k2:g * 4 + k2 + 1],
                                UT[:, nh:CH], ALU.mult, ALU.add)
                        ins = dv.tensor_scalar(UT[:], UT[:], CB[:, g:g + 1],
                                               None, ALU.add)
                        E_.i(ins, "dv")
                        if g == G16 - 1:
                            ins = dv.tensor_copy(XH[:], XMF[:, :, CH - 3:CH])
                            E_.i(ins, "dv")
                    E_.dv(f_cv)
                    cv_ev = E_.ev("dv")

                    def f_si(ac, g=g, cv_ev=cv_ev):
                        ac.wait(cv_ev)
                        ins = ac.activation(XCF[:, g, :], UT[:], AF.Silu)
                        E_.i(ins, "ac")
                    E_.ac(f_si)
                xc_ev = [E_.ev("dv"), E_.ev("ac")]
                E_.setfree("a3_xmf", E_.ev("dv"))

                def f_xp(pe, xc_ev=xc_ev):
                    pe.wait(xc_ev)
                    pe.wait(w0_ev)
                    pe.waitb("a3_pdt")
                    for kk in range(G16):
                        mi = pe.matmul(PH[0][0:64, 0:CH], WXP[:, kk, 0:64],
                                       XCF[:, kk, :], start=(kk == 0),
                                       stop=(kk == G16 - 1))
                    E_.i(mi, "pe")
                    pe.waitb("a3_pbc")
                    for kk in range(G16):
                        mi = pe.matmul(PC[2][0:16, 0:CH], WXP[:, kk, 64:80],
                                       XCF[:, kk, :], start=(kk == 0),
                                       stop=(kk == G16 - 1))
                    E_.i(mi, "pe")
                    for kk in range(G16):
                        mi = pe.matmul(PC[3][0:16, 0:CH], WXP[:, kk, 80:96],
                                       XCF[:, kk, :], start=(kk == 0),
                                       stop=(kk == G16 - 1))
                    E_.i(mi, "pe")
                E_.pe(f_xp)
                xp_ev = E_.ev("pe")

                def f_xe(ac, xp_ev=xp_ev):
                    ac.wait(xp_ev)
                    ac.waitb("a3_dtr")
                    ins = ac.activation(DTR[:], PH[0][0:64, 0:CH], AF.Copy)
                    E_.i(ins, "ac")
                E_.ac(f_xe)
                dtr_ev = E_.ev("ac")
                E_.setfree("a3_pdt", dtr_ev)

                def f_bc(dv, xp_ev=xp_ev):
                    dv.wait(xp_ev)
                    dv.tensor_copy(BR[:], PC[2][0:16, 0:CH])
                    ins = dv.tensor_copy(CR[:], PC[3][0:16, 0:CH])
                    E_.i(ins, "dv")
                E_.dv(f_bc)
                br_ev = E_.ev("dv")
                E_.setfree("a3_pbc", br_ev)

                for n in range(16):
                    def f_bb(pe, n=n, br_ev=br_ev, dtr_ev=dtr_ev):
                        pe.wait(br_ev)
                        pe.wait(dtr_ev)
                        pe.waitb("a3_pbb")
                        pe.waitb("a3_ph0")
                        pe.waitb("a3_ph1")
                        mi = pe.matmul(PH[0][0:128, 0:CH],
                                       C["sel16"][0:16, n * 128:(n + 1) * 128],
                                       BR[:], start=True, stop=True)
                        E_.i(mi, "pe")
                        mi = pe.matmul(PH[1][0:128, 0:CH],
                                       C["sel16"][0:16, n * 128:(n + 1) * 128],
                                       CR[:], start=True, stop=True)
                        E_.i(mi, "pe")
                    E_.pe(f_bb)
                    bb_ev = E_.ev("pe")

                    def f_bcp(dv, n=n, bb_ev=bb_ev):
                        dv.wait(bb_ev)
                        dv.waitb("a3_bbc")
                        dv.tensor_copy(BBC[:, :, n], PH[0][0:128, 0:CH])
                        ins = dv.tensor_copy(CBC[:, :, n], PH[1][0:128, 0:CH])
                        E_.i(ins, "dv")
                        E_.setfree("a3_pbb", E_.ev("dv"))
                    E_.dv(f_bcp)
                bc_ev = E_.ev("dv")

                for g in range(G16):
                    php = PH[g % 2]

                    def f_dm(pe, g=g, php=php, dtr_ev=dtr_ev):
                        pe.wait(dtr_ev)
                        pe.waitb("a3_pbb")
                        pe.waitb(f"a3_ph{g % 2}")
                        mi = pe.matmul(php[0:128, 0:CH],
                                       WDT[0:64, g * 128:(g + 1) * 128],
                                       DTR[:], start=True, stop=True)
                        E_.i(mi, "pe")
                    E_.pe(f_dm)
                    dm_ev = E_.ev("pe")

                    def f_sp(dv, g=g, php=php, dm_ev=dm_ev):
                        dv.wait(dm_ev)
                        ins = dv.tensor_scalar(DT[:, g, :], php[0:128, 0:CH],
                                               DTB[:, g:g + 1], None, ALU.add)
                        ev = E_.i(ins, "dv")
                        E_.setfree(f"a3_ph{g % 2}", ev)
                    E_.dv(f_sp)
                x0_ev = E_.ev("dv")

                def f_spb(ac, x0_ev=x0_ev):
                    ac.wait(x0_ev)
                    for g in range(G16):
                        ins = ac.activation(DAB[:, :, g], DT[:, g, :], AF.Abs)
                    E_.i(ins, "ac")
                    for g in range(G16):
                        ins = ac.activation(DAB[:, :, g], DAB[:, :, g],
                                            AF.Exp, scale=-1.0)
                    E_.i(ins, "ac")
                    for g in range(G16):
                        ins = ac.activation(DAB[:, :, g], DAB[:, :, g],
                                            AF.Ln, bias=1.0)
                    E_.i(ins, "ac")
                E_.ac(f_spb)
                spb_ev = E_.ev("ac")

                def f_spf(dv, spb_ev=spb_ev):
                    dv.wait(spb_ev)
                    for g in range(G16):
                        dv.tensor_scalar(DT[:, g, :], DT[:, g, :], 0.0, None,
                                         ALU.max)
                        ins = dv.tensor_tensor(DT[:, g, :], DT[:, g, :],
                                               DAB[:, :, g], ALU.add)
                    E_.i(ins, "dv")
                E_.dv(f_spf)
                dt_ev = E_.ev("dv")

                for g in range(G16):
                    par = g % 2
                    DBUg = DBU[:, par]
                    HWg = DBUg

                    def f_u(g_, g=g, par=par, dt_ev=dt_ev):
                        g_.wait(dt_ev)
                        g_.waitb(f"a3_dbu{par}")
                        g_.tensor_tensor(UT[:], DT[:, g, :], XCF[:, g, :],
                                         ALU.mult)
                        for n in range(16):
                            g_.tensor_tensor(DBU[:, par, :, n], UT[:],
                                             BBC[:, :, n], ALU.mult)
                    E_.po(f_u)
                    pob = E_.cnt["po"] - 16

                    def f_dag(ac, g=g, dt_ev=dt_ev):
                        ac.wait(dt_ev)
                        for n in range(16):
                            ac.waitb(f"a3_dab{n}")
                            ins = ac.activation(DAB[:, :, n], DT[:, g, :],
                                                AF.Exp,
                                                scale=ANv[:, g, n:n + 1])
                            E_.i(ins, "ac")
                    E_.ac(f_dag)
                    dab_ = E_.cnt["ac"] - 16

                    def f_dbg(dv, g=g, par=par, dab_=dab_, pob=pob):
                        for n in range(16):
                            dv.wait(("ac", dab_ + 1 + n))
                            dv.wait(("po", pob + 1 + n))
                            ins = dv.tensor_tensor_scan(
                                DBU[:, par, :, n], DAB[:, :, n],
                                DBU[:, par, :, n],
                                CAR[:, g, n:n + 1], ALU.mult, ALU.add)
                            ev = E_.i(ins, "dv")
                            E_.setfree(f"a3_dab{n}", ev)
                    E_.dv(f_dbg)
                    def f_y(dv, g=g, par=par):
                        dv.tensor_copy(CAR[:, g, :], DBU[:, par, CH - 1, :])
                        dv.tensor_tensor(DBU[:, par], DBU[:, par], CBC[:],
                                         ALU.mult)
                        ins = dv.tensor_reduce(YG[:], DBU[:, par, :, :],
                                               axis=AX.X, op=ALU.add)
                        ev = E_.i(ins, "dv")
                        E_.setfree(f"a3_dbu{par}", ev)
                        YT = st["T1"]
                        dv.scalar_tensor_tensor(YT[:], XCF[:, g, :],
                                                DP[:, g:g + 1], YG[:],
                                                ALU.mult, ALU.add)
                        ins = dv.tensor_tensor(Y2B[:, g, :], YT[:],
                                               SZ[:, g, :], ALU.mult)
                        E_.i(ins, "dv")
                    E_.dv(f_y)
                y_ev = E_.ev("dv")
                E_.setfree("a3_xcf", y_ev)
                E_.setfree("a3_sz", y_ev)
                E_.setfree("a3_bbc", y_ev)
                E_.setfree("a3_hinb", E_.ev("pe"))

                def f_lx2(sy, ci=ci, y_ev=y_ev):
                    sy.waitb("a3_xf")
                    off = (ci % 2) * CH
                    ins = sy.dma_start(XF[:], xin[ci // 2][:, :, off:off + CH])
                    E_.i(ins, "qx", 16)
                E_.sy(f_lx2)
                ld2_ev = E_.ev("qx")
                for dg in range(G):
                    php = PH[dg % 2]

                    def f_op(pe, dg=dg, php=php, y_ev=y_ev):
                        pe.wait(y_ev)
                        pe.waitb(f"a3_ph{dg % 2}")
                        for kk in range(G16):
                            mi = pe.matmul(php[0:128, 0:CH],
                                           WOP[:, kk, dg * 128:(dg + 1) * 128],
                                           Y2B[:, kk, :], start=(kk == 0),
                                           stop=(kk == G16 - 1))
                        E_.i(mi, "pe")
                    E_.pe(f_op)
                    op_ev = E_.ev("pe")

                    def f_ra(dv, dg=dg, php=php, op_ev=op_ev, ld2_ev=ld2_ev):
                        dv.wait(op_ev)
                        dv.wait(ld2_ev)
                        ins = dv.tensor_tensor(XF[:, dg, :], XF[:, dg, :],
                                               php[0:128, 0:CH], ALU.add)
                        ev = E_.i(ins, "dv")
                        E_.setfree(f"a3_ph{dg % 2}", ev)
                    E_.dv(f_ra)
                xo_ev = E_.ev("dv")

                def f_st(sy, ci=ci, xo_ev=xo_ev):
                    sy.wait(xo_ev)
                    off = (ci % 2) * CH
                    ins = sy.dma_start(xout[ci // 2][:, :, off:off + CH], XF[:])
                    E_.i(ins, "qs", 16)
                E_.sy(f_st)
                E_.setfree("a3_xf", E_.ev("qs"))



def _c(a):
    return np.ascontiguousarray(a)


def _pp(v, g):
    return _c(v.reshape(g, 128).T.astype(np.float32))


def _pkn(wT):
    K = wT.shape[0] // 128
    return _c(np.asarray(wT).reshape(K, 128, wT.shape[1]).transpose(1, 0, 2))


def _blkc(wT, w):
    KK, N = wT.shape
    K = KK // 128
    return _c(np.asarray(wT).reshape(K, 128, N // w, w).transpose(2, 1, 0, 3))


def _blkr(wT, kw):
    KK, N = wT.shape
    nt = KK // (128 * kw)
    return _c(np.asarray(wT).reshape(nt, kw, 128, N).transpose(0, 2, 1, 3))


def _blkx(xb):
    Tn, Dn = xb.shape
    return _c(np.asarray(xb).reshape(Tn // 512, 512, Dn // 128, 128)
              .transpose(0, 3, 2, 1))


def _unblkx(arr):
    nt, p, g, w = arr.shape
    return np.ascontiguousarray(arr.transpose(0, 3, 2, 1).reshape(nt * w, g * p))


def _prep_shared(I):
    S = {}
    S["ones_col"] = np.ones((128, 1), np.float32)
    S["ones_row"] = np.ones((1, 128), np.float32)
    S["ident_f32"] = np.eye(128, dtype=np.float32)
    sel8 = np.zeros((8, 8 * 128), np.float32)
    for e in range(8):
        sel8[e, e * 128:(e + 1) * 128] = 1.0
    S["sel8"] = sel8
    sel16 = np.zeros((16, 16 * 128), np.float32)
    for n in range(16):
        sel16[n, n * 128:(n + 1) * 128] = 1.0
    S["sel16"] = sel16
    qf = np.arange(512)[None, :]
    kf = np.arange(128)[:, None]
    for j in range(4):
        S[f"cmask{j}"] = np.where(qf >= 128 * j + kf, 0.0, -30000.0).astype(np.float32)
    qi = np.arange(512)[None, :]
    for grp in range(8):
        kj = (np.arange(128) + grp * 128)[:, None]
        delta = qi - kj + 512
        valid = (delta >= 0) & (delta <= 512)
        S[f"l2delta{grp}"] = delta.astype(np.float32)
        S[f"l2mask{grp}"] = np.where(valid, 0.0, -30000.0).astype(np.float32)
    S["qidx"] = np.broadcast_to(np.arange(512, dtype=np.float32)[None, :],
                                (128, 512)).copy()

    S["f0_gatew"] = _pkn(I["L0_ffn_gate_w"].T)
    S["f0_gateb"] = _c(np.asarray(I["L0_ffn_gate_bias"], np.float32)[None, :])
    S["f0_w1t"] = np.stack([_blkc(np.asarray(I["L0_ffn_w1"][e]).T, 512)
                            for e in range(8)])
    S["f0_w3t"] = np.stack([_blkc(np.asarray(I["L0_ffn_w3"][e]).T, 512)
                            for e in range(8)])
    S["f0_w2t"] = np.stack([_blkr(np.asarray(I["L0_ffn_w2"][e]).T, 4)
                            for e in range(8)])
    S["f0_sw1t"] = _blkc(np.asarray(I["L0_ffn_sw1"]).T, 512)
    S["f0_sw3t"] = _blkc(np.asarray(I["L0_ffn_sw3"]).T, 512)
    S["f0_sw2t"] = _blkr(np.asarray(I["L0_ffn_sw2"]).T, 4)
    S["a0_int"] = _pkn(I["L0_attn_W_in"].T)
    S["a0_outt"] = _pkn(I["L0_attn_W_out"].T)
    S["a0_convw"] = _c(np.asarray(I["L0_attn_conv_w"]).reshape(1024, 5)
                       .reshape(8, 128, 5).transpose(1, 0, 2).reshape(128, 40))
    S["a0_convb"] = _pp(I["L0_attn_conv_b"], 8)
    S["a0_a"] = _pp(I["L0_attn_a_param"], 8)
    S["a0_lnw"] = _pp(I["L0_attn_ln_w"], 8)
    S["a0_lnb"] = _pp(I["L0_attn_ln_b"], 8)
    S["a1_wqkv"] = _blkc(np.asarray(I["L1_attn_W_qkv"]), 128)
    S["a1_wout"] = _blkc(np.asarray(I["L1_attn_W_out"]), 128)
    S["a1_qs"] = _pp(I["L1_attn_q_scale"], 8)
    S["a1_ks"] = _pp(I["L1_attn_k_scale"], 8)
    S["f1_routw"] = _pkn(I["L1_ffn_router_w"])
    S["f1_up"] = np.stack([_blkc(np.asarray(I["L1_ffn_up_w"][e]), 512)
                           for e in range(8)])
    S["f1_dn"] = np.stack([_blkr(np.asarray(I["L1_ffn_down_w"][e]), 4)
                           for e in range(8)])
    S["a2_int"] = _blkc(np.asarray(I["L2_attn_w_in"]).T, 128)
    S["a2_outt"] = _blkc(np.asarray(I["L2_attn_w_out"]).T, 128)
    S["f2_lns"] = _pp(I["L2_ffn_ln_scale"], 8)
    S["f2_lnb"] = _pp(I["L2_ffn_ln_bias"], 8)
    S["f2_gatew"] = _pkn(I["L2_ffn_gate_w"].T)
    S["f2_wg"] = np.stack([_blkc(np.asarray(I["L2_ffn_Wg"][e]), 512)
                           for e in range(8)])
    S["f2_wu"] = np.stack([_blkc(np.asarray(I["L2_ffn_Wu"][e]), 512)
                           for e in range(8)])
    S["f2_wd"] = np.stack([_blkr(np.asarray(I["L2_ffn_Wd"][e]), 4)
                           for e in range(8)])
    S["f2_swg"] = _blkc(np.asarray(I["L2_ffn_sWg"]), 512)
    S["f2_swu"] = _blkc(np.asarray(I["L2_ffn_sWu"]), 512)
    S["f2_swd"] = _blkr(np.asarray(I["L2_ffn_sWd"]), 4)
    S["a3_nw"] = _pp(I["L3_attn_norm_weight"], 8)
    S["a3_int"] = _blkc(np.asarray(I["L3_attn_in_proj_w"]).T, 128)
    S["a3_convw"] = _c(np.asarray(I["L3_attn_conv_w"]).reshape(16, 128, 4)
                       .transpose(1, 0, 2).reshape(128, 64))
    S["a3_convb"] = _pp(I["L3_attn_conv_b"], 16)
    S["a3_xpt"] = _pkn(I["L3_attn_x_proj_w"].T)
    S["a3_dtpt"] = _c(I["L3_attn_dt_proj_w"].T)
    S["a3_dtb"] = _pp(I["L3_attn_dt_proj_b"], 16)
    S["a3_alog"] = _c(np.asarray(I["L3_attn_A_log"]).reshape(16, 128, 16)
                      .transpose(1, 0, 2).reshape(128, 256))
    S["a3_d"] = _pp(I["L3_attn_D"], 16)
    S["a3_outt"] = _pkn(I["L3_attn_out_proj_w"].T)
    S["f3_ln1g"] = _pp(I["L3_ffn_ln1_g"], 8)
    S["f3_ln1b"] = _pp(I["L3_ffn_ln1_b"], 8)
    S["f3_w1t"] = _blkc(np.asarray(I["L3_ffn_w1"]).T, 256)
    S["f3_b1"] = _pp(I["L3_ffn_b1"], 32)
    S["f3_ln2g"] = _pp(I["L3_ffn_ln2_g"], 32)
    S["f3_ln2b"] = _pp(I["L3_ffn_ln2_b"], 32)
    S["f3_w2t"] = _blkr(np.asarray(I["L3_ffn_w2"]).T, 2)
    S["f3_b2"] = _pp(I["L3_ffn_b2"], 8)
    return S


CONSTS_BASE = ["ones_col", "ones_row"]
CONSTS_BY_SUB = {
    0: ["sel8", "ident_f32"], 2: ["sel8", "ident_f32"],
    3: ["sel8", "ident_f32"], 5: ["sel8", "ident_f32"],
    6: ["sel8", "ident_f32"], 8: ["sel8", "ident_f32"],
    9: [], 11: [],
    1: [],
    4: [f"cmask{j}" for j in range(4)] + ["ident_f32"],
    7: [f"l2delta{g}" for g in range(8)] + [f"l2mask{g}" for g in range(8)]
       + ["qidx", "ident_f32"],
    10: ["sel16", "ident_f32"],
}
NEED_IDENT_BF = {4, 7, 10}

SUB_WEIGHTS = {
    0: ["f0_gatew", "f0_gateb", "f0_w1t", "f0_w3t", "f0_w2t", "f0_sw1t",
        "f0_sw3t", "f0_sw2t"],
    1: ["a0_int", "a0_outt", "a0_convw", "a0_convb", "a0_a", "a0_lnw", "a0_lnb"],
    3: ["f1_routw", "f1_up", "f1_dn"],
    4: ["a1_wqkv", "a1_wout", "a1_qs", "a1_ks"],
    6: ["f2_lns", "f2_lnb", "f2_gatew", "f2_wg", "f2_wu", "f2_wd", "f2_swg",
        "f2_swu", "f2_swd"],
    7: ["a2_int", "a2_outt"],
    9: ["f3_ln1g", "f3_ln1b", "f3_w1t", "f3_b1", "f3_ln2g", "f3_ln2b",
        "f3_w2t", "f3_b2"],
    10: ["a3_nw", "a3_int", "a3_convw", "a3_convb", "a3_xpt", "a3_dtpt",
         "a3_dtb", "a3_alog", "a3_d", "a3_outt"],
}
SUB_WEIGHTS[2] = SUB_WEIGHTS[0]
SUB_WEIGHTS[5] = SUB_WEIGHTS[3]
SUB_WEIGHTS[8] = SUB_WEIGHTS[6]
SUB_WEIGHTS[11] = SUB_WEIGHTS[9]


def _emit_dispatch(E_, C, es, k, W, xin, xout, extra=None):
    nc = E_.nc
    if k in (3, 5):
        experts = []
        for e in range(8):
            up_e = W["f1_up"][e]
            dn_e = W["f1_dn"][e]
            experts.append({
                "nhb": 8, "act": "gelu_tanh", "gated": e,
                "wg_ap": (lambda hb, a=up_e: a[hb]),
                "wd_ap": (lambda hb, a=dn_e: a[hb]),
            })
        cfg = {"experts": experts,
               "router": {"topk": 4, "norm": True, "norm_eps": 0.0}}
        emit_moe_ffn(E_, C, {"gatew": W["f1_routw"]}, xin, xout, cfg)
    elif k in (0, 2):
        experts = []
        for e in range(8):
            experts.append({
                "nhb": 2, "act": "silu_mul", "gated": e,
                "wg_ap": (lambda hb, a=W["f0_w1t"][e]: a[hb]),
                "wu_ap": (lambda hb, a=W["f0_w3t"][e]: a[hb]),
                "wd_ap": (lambda hb, a=W["f0_w2t"][e]: a[hb]),
            })
        experts.append({
            "nhb": 4, "act": "silu_mul", "gated": None,
            "wg_ap": (lambda hb, a=W["f0_sw1t"]: a[hb]),
            "wu_ap": (lambda hb, a=W["f0_sw3t"]: a[hb]),
            "wd_ap": (lambda hb, a=W["f0_sw2t"]: a[hb]),
        })
        cfg = {"experts": experts,
               "router": {"topk": 2, "norm": False, "bias": True}}
        emit_moe_ffn(E_, C, {"gatew": W["f0_gatew"], "gatebias": W["f0_gateb"]},
                     xin, xout, cfg)
    elif k in (6, 8):
        lns_sb = es.enter_context(
            nc.sbuf_tensor(f"f2_lns_sb_{k}", [128, G], F32))
        lnb_sb = es.enter_context(
            nc.sbuf_tensor(f"f2_lnb_sb_{k}", [128, G], F32))
        with E_.block():
            def f_l(sy):
                ins = sy.dma_start(lns_sb[:], W["f2_lns"])
                E_.i(ins, "qc", 16)
                ins = sy.dma_start(lnb_sb[:], W["f2_lnb"])
                E_.i(ins, "qc", 16)
            E_.sy(f_l)
            E_.setfree("ln_gamma_ld", E_.ev("qc"))
        experts = []
        for e in range(8):
            experts.append({
                "nhb": 4, "act": "silu_mul", "gated": e,
                "wg_ap": (lambda hb, a=W["f2_wg"][e]: a[hb]),
                "wu_ap": (lambda hb, a=W["f2_wu"][e]: a[hb]),
                "wd_ap": (lambda hb, a=W["f2_wd"][e]: a[hb]),
            })
        experts.append({
            "nhb": 4, "act": "silu_mul", "gated": None,
            "wg_ap": (lambda hb, a=W["f2_swg"]: a[hb]),
            "wu_ap": (lambda hb, a=W["f2_swu"]: a[hb]),
            "wd_ap": (lambda hb, a=W["f2_swd"]: a[hb]),
        })
        cfg = {"experts": experts, "ln2_eps": 1e-5,
               "router": {"topk": 2, "norm": True, "norm_eps": 1e-5,
                          "internal_ln": True}}
        emit_moe_ffn(E_, C, {"gatew": W["f2_gatew"], "lns_sb": lns_sb,
                             "lnb_sb": lnb_sb}, xin, xout, cfg)
    elif k in (9, 11):
        qc = (extra or {}).get("f3_qcache")
        emit_ffn3(E_, C, W, xin, xout, qcache=qc, first=(k == 9 or qc is None))
    elif k == 1:
        emit_att0(E_, C, W, xin, xout)
    elif k == 4:
        emit_att1(E_, C, W, xin, xout)
    elif k == 7:
        emit_att2(E_, C, W, xin, xout)
    elif k == 10:
        emit_att3(E_, C, W, xin, xout)
    else:
        raise NotImplementedError(f"sublayer {k}")


def _emit_sub_body(E_, nc, es_outer, k, S, const_dram, xin, xout, W,
                   extra=None):
    ckeys = CONSTS_BASE + CONSTS_BY_SUB[k]
    with contextlib.ExitStack() as es:
        C = {}
        for key in ckeys:
            arr = S[key]
            C[key] = es.enter_context(
                nc.sbuf_tensor(f"c{k}_{key}", list(arr.shape), F32))
        if k in NEED_IDENT_BF:
            C["ident_bf16"] = es.enter_context(
                nc.sbuf_tensor(f"c{k}_ident_bf16", [128, 128], BF16))
        C["ones_col_bf"] = es.enter_context(
            nc.sbuf_tensor(f"c{k}_ones_col_bf", [128, 1], BF16))
        with E_.block():
            fence = [E_.ev(s) for s in ("qs", "qx", "qw", "qv", "qg", "qc")]

            def f_fence_s(sy):
                sy.wait(fence)

            def f_fence_g(g_):
                g_.wait(fence)
            E_.sy(f_fence_s)
            E_.po(f_fence_g)

            def f_cl(sy):
                for key in ckeys:
                    ins = sy.dma_start(C[key][:], const_dram[key])
                    E_.i(ins, "qc", 16)
            E_.sy(f_cl)

            def f_ib(g_):
                if k in NEED_IDENT_BF:
                    ins = g_.dma_start(C["ident_bf16"][:],
                                       const_dram["ident_f32"])
                    E_.i(ins, "qg", 16)
                ins = g_.dma_start(C["ones_col_bf"][:], const_dram["ones_col"])
                E_.i(ins, "qg", 16)
            E_.po(f_ib)
            cl_ev = [E_.ev("qc"), E_.ev("qg")]

            def f_cw(sy):
                sy.wait(cl_ev)
            E_.sy(f_cw)

            def f_cw2(g_):
                g_.wait(cl_ev)
            E_.po(f_cw2)
        _emit_dispatch(E_, C, es, k, W, xin, xout, extra=extra)


def _register_const_floats(nc, es):
    for i, val in enumerate(CONST_FLOATS):
        t_ = es.enter_context(nc.sbuf_tensor(f"c_f{i}", [128, 1], F32))
        nc.const_aps.register(F32, float(val), t_[:])
        yield t_, float(val)


def _build_sub_program(k, S):
    _UID[0] = 0
    nc = bass.Bass("TRN2", target_bir_lowering=False)
    xin = nc.dram_tensor("xin", [8, 128, G, 512], F32, kind="ExternalInput").ap()
    xout = nc.dram_tensor("xout", [8, 128, G, 512], F32,
                          kind="ExternalOutput").ap()
    W = {}
    for key in SUB_WEIGHTS[k]:
        arr = S[key]
        W[key] = nc.dram_tensor(key, list(arr.shape), F32,
                                kind="ExternalInput").ap()
    const_dram = {}
    for key in CONSTS_BASE + CONSTS_BY_SUB[k]:
        arr = S[key]
        const_dram[key] = nc.dram_tensor(key, list(arr.shape), F32,
                                         kind="ExternalInput").ap()
    with contextlib.ExitStack() as es:
        E_ = E(nc, es)
        cf_tiles = list(_register_const_floats(nc, es))
        with E_.block():
            def f_cf(g_):
                for t_, val in cf_tiles:
                    g_.memset(t_[:], val)
            E_.po(f_cf)
        _emit_sub_body(E_, nc, es, k, S, const_dram, xin, xout, W)
        with E_.block():
            def f_fin(sy):
                sy.wait(E_.ev("qs"))
            E_.sy(f_fin)
    return nc


ALL_CONSTS = sorted({k2 for ks in ([CONSTS_BASE] + list(CONSTS_BY_SUB.values()))
                     for k2 in ks})
ALL_WEIGHTS = sorted({k2 for ks in SUB_WEIGHTS.values() for k2 in ks})


def _build_fused(S):
    _UID[0] = 0
    nc = bass.Bass("TRN2", target_bir_lowering=False)
    xin = nc.dram_tensor("xin", [8, 128, G, 512], F32, kind="ExternalInput").ap()
    y = nc.dram_tensor("y", [8, 128, G, 512], F32, kind="ExternalOutput").ap()
    r0 = nc.dram_tensor("r0", [8, 128, G, 512], F32).ap()
    r1 = nc.dram_tensor("r1", [8, 128, G, 512], F32).ap()
    f3q1 = nc.dram_tensor("f3q1", [128, G, 4096], BF16).ap()
    f3q2 = nc.dram_tensor("f3q2", [128, 32, 1024], BF16).ap()
    extra = {"f3_qcache": (f3q1, f3q2)}
    W = {}
    for key in ALL_WEIGHTS:
        arr = S[key]
        W[key] = nc.dram_tensor(key, list(arr.shape), F32,
                                kind="ExternalInput").ap()
    const_dram = {}
    for key in ALL_CONSTS:
        arr = S[key]
        const_dram[key] = nc.dram_tensor(key, list(arr.shape), F32,
                                         kind="ExternalInput").ap()
    with contextlib.ExitStack() as es:
        E_ = E(nc, es)
        cf_tiles = list(_register_const_floats(nc, es))
        with E_.block():
            def f_cf(g_):
                for t_, val in cf_tiles:
                    g_.memset(t_[:], val)
            E_.po(f_cf)
        bufs = [r0, r1]
        for k in range(12):
            src_ap = xin if k == 0 else bufs[(k + 1) % 2]
            dst_ap = y if k == 11 else bufs[k % 2]
            Wk = {key: W[key] for key in SUB_WEIGHTS[k]}
            _emit_sub_body(E_, nc, es, k, S, const_dram, src_ap, dst_ap, Wk,
                           extra=extra)
        with E_.block():
            def f_fin(sy):
                sy.wait(E_.ev("qs"))
            E_.sy(f_fin)
    return nc


def kernel(**inputs):
    S = _prep_shared(inputs)
    x = inputs["L0_attn_x"]
    xs = [_blkx(np.asarray(x[b]).astype(np.float32)) for b in range(NCORES)]
    if FUSED:
        nc = _build_fused(S)
        in_maps = []
        for b in range(NCORES):
            im = {"xin": xs[b]}
            for key in ALL_WEIGHTS:
                im[key] = S[key]
            for key in ALL_CONSTS:
                im[key] = S[key]
            in_maps.append(im)
        res = run_bass_kernel_spmd(nc, in_maps, core_ids=list(range(NCORES)))
        out = np.stack([_unblkx(np.asarray(r["y"])) for r in res.results],
                       axis=0)
        return np.ascontiguousarray(out.astype(np.float32))
    for k in range(12):
        nc = _build_sub_program(k, S)
        in_maps = []
        for b in range(NCORES):
            im = {"xin": xs[b]}
            for key in SUB_WEIGHTS[k]:
                im[key] = S[key]
            for key in CONSTS_BASE + CONSTS_BY_SUB[k]:
                im[key] = S[key]
            in_maps.append(im)
        res = run_bass_kernel_spmd(nc, in_maps, core_ids=list(range(NCORES)))
        xs = [np.asarray(r["xout"]) for r in res.results]
    out = np.stack([_unblkx(xs[b]) for b in range(NCORES)], axis=0)
    return np.ascontiguousarray(out.astype(np.float32))


# version 09
# speedup vs baseline: 1.6677x; 1.0168x over previous
import contextlib
import math
import numpy as np
import ml_dtypes
import concourse.bass as bass
import concourse.mybir as mybir
from concourse.bass_utils import run_bass_kernel_spmd

F32 = mybir.dt.float32
BF16 = mybir.dt.bfloat16
I32 = mybir.dt.int32
AF = mybir.ActivationFunctionType
ALU = mybir.AluOpType
AX = mybir.AxisListType

D = 1024
T = 4096
G = 8
NCORES = 8
LN_EPS = 1e-5

GELU_C2 = 2.0 * math.sqrt(2.0 / math.pi)
GELU_C1 = GELU_C2 * 0.044715

FUSED = True

_UID = [0]


def _mk_alloc(nc, es):
    uid = _UID[0]
    _UID[0] += 1
    sb = lambda n, s, d: es.enter_context(nc.sbuf_tensor(f"u{uid}_{n}", s, d))
    ps = lambda n, s, d=None: es.enter_context(
        nc.psum_tensor(f"u{uid}_{n}", s, F32 if d is None else d))
    return uid, sb, ps

CONST_FLOATS = [1e-5, 1e-6, -0.5, -1.0, -40.0, 0.5, 2.0, 0.125] + \
    [-512.0 * 2.0 ** (-(h + 1) * 0.5) for h in range(16)]



AUTO_OPS = {
    "tensor_tensor", "tensor_scalar", "tensor_copy", "tensor_add",
    "tensor_sub", "tensor_mul", "tensor_max", "scalar_tensor_tensor",
    "tensor_tensor_scan", "tensor_reduce", "reduce_max", "reduce_sum",
    "reciprocal", "memset", "memzero", "select", "copy_predicated",
    "tensor_single_scalar", "tensor_scalar_add", "tensor_scalar_sub",
    "tensor_scalar_mul", "tensor_scalar_max", "tensor_scalar_min",
    "tensor_relu", "engine_nop", "activation", "copy", "mul", "add",
    "square", "sqrt", "sign", "transpose", "max", "bn_stats", "bn_aggr",
    "iota",
}
AUTO_SEM = {"vector": "dv", "scalar": "ac", "gpsimd": "po"}


class Eng:
    def __init__(self, E, name, h):
        self.E = E
        self.name = name
        self.h = h
        self.auto = AUTO_SEM.get(name)

    def wait(self, ev):
        if ev is None:
            return
        if isinstance(ev, list):
            for e in ev:
                self.wait(e)
            return
        sem, val = ev
        if val <= 0:
            return
        key = (self.name, sem)
        if self.E.lastw.get(key, -1) >= val:
            return
        self.E.lastw[key] = val
        self.h.wait_ge(self.E.sems[sem], val)

    def waitb(self, name):
        self.wait(self.E.free.get(name))

    def __getattr__(self, a):
        f = getattr(self.h, a)
        if self.auto is None or a not in AUTO_OPS or not callable(f):
            return f
        E_, sem = self.E, self.auto

        def wrapped(*args, **kw):
            self.wait((sem, E_.cnt[sem]))
            ins = f(*args, **kw)
            E_.cnt[sem] += 1
            ins.then_inc(E_.sems[sem], 1)
            return ins
        return wrapped


class E:
    SEMS = ("qw", "qv", "qg", "qc", "qx", "qs", "pe", "dv", "ac", "po")

    def __init__(self, nc, stack):
        self.nc = nc
        self.sems = {}
        self.cnt = {}
        self.lastw = {}
        self.free = {}
        for s in self.SEMS:
            self.sems[s] = stack.enter_context(nc.semaphore("sem_" + s))
            self.cnt[s] = 0
        self.blk = None

    def i(self, instr, sem, n=1):
        if sem in ("dv", "ac", "po") and n == 1:
            return (sem, self.cnt[sem])
        self.cnt[sem] += n
        instr.then_inc(self.sems[sem], n)
        return (sem, self.cnt[sem])

    def ev(self, sem):
        return (sem, self.cnt[sem])

    def setfree(self, name, ev):
        self.free[name] = ev

    def _sec(self, engname, fn):
        def wrap(h):
            fn(Eng(self, engname, h))
        getattr(self.blk, engname)(wrap)

    def pe(self, fn):
        self._sec("tensor", fn)

    def dv(self, fn):
        self._sec("vector", fn)

    def ac(self, fn):
        self._sec("scalar", fn)

    def po(self, fn):
        self._sec("gpsimd", fn)

    def sy(self, fn):
        self._sec("sync", fn)

    @contextlib.contextmanager
    def block(self):
        with self.nc.Block() as blk:
            self.blk = blk
            yield blk
            self.blk = None



def emit_ln_tile(E_, C, st, Xf, ld_ev, Tt, eps, out_bf=None, out_f32=None,
                 out_off=0, gg=G, gamma=None, beta=None, out_free_key=None,
                 kp="ln", ones=None):
    S1, S2, MB, RB = st["S1"], st["S2"], st["MB"], st["RB"]
    SQ, M_, V_, R_, T1 = st["SQ"], st["M"], st["V"], st["R"], st["T1"]
    dinv = 1.0 / (gg * 128)
    if ones is None:
        ones = C["ones_col"][:]

    def f_pe1(pe):
        pe.wait(ld_ev)
        pe.waitb(kp + "_s1")
        for g in range(gg):
            mi = pe.matmul(S1, ones, Xf[:, g, :],
                           start=(g == 0), stop=(g == gg - 1))
        E_.i(mi, "pe")
    E_.pe(f_pe1)
    s1_ev = E_.ev("pe")

    for g in range(gg):
        par = g % 2

        def f_ac1(ac, g=g, par=par):
            ac.wait(ld_ev)
            ac.waitb(kp + f"_sq{par}")
            ins = ac.activation(SQ[:, par, :], Xf[:, g, :], AF.Square)
            E_.i(ins, "ac")
        E_.ac(f_ac1)
        sqg_ev = E_.ev("ac")

        def f_pe2(pe, g=g, par=par, sqg_ev=sqg_ev):
            pe.wait(sqg_ev)
            if g == 0:
                pe.waitb(kp + "_s2")
            mi = pe.matmul(S2, C["ones_col"][:], SQ[:, par, :],
                           start=(g == 0), stop=(g == gg - 1))
            ev = E_.i(mi, "pe")
            E_.setfree(kp + f"_sq{par}", ev)
        E_.pe(f_pe2)
    s2_ev = E_.ev("pe")

    def f_dv1(dv):
        dv.wait(s1_ev)
        dv.waitb(kp + "_mvr")
        dv.tensor_scalar(M_[:], S1, dinv, None, ALU.mult)
        dv.tensor_tensor(T1[0:1, 0:Tt], M_[:], M_[:], ALU.mult)
        dv.wait(s2_ev)
        ins = dv.scalar_tensor_tensor(V_[:], S2, dinv, T1[0:1, 0:Tt],
                                      ALU.mult, ALU.subtract)
        E_.i(ins, "dv")
    E_.dv(f_dv1)
    v_ev = E_.ev("dv")
    E_.setfree(kp + "_s1", v_ev)
    E_.setfree(kp + "_s2", v_ev)

    def f_ac2(ac):
        ac.wait(v_ev)
        ac.activation(R_[:], V_[:], AF.Ln, bias=float(eps))
        ins = ac.activation(R_[:], R_[:], AF.Exp, scale=-0.5)
        E_.i(ins, "ac")
    E_.ac(f_ac2)
    r_ev = E_.ev("ac")

    def f_pe3(pe):
        pe.wait(r_ev)
        pe.waitb(kp + "_mbrb")
        pe.matmul(MB, C["ones_row"][:], M_[:], start=True, stop=True)
        mi = pe.matmul(RB, C["ones_row"][:], R_[:], start=True, stop=True)
        E_.i(mi, "pe")
    E_.pe(f_pe3)
    bc_ev = E_.ev("pe")
    E_.setfree(kp + "_mvr", bc_ev)

    def f_dv2(dv):
        dv.wait(bc_ev)
        if out_free_key:
            dv.waitb(out_free_key)
        if gamma is not None:
            dv.waitb("ln_gamma_ld")
        for g in range(gg):
            dv.tensor_tensor(T1[:], Xf[:, g, :], MB, ALU.subtract)
            if gamma is not None:
                dv.tensor_tensor(T1[:], T1[:], RB, ALU.mult)
                if out_f32 is not None:
                    ins = dv.tensor_scalar(out_f32[:, g, :], T1[:],
                                           gamma[:, g:g + 1], beta[:, g:g + 1],
                                           ALU.mult, ALU.add)
                    if out_bf is not None:
                        ins = dv.tensor_copy(
                            out_bf[:, g, out_off:out_off + Tt], out_f32[:, g, :])
                else:
                    ins = dv.tensor_scalar(out_bf[:, g, out_off:out_off + Tt],
                                           T1[:], gamma[:, g:g + 1],
                                           beta[:, g:g + 1], ALU.mult, ALU.add)
            elif out_f32 is not None:
                ins = dv.tensor_tensor(out_f32[:, g, :], T1[:], RB, ALU.mult)
                if out_bf is not None:
                    ins = dv.tensor_copy(
                        out_bf[:, g, out_off:out_off + Tt], out_f32[:, g, :])
            else:
                ins = dv.tensor_tensor(out_bf[:, g, out_off:out_off + Tt],
                                       T1[:], RB, ALU.mult)
        E_.i(ins, "dv")
    E_.dv(f_dv2)
    out_ev = E_.ev("dv")
    E_.setfree(kp + "_mbrb", out_ev)
    return out_ev


def alloc_ln_scratch(es, nc, pfx, Tt, ps_alias, sq=None):
    sb = lambda n, s, d: es.enter_context(nc.sbuf_tensor(f"u{_UID[0]}x_{pfx}{n}", s, d))
    a, b, c_, d_ = ps_alias
    st = {
        "SQ": sq if sq is not None else sb("SQ", [128, 2, Tt], F32),
        "M": sb("M", [1, Tt], F32),
        "V": (lambda v: v)(sb("V", [1, Tt], F32)),
        "T1": sb("T1", [128, Tt], F32),
        "S1": a[0:1, 0:Tt],
        "S2": b[0:1, 0:Tt],
        "MB": c_[:, 0:Tt],
        "RB": d_[:, 0:Tt],
    }
    st["R"] = st["V"]
    return st



def emit_gate_softmax_topk(E_, C, st, lg_ev, topk, norm, norm_eps,
                           bias_bc=None):
    PSL, PRB, TMP8, SEL, M1 = st["PSL"], st["PRB"], st["TMP8"], st["SEL"], st["M1"]
    BAS = st["BAS"]

    def f_nm(dv):
        dv.wait(lg_ev)
        dv.waitb("gt_m1")
        ins = dv.reduce_max(M1[:], PSL, axis=AX.X, negate=True)
        E_.i(ins, "dv")
    E_.dv(f_nm)
    nm_ev = E_.ev("dv")

    def f_exp(ac):
        ac.wait(nm_ev)
        ac.waitb("gt_prb")
        ins = ac.activation(PRB[:], PSL, AF.Exp, bias=M1[:])
        E_.i(ins, "ac")
    E_.ac(f_exp)
    ex_ev = E_.ev("ac")
    E_.setfree("gt_psl", ex_ev)

    def f_top(dv):
        dv.wait(ex_ev)
        dv.waitb("gt_sel")
        dv.reduce_sum(M1[:], PRB[:], axis=AX.X)
        dv.reciprocal(M1[:], M1[:])
        dv.tensor_scalar(PRB[:], PRB[:], M1[:, 0:1], None, ALU.mult)
        if bias_bc is not None:
            dv.tensor_tensor(BAS[:], PRB[:], bias_bc, ALU.add)
            basis = BAS
        else:
            basis = PRB
        dv.tensor_copy(TMP8[:], basis[:])
        for r in range(topk):
            dv.reduce_max(M1[:], TMP8[:], axis=AX.X)
            if r < topk - 1:
                dv.tensor_scalar(SEL[:], TMP8[:], M1[:, 0:1], None, ALU.is_equal)
                dv.scalar_tensor_tensor(TMP8[:], SEL[:], -1e30, TMP8[:],
                                        ALU.mult, ALU.add)
        dv.tensor_scalar(SEL[:], basis[:], M1[:, 0:1], None, ALU.is_ge)
        ins = dv.tensor_tensor(SEL[:], SEL[:], PRB[:], ALU.mult)
        if norm:
            dv.reduce_sum(M1[:], SEL[:], axis=AX.X)
            if norm_eps:
                dv.tensor_scalar(M1[:], M1[:], float(norm_eps), None, ALU.add)
            dv.reciprocal(M1[:], M1[:])
            ins = dv.tensor_scalar(SEL[:], SEL[:], M1[:, 0:1], None, ALU.mult)
        E_.i(ins, "dv")
    E_.dv(f_top)
    sel_ev = E_.ev("dv")
    E_.setfree("gt_m1", sel_ev)
    E_.setfree("gt_prb", sel_ev)
    return sel_ev


def emit_gating_phase(E_, C, st, XN, GA, gatew_sb, n_tok, topk, norm, norm_eps,
                      xn_ev, bias_bc=None, xn_off=0):
    for s in range(n_tok // 128):
        c0 = xn_off + s * 128

        def f_lg(pe, c0=c0):
            pe.wait(xn_ev)
            pe.waitb("gt_psl")
            pe.waitb("gatew_ld")
            for g in range(G):
                mi = pe.matmul(st["PSL"], XN[:, g, c0:c0 + 128],
                               gatew_sb[:, g, :], start=(g == 0),
                               stop=(g == G - 1))
            E_.i(mi, "pe")
        E_.pe(f_lg)
        lg_ev = E_.ev("pe")
        sel_ev = emit_gate_softmax_topk(E_, C, st, lg_ev, topk, norm, norm_eps,
                                        bias_bc=bias_bc)

        def f_tr(pe, sel_ev=sel_ev):
            pe.wait(sel_ev)
            pe.waitb("gt_gt")
            ins = pe.transpose(st["PSGT"], st["SEL"][:], C["ident_f32"][:])
            E_.i(ins, "pe")
        E_.pe(f_tr)
        tr_ev = E_.ev("pe")
        E_.setfree("gt_sel", tr_ev)

        def f_cp(dv, tr_ev=tr_ev, s=s):
            dv.wait(tr_ev)
            ins = dv.tensor_copy(GA[:, s * 128:(s + 1) * 128], st["PSGT"])
            E_.i(ins, "dv")
        E_.dv(f_cp)
        E_.setfree("gt_gt", E_.ev("dv"))
    return E_.ev("dv")



def emit_moe_ffn(E_, C, W, xin, xout, cfg):
    nc = E_.nc
    HALF = 2048
    r = cfg["router"]
    internal_ln = bool(r.get("internal_ln"))
    has_wu = any(e.get("wu_ap") for e in cfg["experts"])
    HBW = 512

    with contextlib.ExitStack() as es:
        uid, sb, ps_ = _mk_alloc(nc, es)
        XN = sb("mf_XN", [128, G, HALF], BF16)
        OA = sb("mf_OA", [128, G, HALF], F32)
        XF = sb("mf_XF", [128, G, 512], F32)
        GATEW = sb("mf_gw", [128, G, 8], BF16)
        GA = sb("mf_ga", [8, HALF], F32)
        GB4 = XF[:, 0:4, :]
        WB1 = [sb(f"mf_w1{i}", [128, G, HBW], BF16) for i in range(2)]
        WB3 = ([sb(f"mf_w3{i}", [128, G, HBW], BF16) for i in range(2)]
               if has_wu else [None, None])
        WB2 = [sb(f"mf_w2{i}", [128, HBW // 128, 1024], BF16) for i in range(2)]
        ug = 8 if internal_ln else HBW // 128
        U8 = sb("mf_u8", [128, ug, 512], BF16)
        HT = U8[:, 0:HBW // 128, :]
        SCR = sb("mf_scr", [128, 8, 512], F32)
        SC1 = [SCR[:, 0, :], SCR[:, 1, :]]
        SC2 = [SCR[:, 2, :], SCR[:, 3, :]]
        SC3 = [SCR[:, 4, :], SCR[:, 5, :]]
        PH = [ps_(f"mf_ph{i}", [128, 512])
              for i in range(2)]
        PH3 = [ps_(f"mf_ph3{i}", [128, 512])
               for i in range(2)]
        PD = [ps_(f"mf_pd{i}", [128, 512])
              for i in range(2)]
        PMB = ps_("mf_pmb", [128, 512])
        PRBC = ps_("mf_prb", [128, 512])
        st = alloc_ln_scratch(es, nc, "mf_ln", 512, (PD[0], PD[1], PMB, PRBC),
                              sq=SCR[:, 0:2, :])
        st["PSL"] = PH[0][:, 0:8]
        st["PSGT"] = PH[1][0:8, 0:128]
        st["PRB"] = sb("mf_gprb", [128, 8], F32)
        st["TMP8"] = sb("mf_gtmp", [128, 8], F32)
        st["SEL"] = sb("mf_gsel", [128, 8], F32)
        st["BAS"] = (sb("mf_gbas", [128, 8], F32) if r.get("bias") else None)
        st["M1"] = sb("mf_gm1", [128, 1], F32)
        BIASBC = sb("mf_gbias", [128, 8], F32) if r.get("bias") else None
        XN2 = U8[:, 0:8, :] if internal_ln else None

        with E_.block():
            def f_gw(g_):
                ins = g_.dma_start(GATEW[:], W["gatew"])
                E_.i(ins, "qg", 16)
            E_.po(f_gw)
            E_.setfree("gatew_ld", E_.ev("qg"))
            if BIASBC is not None:
                BROW = sb("mf_brow", [1, 8], F32)

                def f_bl(sy):
                    ins = sy.dma_start(BROW[:], W["gatebias"])
                    E_.i(ins, "qc", 16)
                E_.sy(f_bl)
                bl_ev = E_.ev("qc")

                def f_bb(pe):
                    pe.wait(bl_ev)
                    ins = pe.matmul(PH[0][:, 0:8], C["ones_row"][:], BROW[:],
                                    start=True, stop=True)
                    E_.i(ins, "pe")
                E_.pe(f_bb)
                bb_ev = E_.ev("pe")

                def f_bc(dv):
                    dv.wait(bb_ev)
                    ins = dv.tensor_copy(BIASBC[:], PH[0][:, 0:8])
                    E_.i(ins, "dv")
                E_.dv(f_bc)
                E_.setfree("gt_psl", E_.ev("dv"))

        for hf in range(2):
            hof = hf * HALF
            with E_.block():
                for tt in range(4):
                    gtile = hf * 4 + tt

                    def f_lx(sy, gtile=gtile):
                        sy.waitb("mf_xf")
                        ins = sy.dma_start(XF[:], xin[gtile])
                        E_.i(ins, "qx", 16)
                    E_.sy(f_lx)
                    ld_ev = E_.ev("qx")
                    xn_ev = emit_ln_tile(
                        E_, C, st, XF, ld_ev, 512, LN_EPS, out_bf=XN,
                        out_off=tt * 512)
                    E_.setfree("mf_xf", xn_ev)
                    if internal_ln:
                        xn2_ev = emit_ln_tile(
                            E_, C, st, XN[:, :, tt * 512:(tt + 1) * 512],
                            xn_ev, 512, cfg["ln2_eps"], out_bf=XN2,
                            gamma=W["lns_sb"], beta=W["lnb_sb"],
                            out_free_key="mf_xn2", ones=C["ones_col_bf"][:])
                        gate_src, gate_ev, gate_off = XN2, xn2_ev, 0
                    else:
                        gate_src, gate_ev, gate_off = XN, xn_ev, tt * 512
                    emit_gating_phase(
                        E_, C, st, gate_src, GA[:, tt * 512:(tt + 1) * 512],
                        GATEW, 512, r["topk"], r["norm"],
                        r.get("norm_eps", 0.0), gate_ev,
                        bias_bc=(BIASBC[:] if BIASBC is not None else None),
                        xn_off=gate_off)
                    if internal_ln:
                        E_.setfree("mf_xn2", E_.ev("pe"))
                ga_ev = E_.ev("dv")

                def f_ms(dv):
                    ins = dv.memset(OA[:], 0.0)
                    E_.i(ins, "dv")
                E_.dv(f_ms)

            with E_.block():
                wrot = 0
                for ex in cfg["experts"]:
                    NHC = HBW // 128
                    gated = ex.get("gated")
                    if gated is not None:
                        for tt in range(4):
                            def f_gb(pe, gated=gated, tt=tt):
                                pe.wait(ga_ev)
                                pe.waitb("mf_pgb")
                                mi = pe.matmul(
                                    PMB[:],
                                    C["sel8"][0:8, gated * 128:(gated + 1) * 128],
                                    GA[:, tt * 512:(tt + 1) * 512],
                                    start=True, stop=True)
                                E_.i(mi, "pe")
                            E_.pe(f_gb)
                            gb_ev = E_.ev("pe")

                            def f_gc(dv, gb_ev=gb_ev, tt=tt):
                                dv.wait(gb_ev)
                                ins = dv.tensor_copy(GB4[:, tt, :], PMB[:])
                                E_.i(ins, "dv")
                            E_.dv(f_gc)
                            E_.setfree("mf_pgb", E_.ev("dv"))
                    for hb in range(ex["nhb"]):
                        slot = wrot % 2
                        w1b, w3b, w2b = WB1[slot], WB3[slot], WB2[slot]
                        wfree = E_.free.get(f"mf_wb{slot}")
                        wrot += 1

                        wsem = "qw" if slot == 0 else "qv"

                        def f_lw(g_, ex=ex, hb=hb, w1b=w1b, w3b=w3b, w2b=w2b,
                                 wfree=wfree, NHC=NHC, wsem=wsem):
                            g_.wait(wfree)
                            ins = g_.dma_start(w1b[:], ex["wg_ap"](hb))
                            E_.i(ins, wsem, 16)
                            if ex.get("wu_ap"):
                                ins = g_.dma_start(w3b[:], ex["wu_ap"](hb))
                                E_.i(ins, wsem, 16)
                            ins = g_.dma_start(w2b[:], ex["wd_ap"](hb))
                            E_.i(ins, wsem, 16)
                        E_.po(f_lw)
                        w_ev = E_.ev(wsem)

                        for tt in range(4):
                            ts0 = tt * 512
                            gb1_ev = None
                            GB1 = GB4[:, tt, :]

                            if ex["act"] == "gelu_tanh":
                                PHX = [PH[0], PH[1], PH3[0], PH3[1]]

                                def f_up4(pe, w1b=w1b, ts0=ts0, w_ev=w_ev,
                                          PHX=PHX):
                                    pe.wait(w_ev)
                                    for hc in range(4):
                                        pe.waitb(f"mf_phx{hc}")
                                        for k in range(G):
                                            mi = pe.matmul(
                                                PHX[hc][:],
                                                w1b[:, k, hc * 128:(hc + 1) * 128],
                                                XN[:, k, ts0:ts0 + 512],
                                                start=(k == 0),
                                                stop=(k == G - 1))
                                        E_.i(mi, "pe")
                                E_.pe(f_up4)
                                up_evs = [("pe", E_.cnt["pe"] - 3 + j)
                                          for j in range(4)]

                                def f_gl4(ac, up_evs=up_evs, PHX=PHX):
                                    for hc in range(4):
                                        ac.wait(up_evs[hc])
                                        ac.waitb(f"mf_scx{hc}")
                                        ins = ac.activation(SCR[:, hc, :],
                                                            PHX[hc][:],
                                                            AF.Gelu_apprx_tanh)
                                        ev = E_.i(ins, "ac")
                                        E_.setfree(f"mf_phx{hc}", ev)
                                E_.ac(f_gl4)
                                gl_evs = [("ac", E_.cnt["ac"] - 3 + j)
                                          for j in range(4)]

                                def f_m4(dv, gl_evs=gl_evs, gb1_ev=gb1_ev):
                                    dv.wait(gb1_ev)
                                    dv.waitb("mf_ht")
                                    for hc in range(4):
                                        dv.wait(gl_evs[hc])
                                        ins = dv.tensor_tensor(
                                            HT[:, hc, :], SCR[:, hc, :],
                                            GB1, ALU.mult)
                                        ev = E_.i(ins, "dv")
                                        E_.setfree(f"mf_scx{hc}", ev)
                                E_.dv(f_m4)
                            else:
                              for hc in range(NHC):
                                par = (hc + tt * NHC) % 2
                                ph, ph3 = PH[par], PH3[par]

                                def f_up(pe, w1b=w1b, w3b=w3b, ph=ph, ph3=ph3,
                                         hc=hc, ts0=ts0, w_ev=w_ev, par=par,
                                         ex=ex):
                                    pe.wait(w_ev)
                                    pe.waitb(f"mf_ph{par}")
                                    for k in range(G):
                                        mi = pe.matmul(
                                            ph[:],
                                            w1b[:, k, hc * 128:(hc + 1) * 128],
                                            XN[:, k, ts0:ts0 + 512],
                                            start=(k == 0), stop=(k == G - 1))
                                    E_.i(mi, "pe")
                                    if ex.get("wu_ap"):
                                        pe.waitb(f"mf_ph3{par}")
                                        for k in range(G):
                                            mi = pe.matmul(
                                                ph3[:],
                                                w3b[:, k, hc * 128:(hc + 1) * 128],
                                                XN[:, k, ts0:ts0 + 512],
                                                start=(k == 0), stop=(k == G - 1))
                                        E_.i(mi, "pe")
                                E_.pe(f_up)
                                up_ev = E_.ev("pe")

                                if ex["act"] == "silu_mul":
                                    def f_sl(ac, par=par, ph=ph, up_ev=up_ev):
                                        ac.wait(up_ev)
                                        ac.waitb(f"mf_sc1{par}")
                                        ins = ac.activation(SC1[par], ph[:],
                                                            AF.Silu)
                                        E_.i(ins, "ac")
                                    E_.ac(f_sl)
                                    sl_ev = E_.ev("ac")
                                    E_.setfree(f"mf_ph{par}", sl_ev)

                                    def f_m1(dv, par=par, ph3=ph3, hc=hc,
                                             sl_ev=sl_ev, gb1_ev=gb1_ev,
                                             gated=gated):
                                        dv.wait(sl_ev)
                                        dv.waitb("mf_ht")
                                        if gated is not None:
                                            dv.wait(gb1_ev)
                                            dv.tensor_tensor(
                                                SC2[par], SC1[par], ph3[:],
                                                ALU.mult)
                                            ins = dv.tensor_tensor(
                                                HT[:, hc, :], SC2[par],
                                                GB1, ALU.mult)
                                        else:
                                            ins = dv.tensor_tensor(
                                                HT[:, hc, :], SC1[par], ph3[:],
                                                ALU.mult)
                                        ev = E_.i(ins, "dv")
                                        E_.setfree(f"mf_ph3{par}", ev)
                                        E_.setfree(f"mf_sc1{par}", ev)
                                    E_.dv(f_m1)
                                else:
                                    def f_sq(ac, par=par, ph=ph, up_ev=up_ev):
                                        ac.wait(up_ev)
                                        ac.waitb(f"mf_sc1{par}")
                                        ins = ac.activation(SC1[par], ph[:],
                                                            AF.Square)
                                        E_.i(ins, "ac")
                                    E_.ac(f_sq)
                                    sq_ev = E_.ev("ac")

                                    def f_a1(dv, par=par, ph=ph, sq_ev=sq_ev):
                                        dv.wait(sq_ev)
                                        dv.tensor_scalar(SC2[par], SC1[par],
                                                         GELU_C1, GELU_C2,
                                                         ALU.mult, ALU.add)
                                        ins = dv.tensor_tensor(SC1[par],
                                                               SC2[par], ph[:],
                                                               ALU.mult)
                                        E_.i(ins, "dv")
                                    E_.dv(f_a1)
                                    a1_ev = E_.ev("dv")

                                    def f_sg(ac, par=par, a1_ev=a1_ev):
                                        ac.wait(a1_ev)
                                        ins = ac.activation(SC3[par],
                                                            SC1[par],
                                                            AF.Sigmoid)
                                        E_.i(ins, "ac")
                                    E_.ac(f_sg)
                                    sg_ev = E_.ev("ac")

                                    def f_m2(dv, par=par, ph=ph, hc=hc,
                                             sg_ev=sg_ev, gb1_ev=gb1_ev):
                                        dv.wait(sg_ev)
                                        dv.wait(gb1_ev)
                                        ins = dv.tensor_tensor(
                                            SC2[par], SC3[par], ph[:],
                                            ALU.mult)
                                        ev = E_.i(ins, "dv")
                                        E_.setfree(f"mf_ph{par}", ev)
                                        dv.waitb("mf_ht")
                                        ins = dv.tensor_tensor(
                                            HT[:, hc, :], SC2[par],
                                            GB1, ALU.mult)
                                        ev = E_.i(ins, "dv")
                                        E_.setfree(f"mf_sc1{par}", ev)
                                    E_.dv(f_m2)
                            ht_ev = E_.ev("dv")

                            for dg in range(G):
                                pdb = PD[dg % 2]

                                def f_dn(pe, w2b=w2b, pdb=pdb, dg=dg, NHC=NHC,
                                         ht_ev=ht_ev):
                                    pe.wait(ht_ev)
                                    pe.waitb(f"mf_pd{dg % 2}")
                                    for k in range(NHC):
                                        mi = pe.matmul(
                                            pdb[:],
                                            w2b[:, k, dg * 128:(dg + 1) * 128],
                                            HT[:, k, :], start=(k == 0),
                                            stop=(k == NHC - 1))
                                    E_.i(mi, "pe")
                                E_.pe(f_dn)
                                dn_ev = E_.ev("pe")

                                def f_ac_(dv, pdb=pdb, dg=dg, ts0=ts0,
                                          dn_ev=dn_ev):
                                    dv.wait(dn_ev)
                                    ins = dv.tensor_tensor(
                                        OA[:, dg, ts0:ts0 + 512],
                                        OA[:, dg, ts0:ts0 + 512], pdb[:],
                                        ALU.add)
                                    ev = E_.i(ins, "dv")
                                    E_.setfree(f"mf_pd{dg % 2}", ev)
                                E_.dv(f_ac_)
                            E_.setfree("mf_ht", E_.ev("pe"))
                        E_.setfree(f"mf_wb{slot}", E_.ev("pe"))
            E_.setfree("mf_xf", E_.ev("dv"))

            with E_.block():
                for tt in range(4):
                    gtile = hf * 4 + tt

                    def f_lx2(sy, gtile=gtile):
                        sy.waitb("mf_xf")
                        ins = sy.dma_start(XF[:], xin[gtile])
                        E_.i(ins, "qx", 16)
                    E_.sy(f_lx2)
                    ld2_ev = E_.ev("qx")

                    def f_res(dv, tt=tt, ld2_ev=ld2_ev):
                        dv.wait(ld2_ev)
                        for g in range(G):
                            ins = dv.scalar_tensor_tensor(
                                XF[:, g, :], OA[:, g, tt * 512:(tt + 1) * 512],
                                0.5, XF[:, g, :], ALU.mult, ALU.add)
                        E_.i(ins, "dv")
                    E_.dv(f_res)
                    xo_ev = E_.ev("dv")

                    def f_st(sy, gtile=gtile, xo_ev=xo_ev):
                        sy.wait(xo_ev)
                        ins = sy.dma_start(xout[gtile], XF[:])
                        E_.i(ins, "qs", 16)
                    E_.sy(f_st)
                    E_.setfree("mf_xf", E_.ev("qs"))



def emit_ffn3(E_, C, W, xin, xout, qcache=None, first=True):
    nc = E_.nc
    H32 = 32
    w1r = W["f3_w1t"]
    w2r = W["f3_w2t"]

    with contextlib.ExitStack() as es:
        uid, sb, ps_ = _mk_alloc(nc, es)
        WQ1 = sb("f3_wq1", [128, G, 4096], BF16)
        WQ2 = sb("f3_wq2", [128, H32, 1024], BF16)
        QT2 = sb("f3_qt2", [128, 512], F32)
        XF = sb("f3_xf", [128, G, 512], F32)
        SCR = sb("f3_scr", [128, 4, 512], F32)
        WQF = SCR[:].rearrange("p a b -> p (a b)")
        QT = sb("f3_qt", [128, 512], F32)
        XN1 = sb("f3_xn1", [128, G, 512], BF16)
        HSB = sb("f3_h", [128, H32, 512], BF16)
        XA = HSB[:, 0:G, :]
        B1 = sb("f3_b1s", [128, H32], F32)
        L2Gs = sb("f3_l2g", [128, H32], F32)
        L2Bs = sb("f3_l2b", [128, H32], F32)
        L1Gs = sb("f3_l1g", [128, G], F32)
        L1Bs = sb("f3_l1b", [128, G], F32)
        B2 = sb("f3_b2s", [128, G], F32)
        MACC = sb("f3_macc", [128, 1], F32)
        MSC = [sb(f"f3_msc{i}", [128, 1], F32) for i in range(2)]
        PH = [ps_(f"f3_ph{i}", [128, 512])
              for i in range(2)]
        PD = [ps_(f"f3_pd{i}", [128, 512])
              for i in range(2)]
        PMB = ps_("f3_pmb", [128, 512])
        PRBC = ps_("f3_prb", [128, 512])
        st = alloc_ln_scratch(es, nc, "f3_ln", 512, (PD[0], PD[1], PMB, PRBC),
                              sq=SCR[:, 0:2, :])

        with E_.block():
            def f_lp(sy):
                for dst, key, n16 in ((B1, "f3_b1", 16), (L2Gs, "f3_ln2g", 16),
                                      (L2Bs, "f3_ln2b", 16), (L1Gs, "f3_ln1g", 16),
                                      (L1Bs, "f3_ln1b", 16), (B2, "f3_b2", 16)):
                    ins = sy.dma_start(dst[:], W[key])
                    E_.i(ins, "qc", 16)
            E_.sy(f_lp)
            E_.setfree("f3_params", E_.ev("qc"))

        if not first:
            with E_.block():
                def f_lq(g_):
                    ins = g_.dma_start(WQ1[:], qcache[0])
                    E_.i(ins, "qg", 16)
                    ins = g_.dma_start(WQ2[:], qcache[1])
                    E_.i(ins, "qg", 16)
                E_.po(f_lq)
            wq_ev = E_.ev("qg")
        else:
          for wi, (wr, WQ, nch, nel) in enumerate((
                (w1r, WQ1, 16, 1024 * 4096), (w2r, WQ2, 16, 4096 * 1024))):
            with E_.block():
                def f_z(dv):
                    ins = dv.memset(MACC[:], 0.0)
                    E_.i(ins, "dv")
                E_.dv(f_z)
                for c in range(nch):
                    src_ap = wr[c]

                    def f_lq(sy, src_ap=src_ap, wi=wi):
                        sy.waitb("f3_wqf")
                        if wi == 0:
                            ins = sy.dma_start(
                                WQF[:].rearrange("p (a b) -> p a b", a=8), src_ap)
                        else:
                            ins = sy.dma_start(
                                WQF[:].rearrange("p (a b) -> p a b", a=2), src_ap)
                        E_.i(ins, "qx", 16)
                    E_.sy(f_lq)
                    lq_ev = E_.ev("qx")

                    def f_ab(dv, lq_ev=lq_ev):
                        dv.wait(lq_ev)
                        dv.reduce_sum(QT[:, 0:1], WQF[:], axis=AX.X,
                                      apply_absolute_value=True)
                        ins = dv.tensor_tensor(MACC[:], MACC[:], QT[:, 0:1],
                                               ALU.add)
                        E_.i(ins, "dv")
                    E_.dv(f_ab)
                    E_.setfree("f3_wqf", E_.ev("dv"))
                macc_ev = E_.ev("dv")

                def f_tot(pe, macc_ev=macc_ev):
                    pe.wait(macc_ev)
                    ins = pe.matmul(PH[0][0:1, 0:1], C["ones_col"][:], MACC[:],
                                    start=True, stop=True)
                    E_.i(ins, "pe")
                E_.pe(f_tot)
                tot_ev = E_.ev("pe")

                def f_m(dv, tot_ev=tot_ev, nel=nel):
                    dv.wait(tot_ev)
                    dv.tensor_scalar(QT[0:1, 0:1], PH[0][0:1, 0:1],
                                     1.0 / nel, 1e-5, ALU.mult, ALU.max)
                    ins = dv.reciprocal(QT[0:1, 1:2], QT[0:1, 0:1])
                    E_.i(ins, "dv")
                E_.dv(f_m)
                m_ev = E_.ev("dv")

                def f_mb(pe, m_ev=m_ev):
                    pe.wait(m_ev)
                    pe.matmul(PH[0][:, 0:1], C["ones_row"][:], QT[0:1, 0:1],
                              start=True, stop=True)
                    ins = pe.matmul(PH[0][:, 1:2], C["ones_row"][:],
                                    QT[0:1, 1:2], start=True, stop=True)
                    E_.i(ins, "pe")
                E_.pe(f_mb)
                mb_ev = E_.ev("pe")

                def f_mc(dv, mb_ev=mb_ev):
                    dv.wait(mb_ev)
                    dv.tensor_copy(MSC[0][:], PH[0][:, 0:1])
                    ins = dv.tensor_copy(MSC[1][:], PH[0][:, 1:2])
                    E_.i(ins, "dv")
                E_.dv(f_mc)
            with E_.block():
                for c in range(nch):
                    src_ap = wr[c]
                    if wi == 0:
                        dst_ap = WQ[:, :, c * 256:(c + 1) * 256]
                    else:
                        dst_ap = WQ[:, c * 2:(c + 1) * 2, :]

                    def f_lq2(sy, src_ap=src_ap, wi=wi):
                        sy.waitb("f3_wqf")
                        a = 8 if wi == 0 else 2
                        ins = sy.dma_start(
                            WQF[:].rearrange("p (x b) -> p x b", x=a), src_ap)
                        E_.i(ins, "qx", 16)
                    E_.sy(f_lq2)
                    lq_ev = E_.ev("qx")

                    def f_q(dv, lq_ev=lq_ev, dst_ap=dst_ap, wi=wi):
                        dv.wait(lq_ev)
                        for s4 in range(4):
                            sl = slice(s4 * 512, (s4 + 1) * 512)
                            if wi == 0:
                                dst = dst_ap[:, 2 * s4:2 * s4 + 2, :]
                                qv = QT2[:].rearrange("p (x b) -> p x b", x=2)
                            else:
                                dst = dst_ap[:, s4 // 2,
                                             (s4 % 2) * 512:(s4 % 2 + 1) * 512]
                                qv = QT2[:]
                            dv.tensor_scalar(QT[:], WQF[:, sl],
                                             MSC[1][:, 0:1], None, ALU.mult)
                            dv.tensor_scalar(QT2[:], QT[:], 0.5, None, ALU.is_ge)
                            dv.tensor_scalar(QT[:], QT[:], -0.5, None, ALU.is_le)
                            dv.tensor_tensor(QT2[:], QT2[:], QT[:], ALU.subtract)
                            ins = dv.tensor_scalar(dst, qv,
                                                   MSC[0][:, 0:1], None, ALU.mult)
                        E_.i(ins, "dv")
                        E_.setfree("f3_wqf", E_.ev("dv"))
                    E_.dv(f_q)
          wq_ev = E_.ev("dv")
          if qcache is not None:
            with E_.block():
                def f_sq(sy, wq_ev=wq_ev):
                    sy.wait(wq_ev)
                    ins = sy.dma_start(qcache[0], WQ1[:])
                    E_.i(ins, "qs", 16)
                    ins = sy.dma_start(qcache[1], WQ2[:])
                    E_.i(ins, "qs", 16)
                E_.sy(f_sq)

        for tt in range(8):
            with E_.block():
                def f_lx(sy, tt=tt):
                    sy.waitb("f3_xf")
                    ins = sy.dma_start(XF[:], xin[tt])
                    E_.i(ins, "qx", 16)
                E_.sy(f_lx)
                ld_ev = E_.ev("qx")
                xa_ev = emit_ln_tile(E_, C, st, XF, ld_ev, 512, LN_EPS,
                                     out_bf=XA, out_free_key="f3_xa")
                E_.setfree("f3_xf_ln", xa_ev)
                def wait_params(dv):
                    dv.waitb("f3_params")
                E_.dv(wait_params)
                xn1_ev = emit_ln_tile(E_, C, st, XA, xa_ev, 512, LN_EPS,
                                      out_bf=XN1, gamma=L1Gs, beta=L1Bs,
                                      out_free_key="f3_xn1",
                                      ones=C["ones_col_bf"][:])
                E_.setfree("f3_xa", E_.ev("pe"))

                for hc in range(H32):
                    par = hc % 2
                    ph = PH[par]

                    def f_up(pe, hc=hc, ph=ph, xn1_ev=xn1_ev, par=par):
                        pe.wait(xn1_ev)
                        pe.wait(wq_ev)
                        pe.waitb(f"f3_ph{par}")
                        for k in range(G):
                            mi = pe.matmul(ph[:],
                                           WQ1[:, k, hc * 128:(hc + 1) * 128],
                                           XN1[:, k, :],
                                           start=(k == 0), stop=(k == G - 1))
                        E_.i(mi, "pe")
                    E_.pe(f_up)
                    up_ev = E_.ev("pe")

                    def f_gl(ac, hc=hc, ph=ph, up_ev=up_ev, par=par):
                        ac.wait(up_ev)
                        ac.waitb("f3_hsb")
                        ins = ac.activation(HSB[:, hc, :], ph[:], AF.Gelu,
                                            bias=B1[:, hc:hc + 1])
                        E_.i(ins, "ac")
                        E_.setfree(f"f3_ph{par}", E_.ev("ac"))
                    E_.ac(f_gl)
                h_ev = E_.ev("ac")
                E_.setfree("f3_xn1", E_.ev("pe"))

                h2_ev = emit_ln_tile(E_, C, st, HSB, h_ev, 512, LN_EPS,
                                     out_bf=HSB, gamma=L2Gs, beta=L2Bs,
                                     gg=H32, kp="l3b",
                                     ones=C["ones_col_bf"][:])
                E_.setfree("f3_hsb", E_.ev("pe"))

                for dg in range(G):
                    pdb = PD[dg % 2]

                    def f_dn(pe, dg=dg, pdb=pdb, h2_ev=h2_ev):
                        pe.wait(h2_ev)
                        pe.waitb(f"f3_pd{dg % 2}")
                        for k in range(H32):
                            mi = pe.matmul(pdb[:],
                                           WQ2[:, k, dg * 128:(dg + 1) * 128],
                                           HSB[:, k, :],
                                           start=(k == 0), stop=(k == H32 - 1))
                        E_.i(mi, "pe")
                    E_.pe(f_dn)
                    dn_ev = E_.ev("pe")

                    def f_ep(dv, dg=dg, pdb=pdb, dn_ev=dn_ev):
                        dv.wait(dn_ev)
                        dv.tensor_scalar(QT[:], pdb[:], B2[:, dg:dg + 1],
                                         0.5, ALU.add, ALU.mult)
                        ins = dv.tensor_tensor(XF[:, dg, :], QT[:],
                                               XF[:, dg, :], ALU.add)
                        E_.i(ins, "dv")
                        E_.setfree(f"f3_pd{dg % 2}", E_.ev("dv"))
                    E_.dv(f_ep)
                xo_ev = E_.ev("dv")

                def f_st(sy, tt=tt, xo_ev=xo_ev):
                    sy.wait(xo_ev)
                    ins = sy.dma_start(xout[tt], XF[:])
                    E_.i(ins, "qs", 16)
                E_.sy(f_st)
                E_.setfree("f3_xf", E_.ev("qs"))



def emit_att0(E_, C, W, xin, xout):
    nc = E_.nc
    NT = 8
    wint_r = W["a0_int"]
    woutt_r = W["a0_outt"]

    with contextlib.ExitStack() as es:
        uid, sb, ps_ = _mk_alloc(nc, es)
        WIN = sb("a0_win", [128, G, 2048], BF16)
        WOUT = sb("a0_wout", [128, G, 1024], BF16)
        CW = sb("a0_cw_sb", [128, 40], F32)
        CB = sb("a0_cb_sb", [128, G], F32)
        APAR = sb("a0_apar_sb", [128, G], F32)
        ATH = sb("a0_ath", [128, G], F32)
        LNW = sb("a0_lnw_sb", [128, G], F32)
        LNB = sb("a0_lnb_sb", [128, G], F32)
        XF = [sb(f"a0_xf{i}", [128, G, 512], F32) for i in range(2)]
        XNF = [sb(f"a0_xnf{i}", [128, G, 512], F32) for i in range(2)]
        XNB = sb("a0_xnb", [128, G, 512], BF16)
        VB = [sb(f"a0_v{i}", [128, G, 512], F32) for i in range(2)]
        GB = [sb(f"a0_g{i}", [128, G, 512], BF16) for i in range(2)]
        VHL = sb("a0_vhl", [128, G, 2], F32)
        ONESW = sb("a0_ones", [128, 512], F32)
        ABC1 = sb("a0_abc", [128, 512], F32)
        CT = sb("a0_ct", [128, 512], F32)
        GV = sb("a0_gv", [128, 512], F32)
        YS = sb("a0_ys", [128, 512], F32)
        CARRY = sb("a0_carry", [128, G], F32)
        YB = sb("a0_yb", [128, G, 512], BF16)
        PH = [ps_(f"a0_ph{i}", [128, 512])
              for i in range(2)]
        PD = [ps_(f"a0_pd{i}", [128, 512])
              for i in range(2)]
        PMB = ps_("a0_pmb", [128, 512])
        PRBC = ps_("a0_prb", [128, 512])
        st = alloc_ln_scratch(es, nc, "a0_ln", 512, (PD[0], PD[1], PMB, PRBC))

        with E_.block():
            def f_lw(g_):
                ins = g_.dma_start(WIN[:], wint_r)
                E_.i(ins, "qg", 16)
                ins = g_.dma_start(WOUT[:], woutt_r)
                E_.i(ins, "qg", 16)
            E_.po(f_lw)
            wl_ev = E_.ev("qg")

            def f_lp(sy):
                for dst, key in ((CW, "a0_convw"), (CB, "a0_convb"),
                                 (APAR, "a0_a"), (LNW, "a0_lnw"),
                                 (LNB, "a0_lnb")):
                    ins = sy.dma_start(dst[:], W[key])
                    E_.i(ins, "qc", 16)
            E_.sy(f_lp)
            lp_ev = E_.ev("qc")
            E_.setfree("ln_gamma_ld", lp_ev)

            def f_pc(dv):
                dv.wait(lp_ev)
                dv.memset(ONESW[:], 1.0)
                dv.memset(VHL[:], 0.0)
                dv.memset(CARRY[:], 0.0)
                ins = dv.memset(CT[:], 0.0)
                E_.i(ins, "dv")
            E_.dv(f_pc)

            def f_th(ac):
                ac.wait(lp_ev)
                ins = ac.activation(ATH[:], APAR[:], AF.Tanh)
                E_.i(ins, "ac")
            E_.ac(f_th)

        for s_ in range(NT + 1):
            with E_.block():
                cur, prv = s_ % 2, (s_ + 1) % 2
                if s_ < NT:
                    def f_lx(sy, s_=s_, cur=cur):
                        sy.waitb(f"a0_xf{cur}")
                        ins = sy.dma_start(XF[cur][:], xin[s_])
                        E_.i(ins, "qx", 16)
                    E_.sy(f_lx)
                    ld_ev = E_.ev("qx")
                    xn_ev = emit_ln_tile(E_, C, st, XF[cur], ld_ev, 512,
                                         LN_EPS, out_bf=XNB,
                                         out_f32=XNF[cur])
                    for ec in range(16):
                        php = PH[ec % 2]

                        def f_pj(pe, ec=ec, php=php, xn_ev=xn_ev):
                            pe.wait(xn_ev)
                            pe.wait(wl_ev)
                            pe.waitb(f"a0_ph{ec % 2}")
                            for k in range(G):
                                mi = pe.matmul(
                                    php[:], WIN[:, k, ec * 128:(ec + 1) * 128],
                                    XNB[:, k, :], start=(k == 0),
                                    stop=(k == G - 1))
                            E_.i(mi, "pe")
                        E_.pe(f_pj)
                        pj_ev = E_.ev("pe")

                        def f_ev(ac, ec=ec, php=php, pj_ev=pj_ev, cur=cur):
                            ac.wait(pj_ev)
                            if ec < 8:
                                ins = ac.activation(GB[cur][:, ec, :], php[:],
                                                    AF.Sigmoid)
                            else:
                                ins = ac.activation(VB[cur][:, ec - 8, :],
                                                    php[:], AF.Copy)
                            ev = E_.i(ins, "ac")
                            E_.setfree(f"a0_ph{ec % 2}", ev)
                        E_.ac(f_ev)
                    pv_ev = E_.ev("ac")
                else:
                    def f_z(dv, cur=cur):
                        ins = dv.memset(VB[cur][:, :, 0:2], 0.0)
                        E_.i(ins, "dv")
                    E_.dv(f_z)
                    pv_ev = E_.ev("dv")

                if s_ == 0:
                    continue
                ptok0 = (s_ - 1) * 512
                for g in range(G):
                    def f_cv(dv, g=g, prv=prv, cur=cur, pv_ev=pv_ev, s_=s_):
                        dv.wait(pv_ev)
                        vp = VB[prv]
                        vc = VB[cur]
                        dv.tensor_scalar(CT[:, 0:2], VHL[:, g, :],
                                         CW[:, g * 5 + 0:g * 5 + 1], None,
                                         ALU.mult)
                        dv.tensor_scalar(CT[:, 2:512], vp[:, g, 0:510],
                                         CW[:, g * 5 + 0:g * 5 + 1], None,
                                         ALU.mult)
                        dv.scalar_tensor_tensor(CT[:, 0:1], VHL[:, g, 1:2],
                                                CW[:, g * 5 + 1:g * 5 + 2],
                                                CT[:, 0:1], ALU.mult, ALU.add)
                        dv.scalar_tensor_tensor(CT[:, 1:512], vp[:, g, 0:511],
                                                CW[:, g * 5 + 1:g * 5 + 2],
                                                CT[:, 1:512], ALU.mult, ALU.add)
                        dv.scalar_tensor_tensor(CT[:], vp[:, g, :],
                                                CW[:, g * 5 + 2:g * 5 + 3],
                                                CT[:], ALU.mult, ALU.add)
                        dv.scalar_tensor_tensor(CT[:, 0:511], vp[:, g, 1:512],
                                                CW[:, g * 5 + 3:g * 5 + 4],
                                                CT[:, 0:511], ALU.mult, ALU.add)
                        dv.scalar_tensor_tensor(CT[:, 511:512], vc[:, g, 0:1],
                                                CW[:, g * 5 + 3:g * 5 + 4],
                                                CT[:, 511:512], ALU.mult, ALU.add)
                        dv.scalar_tensor_tensor(CT[:, 0:510], vp[:, g, 2:512],
                                                CW[:, g * 5 + 4:g * 5 + 5],
                                                CT[:, 0:510], ALU.mult, ALU.add)
                        dv.scalar_tensor_tensor(CT[:, 510:512], vc[:, g, 0:2],
                                                CW[:, g * 5 + 4:g * 5 + 5],
                                                CT[:, 510:512], ALU.mult, ALU.add)
                        dv.tensor_scalar(CT[:], CT[:], CB[:, g:g + 1], None,
                                         ALU.add)
                        dv.tensor_tensor(GV[:], GB[prv][:, g, :], CT[:],
                                         ALU.mult)
                        dv.tensor_scalar(ABC1[:], ONESW[:], ATH[:, g:g + 1],
                                         None, ALU.mult)
                        dv.tensor_tensor_scan(YS[:], ABC1[:], GV[:],
                                              CARRY[:, g:g + 1], ALU.mult,
                                              ALU.add)
                        dv.tensor_copy(CARRY[:, g:g + 1], YS[:, 511:512])
                        ins = dv.tensor_copy(YB[:, g, :], YS[:])
                        E_.i(ins, "dv")
                        if g == G - 1:
                            ins = dv.tensor_copy(VHL[:], vp[:, :, 510:512])
                            E_.i(ins, "dv")
                    E_.dv(f_cv)
                yb_ev = E_.ev("dv")

                for dg in range(G):
                    pdb = PD[dg % 2]

                    def f_op(pe, dg=dg, pdb=pdb, yb_ev=yb_ev):
                        pe.wait(yb_ev)
                        pe.waitb(f"a0_pd{dg % 2}")
                        for k in range(G):
                            mi = pe.matmul(pdb[:],
                                           WOUT[:, k, dg * 128:(dg + 1) * 128],
                                           YB[:, k, :], start=(k == 0),
                                           stop=(k == G - 1))
                        E_.i(mi, "pe")
                    E_.pe(f_op)
                    op_ev = E_.ev("pe")

                    def f_rs(dv, dg=dg, pdb=pdb, prv=prv, op_ev=op_ev):
                        dv.wait(op_ev)
                        ins = dv.tensor_tensor(XNF[prv][:, dg, :],
                                               XNF[prv][:, dg, :], pdb[:],
                                               ALU.add)
                        ev = E_.i(ins, "dv")
                        E_.setfree(f"a0_pd{dg % 2}", ev)
                    E_.dv(f_rs)
                rs_ev = E_.ev("dv")

                ln2_ev = emit_ln_tile(E_, C, st, XNF[prv], rs_ev, 512, LN_EPS,
                                      out_f32=XNF[prv], gamma=LNW, beta=LNB,
                                      kp="a0l2")

                def f_add(dv, prv=prv, ln2_ev=ln2_ev):
                    dv.wait(ln2_ev)
                    for g in range(G):
                        ins = dv.tensor_tensor(XF[prv][:, g, :],
                                               XF[prv][:, g, :],
                                               XNF[prv][:, g, :], ALU.add)
                    E_.i(ins, "dv")
                E_.dv(f_add)
                xo_ev = E_.ev("dv")

                def f_st(sy, s_=s_, prv=prv, xo_ev=xo_ev):
                    sy.wait(xo_ev)
                    ins = sy.dma_start(xout[s_ - 1], XF[prv][:])
                    E_.i(ins, "qs", 16)
                E_.sy(f_st)
                E_.setfree(f"a0_xf{prv}", E_.ev("qs"))



def emit_att1(E_, C, W, xin, xout):
    nc = E_.nc
    NT = 8
    wq_r = W["a1_wqkv"]

    with contextlib.ExitStack() as es:
        uid, sb, ps_ = _mk_alloc(nc, es)
        KS = sb("a1_ks_sb", [128, G, T], BF16)
        VT = sb("a1_vt", [128, 32, 16, 65], BF16)
        XF = sb("a1_xf", [128, G, 512], F32)
        XNB = sb("a1_xnb", [128, G, 512], BF16)
        QS = XNB
        CTXF = sb("a1_ctxf", [128, G, 512], BF16)
        PSB = [sb(f"a1_p{i}", [128, 512], BF16) for i in range(3)]
        CTB = sb("a1_ctb", [128, 64], BF16)
        RC = sb("a1_rc", [128, 1], F32)
        QSC = sb("a1_qsc_sb", [128, G], F32)
        KSC = sb("a1_ksc_sb", [128, G], F32)
        QS8 = sb("a1_qs8", [128, G], F32)
        BZ = sb("a1_bz", [128, G], F32)
        WSL = [sb(f"a1_wsl{i}", [128, G, 128], BF16) for i in range(2)]
        PH = [ps_(f"a1_ph{i}", [128, 512])
              for i in range(3)]
        PC = [ps_(f"a1_pc{i}", [128, 512])
              for i in range(4)]
        PVT = ps_("a1_pvt", [128, 128], BF16)
        st = alloc_ln_scratch(es, nc, "a1_ln", 512,
                              (PC[0], PC[1], PC[2], PC[3]))

        wslot = [0]

        def load_wslice(col0, name):
            s_ = wslot[0] % 2
            wslot[0] += 1
            buf = WSL[s_]
            wfree = E_.free.get(f"a1_wsl{s_}")
            wsem = "qw" if s_ == 0 else "qv"

            def f_lw(g_, buf=buf, col0=col0, wfree=wfree, wsem=wsem):
                g_.wait(wfree)
                ins = g_.dma_start(buf[:], wq_r[col0 // 128])
                E_.i(ins, wsem, 16)
            E_.po(f_lw)
            return buf, s_, E_.ev(wsem)

        with E_.block():
            def f_lp(sy):
                for dst, key in ((QSC, "a1_qs"), (KSC, "a1_ks")):
                    ins = sy.dma_start(dst[:], W[key])
                    E_.i(ins, "qc", 16)
            E_.sy(f_lp)
            lp_ev = E_.ev("qc")

            def f_pc(dv):
                dv.wait(lp_ev)
                dv.memset(BZ[:], 0.0)
                dv.tensor_scalar(QS8[:], QSC[:], 0.125, None, ALU.mult)
                ins = dv.memset(VT[:, :, :, 64:65], 1.0)
                E_.i(ins, "dv")
            E_.dv(f_pc)
            E_.setfree("ln_gamma_ld", E_.ev("dv"))

        for tt in range(NT):
            with E_.block():
                tok0 = tt * 512

                def f_lx(sy, tt=tt):
                    sy.waitb("a1_xf")
                    ins = sy.dma_start(XF[:], xin[tt])
                    E_.i(ins, "qx", 16)
                E_.sy(f_lx)
                ld_ev = E_.ev("qx")
                xn_ev = emit_ln_tile(E_, C, st, XF, ld_ev, 512, LN_EPS,
                                     out_bf=XNB)
                for ec in range(G):
                    php = PH[ec % 2]
                    wb, ws, wev = load_wslice(1024 + ec * 128, "k")

                    def f_km(pe, ec=ec, php=php, xn_ev=xn_ev, wb=wb, wev=wev):
                        pe.wait(xn_ev)
                        pe.wait(wev)
                        pe.waitb(f"a1_ph{ec % 2}")
                        for kk in range(G):
                            mi = pe.matmul(php[:], wb[:, kk, :],
                                           XNB[:, kk, :], start=(kk == 0),
                                           stop=(kk == G - 1))
                        E_.i(mi, "pe")
                    E_.pe(f_km)
                    km_ev = E_.ev("pe")
                    E_.setfree(f"a1_wsl{ws}", km_ev)

                    def f_ke(dv, ec=ec, php=php, km_ev=km_ev):
                        dv.wait(km_ev)
                        ins = dv.tensor_copy(XF[:, ec, :], php[:])
                        ev = E_.i(ins, "dv")
                        E_.setfree(f"a1_ph{ec % 2}", ev)
                    E_.dv(f_ke)
                kr_ev = E_.ev("dv")
                emit_ln_tile(E_, C, st, XF, kr_ev, 512, 1e-6, out_bf=KS,
                             out_off=tok0, gamma=KSC, beta=BZ, kp="a1k")
                E_.setfree("a1_xf", E_.ev("dv"))
                for ec in range(G):
                    php = PH[ec % 2]
                    wb, ws, wev = load_wslice(2048 + ec * 128, "v")

                    def f_vm(pe, ec=ec, php=php, xn_ev=xn_ev, wb=wb, wev=wev):
                        pe.wait(xn_ev)
                        pe.wait(wev)
                        pe.waitb(f"a1_ph{ec % 2}")
                        for kk in range(G):
                            mi = pe.matmul(
                                php[:], wb[:, kk, :],
                                XNB[:, kk, :], start=(kk == 0),
                                stop=(kk == G - 1))
                        E_.i(mi, "pe")
                    E_.pe(f_vm)
                    vm_ev = E_.ev("pe")
                    E_.setfree(f"a1_wsl{ws}", vm_ev)

                    def f_ve(dv, ec=ec, php=php, vm_ev=vm_ev):
                        dv.wait(vm_ev)
                        dv.waitb("a1_psb0")
                        ins = dv.tensor_copy(PSB[0][:], php[:])
                        ev = E_.i(ins, "dv")
                        E_.setfree(f"a1_ph{ec % 2}", ev)
                    E_.dv(f_ve)
                    ve_ev = E_.ev("dv")

                    for tb in range(4):
                        def f_tr(pe, tb=tb, ve_ev=ve_ev):
                            pe.wait(ve_ev)
                            pe.waitb("a1_pvt")
                            ins = pe.transpose(PVT[:],
                                               PSB[0][:, tb * 128:(tb + 1) * 128],
                                               C["ident_bf16"][:])
                            E_.i(ins, "pe")
                        E_.pe(f_tr)
                        tr_ev = E_.ev("pe")

                        def f_tc(dv, tb=tb, ec=ec, tt=tt, tr_ev=tr_ev):
                            dv.wait(tr_ev)
                            tg = tt * 4 + tb
                            dv.tensor_copy(VT[:, tg, 2 * ec, 0:64],
                                           PVT[:, 0:64])
                            ins = dv.tensor_copy(VT[:, tg, 2 * ec + 1, 0:64],
                                                 PVT[:, 64:128])
                            E_.i(ins, "dv")
                            E_.setfree("a1_pvt", E_.ev("dv"))
                        E_.dv(f_tc)
                    E_.setfree("a1_psb0", E_.ev("pe"))

        wo_r = W["a1_wout"]
        for qb in range(NT):
            with E_.block():
                def f_lx2(sy, qb=qb):
                    sy.waitb("a1_xf")
                    ins = sy.dma_start(XF[:], xin[qb])
                    E_.i(ins, "qx", 16)
                E_.sy(f_lx2)
                ld_ev = E_.ev("qx")
                xn_ev = emit_ln_tile(E_, C, st, XF, ld_ev, 512, LN_EPS,
                                     out_bf=XNB)
                for ec in range(G):
                    php = PH[ec % 2]
                    wb, ws, wev = load_wslice(ec * 128, "q")

                    def f_qm(pe, ec=ec, php=php, xn_ev=xn_ev, wb=wb, wev=wev):
                        pe.wait(xn_ev)
                        pe.wait(wev)
                        pe.waitb(f"a1_ph{ec % 2}")
                        for kk in range(G):
                            mi = pe.matmul(php[:], wb[:, kk, :],
                                           XNB[:, kk, :], start=(kk == 0),
                                           stop=(kk == G - 1))
                        E_.i(mi, "pe")
                    E_.pe(f_qm)
                    qm_ev = E_.ev("pe")
                    E_.setfree(f"a1_wsl{ws}", qm_ev)

                    def f_qe(dv, ec=ec, php=php, qm_ev=qm_ev):
                        dv.wait(qm_ev)
                        ins = dv.tensor_copy(XF[:, ec, :], php[:])
                        ev = E_.i(ins, "dv")
                        E_.setfree(f"a1_ph{ec % 2}", ev)
                    E_.dv(f_qe)
                qr_ev = E_.ev("dv")
                qs_ev = emit_ln_tile(E_, C, st, XF, qr_ev, 512, 1e-6,
                                     out_bf=QS, gamma=QS8, beta=BZ, kp="a1k")
                E_.setfree("a1_xf", E_.ev("dv"))

                nk = 4 * qb + 4
                for h in range(16):
                    hp = 64 * (h % 2)
                    hg = h // 2
                    for kg in range(0, nk, 3):
                        kts = list(range(kg, min(kg + 3, nk)))
                        masked = [kt for kt in kts if kt - 4 * qb >= 0]
                        if masked:
                            def f_pre(dv, masked=masked, qb=qb):
                                for kt in masked:
                                    dv.waitb(f"a1_ph{kt % 3}")
                                    ins = dv.tensor_copy(
                                        PH[kt % 3][:],
                                        C[f"cmask{kt - 4 * qb}"][:])
                                    E_.i(ins, "dv")
                            E_.dv(f_pre)
                        preb = E_.cnt["dv"] - len(masked)

                        def f_scg(pe, kts=kts, masked=masked, hp=hp, hg=hg,
                                  qs_ev=qs_ev, preb=preb):
                            pe.wait(qs_ev)
                            for kt in kts:
                                if kt in masked:
                                    pe.wait(("dv",
                                             preb + 1 + masked.index(kt)))
                                else:
                                    pe.waitb(f"a1_ph{kt % 3}")
                                mi = pe.matmul(
                                    PH[kt % 3][:],
                                    KS[hp:hp + 64, hg, kt * 128:(kt + 1) * 128],
                                    QS[hp:hp + 64, hg, :],
                                    start=(kt not in masked), stop=True)
                                E_.i(mi, "pe")
                        E_.pe(f_scg)
                        scb = E_.cnt["pe"] - len(kts)

                        def f_exg(ac, kts=kts, scb=scb):
                            for i, kt in enumerate(kts):
                                ac.wait(("pe", scb + 1 + i))
                                ac.waitb(f"a1_psb{kt % 3}")
                                ins = ac.activation(PSB[kt % 3][:],
                                                    PH[kt % 3][:], AF.Exp,
                                                    bias=-40.0)
                                ev = E_.i(ins, "ac")
                                E_.setfree(f"a1_ph{kt % 3}", ev)
                        E_.ac(f_exg)
                        exb = E_.cnt["ac"] - len(kts)

                        def f_pvg(pe, kts=kts, h=h, nk=nk, exb=exb):
                            for i, kt in enumerate(kts):
                                pe.wait(("ac", exb + 1 + i))
                                for q4 in range(4):
                                    mi = pe.matmul(
                                        PC[q4][0:128, 0:65],
                                        PSB[kt % 3][:, q4 * 128:(q4 + 1) * 128],
                                        VT[:, kt, h, :],
                                        start=(kt == 0), stop=(kt == nk - 1))
                                    E_.i(mi, "pe")
                                E_.setfree(f"a1_psb{kt % 3}", E_.ev("pe"))
                        E_.pe(f_pvg)
                    pv_ev = E_.ev("pe")

                    for q4 in range(4):
                        def f_nr(dv, q4=q4, pv_ev=pv_ev):
                            dv.wait(pv_ev)
                            dv.waitb("a1_rc")
                            ins = dv.reciprocal(RC[:], PC[q4][:, 64:65])
                            E_.i(ins, "dv")
                        E_.dv(f_nr)
                        nr_ev = E_.ev("dv")

                        def f_sc2(dv, q4=q4, nr_ev=nr_ev):
                            dv.waitb("a1_ctb")
                            ins = dv.tensor_scalar(CTB[:], PC[q4][0:128, 0:64],
                                                   RC[:, 0:1], None, ALU.mult)
                            E_.i(ins, "dv")
                        E_.dv(f_sc2)
                        s2_ev = E_.ev("dv")
                        E_.setfree("a1_rc", s2_ev)

                        def f_tr2(pe, q4=q4, s2_ev=s2_ev):
                            pe.wait(s2_ev)
                            pe.waitb("a1_pvt")
                            ins = pe.transpose(PVT[0:64, :], CTB[:],
                                               C["ident_bf16"][:])
                            E_.i(ins, "pe")
                        E_.pe(f_tr2)
                        t2_ev = E_.ev("pe")
                        E_.setfree("a1_ctb", t2_ev)

                        def f_cc(dv, q4=q4, h=h, hp=hp, hg=hg, t2_ev=t2_ev):
                            dv.wait(t2_ev)
                            ins = dv.tensor_copy(
                                CTXF[hp:hp + 64, hg, q4 * 128:(q4 + 1) * 128],
                                PVT[0:64, :])
                            E_.i(ins, "dv")
                            E_.setfree("a1_pvt", E_.ev("dv"))
                        E_.dv(f_cc)
                ctx_ev = E_.ev("dv")

                def f_lx3(sy, qb=qb, ctx_ev=ctx_ev):
                    sy.waitb("a1_xf")
                    ins = sy.dma_start(XF[:], xin[qb])
                    E_.i(ins, "qx", 16)
                E_.sy(f_lx3)
                ld3_ev = E_.ev("qx")
                for dg in range(G):
                    php = PH[dg % 2]
                    s_ = wslot[0] % 2
                    wslot[0] += 1
                    wob = WSL[s_]
                    wofree = E_.free.get(f"a1_wsl{s_}")
                    wosem = "qw" if s_ == 0 else "qv"

                    def f_low(g_, wob=wob, dg=dg, wofree=wofree, wosem=wosem):
                        g_.wait(wofree)
                        ins = g_.dma_start(wob[:], wo_r[dg])
                        E_.i(ins, wosem, 16)
                    E_.po(f_low)
                    wo_ev = E_.ev(wosem)

                    def f_om(pe, dg=dg, php=php, ctx_ev=ctx_ev, wob=wob,
                             wo_ev=wo_ev):
                        pe.wait(ctx_ev)
                        pe.wait(wo_ev)
                        pe.waitb(f"a1_ph{dg % 2}")
                        for kk in range(G):
                            mi = pe.matmul(php[:], wob[:, kk, :],
                                           CTXF[:, kk, :], start=(kk == 0),
                                           stop=(kk == G - 1))
                        E_.i(mi, "pe")
                    E_.pe(f_om)
                    om_ev = E_.ev("pe")
                    E_.setfree(f"a1_wsl{s_}", om_ev)

                    def f_ra(dv, dg=dg, php=php, om_ev=om_ev, ld3_ev=ld3_ev):
                        dv.wait(om_ev)
                        dv.wait(ld3_ev)
                        ins = dv.tensor_tensor(XF[:, dg, :], XF[:, dg, :],
                                               php[:], ALU.add)
                        ev = E_.i(ins, "dv")
                        E_.setfree(f"a1_ph{dg % 2}", ev)
                    E_.dv(f_ra)
                xo_ev = E_.ev("dv")

                def f_st(sy, qb=qb, xo_ev=xo_ev):
                    sy.wait(xo_ev)
                    ins = sy.dma_start(xout[qb], XF[:])
                    E_.i(ins, "qs", 16)
                E_.sy(f_st)
                E_.setfree("a1_xf", E_.ev("qs"))



def emit_att2(E_, C, W, xin, xout):
    nc = E_.nc
    NB = 8
    wi_r = W["a2_int"]
    wo_r = W["a2_outt"]
    slopes = [2.0 ** (-(h + 1) * 0.5) for h in range(16)]

    with contextlib.ExitStack() as es:
        uid, sb, ps_ = _mk_alloc(nc, es)
        KC = [sb(f"a2_kc{i}", [128, G, 512], BF16) for i in range(2)]
        VT = [sb(f"a2_vt{i}", [128, 4, 16, 65], BF16) for i in range(2)]
        XF = sb("a2_xf", [128, G, 512], F32)
        XNB = sb("a2_xnb", [128, G, 512], BF16)
        QSB = sb("a2_qsb", [128, G, 512], BF16)
        CTXF = sb("a2_ctxf", [128, G, 512], BF16)
        PSB = [sb(f"a2_p{i}", [128, 512], BF16) for i in range(3)]
        CTB = sb("a2_ctb", [128, 64], BF16)
        RC = sb("a2_rc", [128, 1], F32)
        WSL = [sb(f"a2_wsl{i}", [128, G, 128], BF16) for i in range(2)]
        PH = [ps_(f"a2_ph{i}", [128, 512])
              for i in range(3)]
        PC = [ps_(f"a2_pc{i}", [128, 512])
              for i in range(4)]
        PVT = ps_("a2_pvt", [128, 128], BF16)
        st = alloc_ln_scratch(es, nc, "a2_ln", 512,
                              (PC[0], PC[1], PC[2], PC[3]))
        wslot = [0]

        def load_wslice(rsrc, col0):
            s_ = wslot[0] % 2
            wslot[0] += 1
            buf = WSL[s_]
            wfree = E_.free.get(f"a2_wsl{s_}")
            wsem = "qw" if s_ == 0 else "qv"

            def f_lw(g_, buf=buf, rsrc=rsrc, col0=col0, wfree=wfree, wsem=wsem):
                g_.wait(wfree)
                ins = g_.dma_start(buf[:], rsrc[col0 // 128])
                E_.i(ins, wsem, 16)
            E_.po(f_lw)
            return buf, s_, E_.ev(wsem)

        with E_.block():
            def f_pc(dv):
                dv.memset(VT[0][:, :, :, 64:65], 1.0)
                ins = dv.memset(VT[1][:, :, :, 64:65], 1.0)
                E_.i(ins, "dv")
            E_.dv(f_pc)

        for blk in range(NB):
            with E_.block():
                cur, prv = blk % 2, (blk + 1) % 2
                def f_lx(sy, blk=blk):
                    sy.waitb("a2_xf")
                    ins = sy.dma_start(XF[:], xin[blk])
                    E_.i(ins, "qx", 16)
                E_.sy(f_lx)
                ld_ev = E_.ev("qx")
                xn_ev = emit_ln_tile(E_, C, st, XF, ld_ev, 512, LN_EPS,
                                     out_bf=XNB)
                E_.setfree("a2_xf", xn_ev)
                for ec in range(24):
                    php = PH[ec % 2]
                    wb, ws, wev = load_wslice(wi_r, ec * 128)

                    def f_pj(pe, php=php, wb=wb, wev=wev, xn_ev=xn_ev,
                             ec=ec):
                        pe.wait(xn_ev)
                        pe.wait(wev)
                        pe.waitb(f"a2_ph{ec % 2}")
                        for kk in range(G):
                            mi = pe.matmul(php[:], wb[:, kk, :], XNB[:, kk, :],
                                           start=(kk == 0), stop=(kk == G - 1))
                        E_.i(mi, "pe")
                    E_.pe(f_pj)
                    pj_ev = E_.ev("pe")
                    E_.setfree(f"a2_wsl{ws}", pj_ev)

                    if ec < 8:
                        def f_ev(ac, ec=ec, php=php, pj_ev=pj_ev):
                            ac.wait(pj_ev)
                            ins = ac.activation(QSB[:, ec, :], php[:], AF.Copy,
                                                scale=0.125)
                            ev = E_.i(ins, "ac")
                            E_.setfree(f"a2_ph{ec % 2}", ev)
                        E_.ac(f_ev)
                    elif ec < 16:
                        def f_ev(ac, ec=ec, php=php, pj_ev=pj_ev, cur=cur):
                            ac.wait(pj_ev)
                            ins = ac.activation(KC[cur][:, ec - 8, :], php[:],
                                                AF.Copy)
                            ev = E_.i(ins, "ac")
                            E_.setfree(f"a2_ph{ec % 2}", ev)
                        E_.ac(f_ev)
                    else:
                        def f_ev(ac, ec=ec, php=php, pj_ev=pj_ev):
                            ac.wait(pj_ev)
                            ac.waitb("a2_psb0")
                            ins = ac.activation(PSB[0][:], php[:], AF.Copy)
                            ev = E_.i(ins, "ac")
                            E_.setfree(f"a2_ph{ec % 2}", ev)
                        E_.ac(f_ev)
                        ve_ev = E_.ev("ac")
                        vc = ec - 16
                        for tb in range(4):
                            def f_tr(pe, tb=tb, ve_ev=ve_ev):
                                pe.wait(ve_ev)
                                pe.waitb("a2_pvt")
                                ins = pe.transpose(
                                    PVT[:], PSB[0][:, tb * 128:(tb + 1) * 128],
                                    C["ident_bf16"][:])
                                E_.i(ins, "pe")
                            E_.pe(f_tr)
                            tr_ev = E_.ev("pe")

                            def f_tc(dv, tb=tb, vc=vc, cur=cur, tr_ev=tr_ev):
                                dv.wait(tr_ev)
                                dv.tensor_copy(VT[cur][:, tb, 2 * vc, 0:64],
                                               PVT[:, 0:64])
                                ins = dv.tensor_copy(
                                    VT[cur][:, tb, 2 * vc + 1, 0:64],
                                    PVT[:, 64:128])
                                E_.i(ins, "dv")
                                E_.setfree("a2_pvt", E_.ev("dv"))
                            E_.dv(f_tc)
                        E_.setfree("a2_psb0", E_.ev("pe"))
                qkv_ev = [E_.ev("ac"), E_.ev("dv")]

                kts_all = list(range(4, 8) if blk == 0 else range(8))
                for h in range(16):
                    hp, hg = 64 * (h % 2), h // 2
                    for kg in range(0, len(kts_all), 3):
                        kts = kts_all[kg:kg + 3]

                        def f_pre(dv, kts=kts, h=h, blk=blk):
                            for kt in kts:
                                dv.waitb(f"a2_ph{kt % 3}")
                                dv.scalar_tensor_tensor(
                                    PH[kt % 3][:], C[f"l2delta{kt}"][:],
                                    slopes[h], C[f"l2mask{kt}"][:],
                                    ALU.mult, ALU.add)
                                if blk == 0:
                                    dv.scalar_tensor_tensor(
                                        PH[kt % 3][:], C["qidx"][:], -slopes[h],
                                        PH[kt % 3][:], ALU.mult, ALU.add)
                        E_.dv(f_pre)
                        per = 2 if blk == 0 else 1
                        preb = E_.cnt["dv"] - per * len(kts)

                        def f_scg(pe, kts=kts, hp=hp, hg=hg, qkv_ev=qkv_ev,
                                  cur=cur, prv=prv, preb=preb, per=per):
                            pe.wait(qkv_ev)
                            for i, kt in enumerate(kts):
                                ksrc = KC[cur] if kt >= 4 else KC[prv]
                                pe.wait(("dv", preb + per * (i + 1)))
                                mi = pe.matmul(
                                    PH[kt % 3][:],
                                    ksrc[hp:hp + 64, hg,
                                         (kt % 4) * 128:(kt % 4 + 1) * 128],
                                    QSB[hp:hp + 64, hg, :], start=False,
                                    stop=True)
                                E_.i(mi, "pe")
                        E_.pe(f_scg)
                        scb = E_.cnt["pe"] - len(kts)

                        def f_exg(ac, kts=kts, h=h, blk=blk, scb=scb):
                            for i, kt in enumerate(kts):
                                ac.wait(("pe", scb + 1 + i))
                                ac.waitb(f"a2_psb{kt % 3}")
                                bias = 0.0 if blk == 0 else -512.0 * slopes[h]
                                ins = ac.activation(PSB[kt % 3][:],
                                                    PH[kt % 3][:], AF.Exp,
                                                    bias=bias)
                                ev = E_.i(ins, "ac")
                                E_.setfree(f"a2_ph{kt % 3}", ev)
                        E_.ac(f_exg)
                        exb = E_.cnt["ac"] - len(kts)

                        def f_pvg(pe, kts=kts, h=h, exb=exb, cur=cur, prv=prv,
                                  kts_all=kts_all):
                            for i, kt in enumerate(kts):
                                vsrc = VT[cur] if kt >= 4 else VT[prv]
                                pe.wait(("ac", exb + 1 + i))
                                for q4 in range(4):
                                    mi = pe.matmul(
                                        PC[q4][0:128, 0:65],
                                        PSB[kt % 3][:, q4 * 128:(q4 + 1) * 128],
                                        vsrc[:, kt % 4, h, :],
                                        start=(kt == kts_all[0]),
                                        stop=(kt == 7))
                                    E_.i(mi, "pe")
                                E_.setfree(f"a2_psb{kt % 3}", E_.ev("pe"))
                        E_.pe(f_pvg)
                    pv_ev = E_.ev("pe")

                    for q4 in range(4):
                        def f_nr(dv, q4=q4, pv_ev=pv_ev):
                            dv.wait(pv_ev)
                            dv.waitb("a2_rc")
                            ins = dv.reciprocal(RC[:], PC[q4][:, 64:65])
                            E_.i(ins, "dv")
                        E_.dv(f_nr)
                        nr_ev = E_.ev("dv")

                        def f_sc2(dv, q4=q4, nr_ev=nr_ev):
                            dv.waitb("a2_ctb")
                            ins = dv.tensor_scalar(CTB[:], PC[q4][0:128, 0:64],
                                                   RC[:, 0:1], None, ALU.mult)
                            E_.i(ins, "dv")
                        E_.dv(f_sc2)
                        s2_ev = E_.ev("dv")
                        E_.setfree("a2_rc", s2_ev)

                        def f_tr2(pe, q4=q4, s2_ev=s2_ev):
                            pe.wait(s2_ev)
                            pe.waitb("a2_pvt")
                            ins = pe.transpose(PVT[0:64, :], CTB[:],
                                               C["ident_bf16"][:])
                            E_.i(ins, "pe")
                        E_.pe(f_tr2)
                        t2_ev = E_.ev("pe")
                        E_.setfree("a2_ctb", t2_ev)

                        def f_cc(dv, q4=q4, hp=hp, hg=hg, t2_ev=t2_ev):
                            dv.wait(t2_ev)
                            ins = dv.tensor_copy(
                                CTXF[hp:hp + 64, hg, q4 * 128:(q4 + 1) * 128],
                                PVT[0:64, :])
                            E_.i(ins, "dv")
                            E_.setfree("a2_pvt", E_.ev("dv"))
                        E_.dv(f_cc)
                ctx_ev = E_.ev("dv")

                def f_lx3(sy, blk=blk, ctx_ev=ctx_ev):
                    sy.waitb("a2_xf")
                    ins = sy.dma_start(XF[:], xin[blk])
                    E_.i(ins, "qx", 16)
                E_.sy(f_lx3)
                ld3_ev = E_.ev("qx")
                for dg in range(G):
                    php = PH[dg % 2]
                    wb, ws, wev = load_wslice(wo_r, dg * 128)

                    def f_om(pe, dg=dg, php=php, ctx_ev=ctx_ev, wb=wb,
                             wev=wev):
                        pe.wait(ctx_ev)
                        pe.wait(wev)
                        pe.waitb(f"a2_ph{dg % 2}")
                        for kk in range(G):
                            mi = pe.matmul(php[:], wb[:, kk, :],
                                           CTXF[:, kk, :], start=(kk == 0),
                                           stop=(kk == G - 1))
                        E_.i(mi, "pe")
                    E_.pe(f_om)
                    om_ev = E_.ev("pe")
                    E_.setfree(f"a2_wsl{ws}", om_ev)

                    def f_ra(dv, dg=dg, php=php, om_ev=om_ev, ld3_ev=ld3_ev):
                        dv.wait(om_ev)
                        dv.wait(ld3_ev)
                        ins = dv.tensor_tensor(XF[:, dg, :], XF[:, dg, :],
                                               php[:], ALU.add)
                        ev = E_.i(ins, "dv")
                        E_.setfree(f"a2_ph{dg % 2}", ev)
                    E_.dv(f_ra)
                xo_ev = E_.ev("dv")

                def f_st(sy, blk=blk, xo_ev=xo_ev):
                    sy.wait(xo_ev)
                    ins = sy.dma_start(xout[blk], XF[:])
                    E_.i(ins, "qs", 16)
                E_.sy(f_st)
                E_.setfree("a2_xf", E_.ev("qs"))



def emit_att3(E_, C, W, xin, xout):
    nc = E_.nc
    CH = 256
    NCH = T // CH
    G16 = 16
    wi_r = W["a3_int"]
    wo_r = W["a3_outt"]
    xp_r = W["a3_xpt"]

    with contextlib.ExitStack() as es:
        uid, sb, ps_ = _mk_alloc(nc, es)
        WOP = sb("a3_wop", [128, G16, 1024], BF16)
        WXP = sb("a3_wxp", [128, G16, 96], F32)
        WDT = sb("a3_wdt", [64, 2048], BF16)
        WSL = [sb(f"a3_wsl{i}", [128, G, 128], BF16) for i in range(2)]
        NW = sb("a3_nw_sb", [128, G], F32)
        CW = sb("a3_cw2", [128, 64], F32)
        CB = sb("a3_cb2", [128, G16], F32)
        DTB = sb("a3_dtb_sb", [128, G16], F32)
        DP = sb("a3_dp", [128, G16], F32)
        AN = sb("a3_an", [128, 256], F32)
        XF = [sb(f"a3_xf2{i}", [128, G, CH], F32) for i in range(2)]
        HINB = sb("a3_hinb", [128, G, CH], BF16)
        XMF = sb("a3_xmf", [128, G16, CH], F32)
        XCF = sb("a3_xcf", [128, G16, CH], F32)
        SZ = sb("a3_sz", [128, G16, CH], BF16)
        DT = XMF
        DTR = sb("a3_dtr", [64, CH], BF16)
        BR = sb("a3_br", [16, CH], F32)
        CR = sb("a3_cr", [16, CH], F32)
        BBC = sb("a3_bbc", [128, CH, 16], BF16)
        CBC = sb("a3_cbc", [128, CH, 16], BF16)
        DAB = sb("a3_dab", [128, CH, 16], F32)
        DBU = sb("a3_dbu", [128, 2, CH, 16], F32)
        YG = sb("a3_yg", [128, CH], F32)
        UT = sb("a3_ut", [128, CH], F32)
        Y2B = sb("a3_y2b", [128, G16, CH], BF16)
        CAR = sb("a3_car", [128, G16, 16], F32)
        XH = sb("a3_xh", [128, G16, 3], F32)
        PH = [ps_(f"a3_ph{i}", [128, 512])
              for i in range(2)]
        PC = [ps_(f"a3_pc{i}", [128, 512])
              for i in range(4)]
        st = alloc_ln_scratch(es, nc, "a3_ln", CH,
                              (PC[0], PC[1], PC[2], PC[3]))
        wslot = [0]

        def load_wslice(rsrc, col0):
            s_ = wslot[0] % 2
            wslot[0] += 1
            buf = WSL[s_]
            wfree = E_.free.get(f"a3_wsl{s_}")
            wsem = "qw" if s_ == 0 else "qv"

            def f_lw(g_, buf=buf, rsrc=rsrc, col0=col0, wfree=wfree, wsem=wsem):
                g_.wait(wfree)
                ins = g_.dma_start(buf[:], rsrc[col0 // 128])
                E_.i(ins, wsem, 16)
            E_.po(f_lw)
            return buf, s_, E_.ev(wsem)

        with E_.block():
            def f_lw0(g_):
                ins = g_.dma_start(WOP[:], wo_r)
                E_.i(ins, "qg", 16)
                ins = g_.dma_start(WXP[:], xp_r)
                E_.i(ins, "qg", 16)
                ins = g_.dma_start(WDT[:], W["a3_dtpt"])
                E_.i(ins, "qg", 16)
            E_.po(f_lw0)
            w0_ev = E_.ev("qg")

            def f_lp(sy):
                for dst, key in ((NW, "a3_nw"), (CW, "a3_convw"),
                                 (CB, "a3_convb"), (DTB, "a3_dtb"),
                                 (DP, "a3_d"), (AN, "a3_alog")):
                    ins = sy.dma_start(dst[:], W[key])
                    E_.i(ins, "qc", 16)
            E_.sy(f_lp)
            lp_ev = E_.ev("qc")

            def f_an(ac, lp_ev=lp_ev):
                ac.wait(lp_ev)
                ins = ac.activation(AN[:], AN[:], AF.Exp)
                E_.i(ins, "ac")
            E_.ac(f_an)
            an_ev = E_.ev("ac")

            def f_an2(dv, an_ev=an_ev):
                dv.wait(an_ev)
                dv.tensor_scalar(AN[:], AN[:], -1.0, None, ALU.mult)
                dv.memset(CAR[:], 0.0)
                ins = dv.memset(XH[:], 0.0)
                E_.i(ins, "dv")
            E_.dv(f_an2)

        ANv = AN[:].rearrange("p (g n) -> p g n", g=16)


        def emit_pre_a(ci, cur):
            cpar = ci % 2
            Xb = XF[cpar]
            cur["Xb"] = Xb

            def f_lx(sy, ci=ci, cpar=cpar, Xb=Xb):
                sy.waitb(f"a3_xf{cpar}")
                off = (ci % 2) * CH
                ins = sy.dma_start(Xb[:], xin[ci // 2][:, :, off:off + CH])
                E_.i(ins, "qx", 16)
            E_.sy(f_lx)
            ld_ev = E_.ev("qx")
            xn_ev = emit_ln_tile(E_, C, st, Xb, ld_ev, CH, LN_EPS,
                                 out_f32=Xb)

            for g in range(G):
                par = g % 2

                def f_sq(ac, g=g, par=par, xn_ev=xn_ev, Xb=Xb):
                    ac.wait(xn_ev)
                    ac.waitb(f"a3_sq{par}")
                    ins = ac.activation(st["SQ"][:, par, :], Xb[:, g, :],
                                        AF.Square)
                    E_.i(ins, "ac")
                E_.ac(f_sq)
                sq_ev = E_.ev("ac")

                def f_s1(pe, g=g, par=par, sq_ev=sq_ev):
                    pe.wait(sq_ev)
                    if g == 0:
                        pe.waitb("a3_ssq")
                    mi = pe.matmul(st["S1"], C["ones_col"][:],
                                   st["SQ"][:, par, :], start=(g == 0),
                                   stop=(g == G - 1))
                    ev = E_.i(mi, "pe")
                    E_.setfree(f"a3_sq{par}", ev)
                E_.pe(f_s1)
            ssq_ev = E_.ev("pe")

            def f_rms(ac, ssq_ev=ssq_ev):
                ac.wait(ssq_ev)
                ac.activation(st["M"][:], st["S1"], AF.Sqrt)
                ins = ac.activation(st["M"][:], st["M"][:], AF.Copy,
                                    scale=1.0 / 32.0)
                E_.i(ins, "ac")
            E_.ac(f_rms)
            rms_ev = E_.ev("ac")

            def f_ri(dv, rms_ev=rms_ev):
                dv.wait(rms_ev)
                dv.tensor_scalar(st["M"][:], st["M"][:], 1e-5, None,
                                 ALU.add)
                ins = dv.reciprocal(st["M"][:], st["M"][:])
                E_.i(ins, "dv")
            E_.dv(f_ri)
            ri_ev = E_.ev("dv")
            E_.setfree("a3_ssq", ri_ev)

            def f_rb(pe, ri_ev=ri_ev):
                pe.wait(ri_ev)
                pe.waitb("a3_rb")
                ins = pe.matmul(st["RB"], C["ones_row"][:], st["M"][:],
                                start=True, stop=True)
                E_.i(ins, "pe")
            E_.pe(f_rb)
            rb_ev = E_.ev("pe")

            def f_hin(dv, rb_ev=rb_ev, Xb=Xb):
                dv.wait(rb_ev)
                dv.waitb("ln_gamma_ld")
                dv.waitb("a3_hinb")
                for g in range(G):
                    dv.tensor_tensor(st["T1"][:], Xb[:, g, :], st["RB"],
                                     ALU.mult)
                    ins = dv.tensor_scalar(HINB[:, g, :], st["T1"][:],
                                           NW[:, g:g + 1], None, ALU.mult)
                E_.i(ins, "dv")
            E_.dv(f_hin)
            hin_ev = E_.ev("dv")
            E_.setfree("a3_rb", hin_ev)
            E_.setfree(f"a3_xf{cpar}", hin_ev)
            cur["hin_ev"] = hin_ev

        def emit_scan_g(pv, g):
            par = g % 2
            dt_ev = pv["dt_ev"]

            def f_u(g_, g=g, par=par, dt_ev=dt_ev):
                g_.wait(dt_ev)
                g_.waitb(f"a3_dbu{par}")
                g_.tensor_tensor(UT[:], DT[:, g, :], XCF[:, g, :],
                                 ALU.mult)
                for n in range(16):
                    g_.tensor_tensor(DBU[:, par, :, n], UT[:],
                                     BBC[:, :, n], ALU.mult)
            E_.po(f_u)
            pob = E_.cnt["po"] - 16
            pv["po_ev"][g] = E_.ev("po")

            def f_dag(ac, g=g, dt_ev=dt_ev):
                ac.wait(dt_ev)
                for n in range(16):
                    ac.waitb(f"a3_dab{n}")
                    ins = ac.activation(DAB[:, :, n], DT[:, g, :],
                                        AF.Exp,
                                        scale=ANv[:, g, n:n + 1])
                    E_.i(ins, "ac")
            E_.ac(f_dag)
            dab_ = E_.cnt["ac"] - 16

            def f_dbg(dv, g=g, par=par, dab_=dab_, pob=pob):
                for n in range(16):
                    dv.wait(("ac", dab_ + 1 + n))
                    dv.wait(("po", pob + 1 + n))
                    ins = dv.tensor_tensor_scan(
                        DBU[:, par, :, n], DAB[:, :, n],
                        DBU[:, par, :, n],
                        CAR[:, g, n:n + 1], ALU.mult, ALU.add)
                    ev = E_.i(ins, "dv")
                    E_.setfree(f"a3_dab{n}", ev)
            E_.dv(f_dbg)

            def f_y(dv, g=g, par=par):
                dv.tensor_copy(CAR[:, g, :], DBU[:, par, CH - 1, :])
                dv.tensor_tensor(DBU[:, par], DBU[:, par], CBC[:],
                                 ALU.mult)
                ins = dv.tensor_reduce(YG[:], DBU[:, par, :, :],
                                       axis=AX.X, op=ALU.add)
                ev = E_.i(ins, "dv")
                E_.setfree(f"a3_dbu{par}", ev)
                YT = st["T1"]
                dv.scalar_tensor_tensor(YT[:], XCF[:, g, :],
                                        DP[:, g:g + 1], YG[:],
                                        ALU.mult, ALU.add)
                ins = dv.tensor_tensor(Y2B[:, g, :], YT[:],
                                       SZ[:, g, :], ALU.mult)
                E_.i(ins, "dv")
            E_.dv(f_y)
            pv["y_ev"][g] = E_.ev("dv")

        def emit_inproj_ec(cur, ec, pv):
            php = PH[ec % 2]
            wb, ws, wev = load_wslice(wi_r, ec * 128)
            hin_ev = cur["hin_ev"]

            def f_ip(pe, php=php, wb=wb, wev=wev, hin_ev=hin_ev, ec=ec):
                pe.wait(hin_ev)
                pe.wait(wev)
                pe.waitb(f"a3_ph{ec % 2}")
                for kk in range(G):
                    mi = pe.matmul(php[0:128, 0:CH], wb[:, kk, :],
                                   HINB[:, kk, :], start=(kk == 0),
                                   stop=(kk == G - 1))
                E_.i(mi, "pe")
            E_.pe(f_ip)
            ip_ev = E_.ev("pe")
            E_.setfree(f"a3_wsl{ws}", ip_ev)
            gd = ec if ec < 16 else ec - 16
            w_ev = None
            if pv is not None:
                w_ev = pv["po_ev"][gd] if ec < 16 else pv["y_ev"][gd]

            def f_ie(ac, ec=ec, php=php, ip_ev=ip_ev, w_ev=w_ev, gd=gd):
                ac.wait(ip_ev)
                ac.wait(w_ev)
                if ec < 16:
                    ins = ac.activation(XMF[:, gd, :], php[0:128, 0:CH],
                                        AF.Copy)
                else:
                    ins = ac.activation(SZ[:, gd, :],
                                        php[0:128, 0:CH], AF.Silu)
                ev = E_.i(ins, "ac")
                E_.setfree(f"a3_ph{ec % 2}", ev)
            E_.ac(f_ie)

        def emit_pre_b(cur):
            xm_ev = cur["xm_ev"]
            ipz_ev = cur["ipz_ev"]

            for g in range(G16):
                def f_cv(dv, g=g, xm_ev=xm_ev):
                    dv.wait(xm_ev)
                    dv.waitb("a3_xcf")
                    dv.tensor_scalar(UT[:, 0:3], XH[:, g, :],
                                     CW[:, g * 4:g * 4 + 1], None, ALU.mult)
                    dv.tensor_scalar(UT[:, 3:CH], XMF[:, g, 0:CH - 3],
                                     CW[:, g * 4:g * 4 + 1], None, ALU.mult)
                    for k2 in range(1, 4):
                        nh = 3 - k2
                        if nh > 0:
                            dv.scalar_tensor_tensor(
                                UT[:, 0:nh], XH[:, g, 3 - nh:3],
                                CW[:, g * 4 + k2:g * 4 + k2 + 1],
                                UT[:, 0:nh], ALU.mult, ALU.add)
                        dv.scalar_tensor_tensor(
                            UT[:, nh:CH], XMF[:, g, 0:CH - nh],
                            CW[:, g * 4 + k2:g * 4 + k2 + 1],
                            UT[:, nh:CH], ALU.mult, ALU.add)
                    ins = dv.tensor_scalar(UT[:], UT[:], CB[:, g:g + 1],
                                           None, ALU.add)
                    E_.i(ins, "dv")
                    if g == G16 - 1:
                        ins = dv.tensor_copy(XH[:], XMF[:, :, CH - 3:CH])
                        E_.i(ins, "dv")
                E_.dv(f_cv)
                cv_ev = E_.ev("dv")

                def f_si(ac, g=g, cv_ev=cv_ev):
                    ac.wait(cv_ev)
                    ins = ac.activation(XCF[:, g, :], UT[:], AF.Silu)
                    E_.i(ins, "ac")
                E_.ac(f_si)
            xc_ev = [E_.ev("dv"), E_.ev("ac")]

            def f_xp(pe, xc_ev=xc_ev, ipz_ev=ipz_ev):
                pe.wait(xc_ev)
                pe.wait(ipz_ev)
                pe.wait(w0_ev)
                pe.waitb("a3_pdt")
                for kk in range(G16):
                    mi = pe.matmul(PH[0][0:64, 0:CH], WXP[:, kk, 0:64],
                                   XCF[:, kk, :], start=(kk == 0),
                                   stop=(kk == G16 - 1))
                E_.i(mi, "pe")
                pe.waitb("a3_pbc")
                for kk in range(G16):
                    mi = pe.matmul(PC[2][0:16, 0:CH], WXP[:, kk, 64:80],
                                   XCF[:, kk, :], start=(kk == 0),
                                   stop=(kk == G16 - 1))
                E_.i(mi, "pe")
                for kk in range(G16):
                    mi = pe.matmul(PC[3][0:16, 0:CH], WXP[:, kk, 80:96],
                                   XCF[:, kk, :], start=(kk == 0),
                                   stop=(kk == G16 - 1))
                E_.i(mi, "pe")
            E_.pe(f_xp)
            xp_ev = E_.ev("pe")

            def f_xe(ac, xp_ev=xp_ev):
                ac.wait(xp_ev)
                ac.waitb("a3_dtr")
                ins = ac.activation(DTR[:], PH[0][0:64, 0:CH], AF.Copy)
                E_.i(ins, "ac")
            E_.ac(f_xe)
            dtr_ev = E_.ev("ac")
            E_.setfree("a3_pdt", dtr_ev)

            def f_bc(dv, xp_ev=xp_ev):
                dv.wait(xp_ev)
                dv.tensor_copy(BR[:], PC[2][0:16, 0:CH])
                ins = dv.tensor_copy(CR[:], PC[3][0:16, 0:CH])
                E_.i(ins, "dv")
            E_.dv(f_bc)
            br_ev = E_.ev("dv")
            E_.setfree("a3_pbc", br_ev)

            for n in range(16):
                def f_bb(pe, n=n, br_ev=br_ev, dtr_ev=dtr_ev):
                    pe.wait(br_ev)
                    pe.wait(dtr_ev)
                    pe.waitb("a3_pbb")
                    pe.waitb("a3_ph0")
                    pe.waitb("a3_ph1")
                    mi = pe.matmul(PH[0][0:128, 0:CH],
                                   C["sel16"][0:16, n * 128:(n + 1) * 128],
                                   BR[:], start=True, stop=True)
                    E_.i(mi, "pe")
                    mi = pe.matmul(PH[1][0:128, 0:CH],
                                   C["sel16"][0:16, n * 128:(n + 1) * 128],
                                   CR[:], start=True, stop=True)
                    E_.i(mi, "pe")
                E_.pe(f_bb)
                bb_ev = E_.ev("pe")

                def f_bcp(dv, n=n, bb_ev=bb_ev):
                    dv.wait(bb_ev)
                    dv.waitb("a3_bbc")
                    dv.tensor_copy(BBC[:, :, n], PH[0][0:128, 0:CH])
                    ins = dv.tensor_copy(CBC[:, :, n], PH[1][0:128, 0:CH])
                    E_.i(ins, "dv")
                    E_.setfree("a3_pbb", E_.ev("dv"))
                E_.dv(f_bcp)

            for g in range(G16):
                php = PH[g % 2]

                def f_dm(pe, g=g, php=php, dtr_ev=dtr_ev):
                    pe.wait(dtr_ev)
                    pe.waitb("a3_pbb")
                    pe.waitb(f"a3_ph{g % 2}")
                    mi = pe.matmul(php[0:128, 0:CH],
                                   WDT[0:64, g * 128:(g + 1) * 128],
                                   DTR[:], start=True, stop=True)
                    E_.i(mi, "pe")
                E_.pe(f_dm)
                dm_ev = E_.ev("pe")

                def f_sp(dv, g=g, php=php, dm_ev=dm_ev):
                    dv.wait(dm_ev)
                    ins = dv.tensor_scalar(DT[:, g, :], php[0:128, 0:CH],
                                           DTB[:, g:g + 1], None, ALU.add)
                    ev = E_.i(ins, "dv")
                    E_.setfree(f"a3_ph{g % 2}", ev)
                E_.dv(f_sp)
            x0_ev = E_.ev("dv")

            def f_spb(ac, x0_ev=x0_ev):
                ac.wait(x0_ev)
                for g in range(G16):
                    ins = ac.activation(DAB[:, :, g], DT[:, g, :], AF.Abs)
                E_.i(ins, "ac")
                for g in range(G16):
                    ins = ac.activation(DAB[:, :, g], DAB[:, :, g],
                                        AF.Exp, scale=-1.0)
                E_.i(ins, "ac")
                for g in range(G16):
                    ins = ac.activation(DAB[:, :, g], DAB[:, :, g],
                                        AF.Ln, bias=1.0)
                E_.i(ins, "ac")
            E_.ac(f_spb)
            spb_ev = E_.ev("ac")

            def f_spf(dv, spb_ev=spb_ev):
                dv.wait(spb_ev)
                for g in range(G16):
                    dv.tensor_scalar(DT[:, g, :], DT[:, g, :], 0.0, None,
                                     ALU.max)
                    ins = dv.tensor_tensor(DT[:, g, :], DT[:, g, :],
                                           DAB[:, :, g], ALU.add)
                E_.i(ins, "dv")
            E_.dv(f_spf)
            cur["dt_ev"] = E_.ev("dv")

        def emit_post(pv):
            ci = pv["ci"]
            cpar = ci % 2
            Xb = pv["Xb"]
            y_all = pv["y_ev"][G16 - 1]

            def f_lx2(sy, ci=ci, cpar=cpar, Xb=Xb):
                sy.waitb(f"a3_xf{cpar}")
                off = (ci % 2) * CH
                ins = sy.dma_start(Xb[:], xin[ci // 2][:, :, off:off + CH])
                E_.i(ins, "qx", 16)
            E_.sy(f_lx2)
            ld2_ev = E_.ev("qx")
            for dg in range(G):
                php = PH[dg % 2]

                def f_op(pe, dg=dg, php=php, y_all=y_all):
                    pe.wait(y_all)
                    pe.waitb(f"a3_ph{dg % 2}")
                    for kk in range(G16):
                        mi = pe.matmul(php[0:128, 0:CH],
                                       WOP[:, kk, dg * 128:(dg + 1) * 128],
                                       Y2B[:, kk, :], start=(kk == 0),
                                       stop=(kk == G16 - 1))
                    E_.i(mi, "pe")
                E_.pe(f_op)
                op_ev = E_.ev("pe")

                def f_ra(dv, dg=dg, php=php, op_ev=op_ev, ld2_ev=ld2_ev,
                         Xb=Xb):
                    dv.wait(op_ev)
                    dv.wait(ld2_ev)
                    ins = dv.tensor_tensor(Xb[:, dg, :], Xb[:, dg, :],
                                           php[0:128, 0:CH], ALU.add)
                    ev = E_.i(ins, "dv")
                    E_.setfree(f"a3_ph{dg % 2}", ev)
                E_.dv(f_ra)
            xo_ev = E_.ev("dv")

            def f_st(sy, ci=ci, xo_ev=xo_ev, Xb=Xb):
                sy.wait(xo_ev)
                off = (ci % 2) * CH
                ins = sy.dma_start(xout[ci // 2][:, :, off:off + CH], Xb[:])
                E_.i(ins, "qs", 16)
            E_.sy(f_st)
            E_.setfree(f"a3_xf{cpar}", E_.ev("qs"))

        prev = None
        for ci in range(NCH + 1):
            with E_.block():
                cur = None
                if ci < NCH:
                    cur = {"ci": ci}
                    emit_pre_a(ci, cur)
                if prev is not None:
                    prev["po_ev"] = {}
                    prev["y_ev"] = {}
                for g in range(G16):
                    if prev is not None:
                        emit_scan_g(prev, g)
                    if cur is not None:
                        emit_inproj_ec(cur, g, prev)
                if cur is not None:
                    cur["xm_ev"] = E_.ev("ac")
                    for g in range(G16):
                        emit_inproj_ec(cur, 16 + g, prev)
                    cur["ipz_ev"] = E_.ev("ac")
                    E_.setfree("a3_hinb", E_.ev("pe"))
                    if prev is not None:
                        E_.setfree("a3_xcf", prev["y_ev"][G16 - 1])
                        E_.setfree("a3_bbc", prev["y_ev"][G16 - 1])
                    emit_pre_b(cur)
                if prev is not None:
                    emit_post(prev)
                prev = cur



def _c(a):
    return np.ascontiguousarray(a)


def _pp(v, g):
    return _c(v.reshape(g, 128).T.astype(np.float32))


def _pkn(wT):
    K = wT.shape[0] // 128
    return _c(np.asarray(wT).reshape(K, 128, wT.shape[1]).transpose(1, 0, 2))


def _blkc(wT, w):
    KK, N = wT.shape
    K = KK // 128
    return _c(np.asarray(wT).reshape(K, 128, N // w, w).transpose(2, 1, 0, 3))


def _blkr(wT, kw):
    KK, N = wT.shape
    nt = KK // (128 * kw)
    return _c(np.asarray(wT).reshape(nt, kw, 128, N).transpose(0, 2, 1, 3))


def _blkx(xb):
    Tn, Dn = xb.shape
    return _c(np.asarray(xb).reshape(Tn // 512, 512, Dn // 128, 128)
              .transpose(0, 3, 2, 1))


def _unblkx(arr):
    nt, p, g, w = arr.shape
    return np.ascontiguousarray(arr.transpose(0, 3, 2, 1).reshape(nt * w, g * p))


def _prep_shared(I):
    S = {}
    S["ones_col"] = np.ones((128, 1), np.float32)
    S["ones_row"] = np.ones((1, 128), np.float32)
    S["ident_f32"] = np.eye(128, dtype=np.float32)
    sel8 = np.zeros((8, 8 * 128), np.float32)
    for e in range(8):
        sel8[e, e * 128:(e + 1) * 128] = 1.0
    S["sel8"] = sel8
    sel16 = np.zeros((16, 16 * 128), np.float32)
    for n in range(16):
        sel16[n, n * 128:(n + 1) * 128] = 1.0
    S["sel16"] = sel16
    qf = np.arange(512)[None, :]
    kf = np.arange(128)[:, None]
    for j in range(4):
        S[f"cmask{j}"] = np.where(qf >= 128 * j + kf, 0.0, -30000.0).astype(np.float32)
    qi = np.arange(512)[None, :]
    for grp in range(8):
        kj = (np.arange(128) + grp * 128)[:, None]
        delta = qi - kj + 512
        valid = (delta >= 0) & (delta <= 512)
        S[f"l2delta{grp}"] = delta.astype(np.float32)
        S[f"l2mask{grp}"] = np.where(valid, 0.0, -30000.0).astype(np.float32)
    S["qidx"] = np.broadcast_to(np.arange(512, dtype=np.float32)[None, :],
                                (128, 512)).copy()

    S["f0_gatew"] = _pkn(I["L0_ffn_gate_w"].T)
    S["f0_gateb"] = _c(np.asarray(I["L0_ffn_gate_bias"], np.float32)[None, :])
    S["f0_w1t"] = np.stack([_blkc(np.asarray(I["L0_ffn_w1"][e]).T, 512)
                            for e in range(8)])
    S["f0_w3t"] = np.stack([_blkc(np.asarray(I["L0_ffn_w3"][e]).T, 512)
                            for e in range(8)])
    S["f0_w2t"] = np.stack([_blkr(np.asarray(I["L0_ffn_w2"][e]).T, 4)
                            for e in range(8)])
    S["f0_sw1t"] = _blkc(np.asarray(I["L0_ffn_sw1"]).T, 512)
    S["f0_sw3t"] = _blkc(np.asarray(I["L0_ffn_sw3"]).T, 512)
    S["f0_sw2t"] = _blkr(np.asarray(I["L0_ffn_sw2"]).T, 4)
    S["a0_int"] = _pkn(I["L0_attn_W_in"].T)
    S["a0_outt"] = _pkn(I["L0_attn_W_out"].T)
    S["a0_convw"] = _c(np.asarray(I["L0_attn_conv_w"]).reshape(1024, 5)
                       .reshape(8, 128, 5).transpose(1, 0, 2).reshape(128, 40))
    S["a0_convb"] = _pp(I["L0_attn_conv_b"], 8)
    S["a0_a"] = _pp(I["L0_attn_a_param"], 8)
    S["a0_lnw"] = _pp(I["L0_attn_ln_w"], 8)
    S["a0_lnb"] = _pp(I["L0_attn_ln_b"], 8)
    S["a1_wqkv"] = _blkc(np.asarray(I["L1_attn_W_qkv"]), 128)
    S["a1_wout"] = _blkc(np.asarray(I["L1_attn_W_out"]), 128)
    S["a1_qs"] = _pp(I["L1_attn_q_scale"], 8)
    S["a1_ks"] = _pp(I["L1_attn_k_scale"], 8)
    S["f1_routw"] = _pkn(I["L1_ffn_router_w"])
    S["f1_up"] = np.stack([_blkc(np.asarray(I["L1_ffn_up_w"][e]), 512)
                           for e in range(8)])
    S["f1_dn"] = np.stack([_blkr(np.asarray(I["L1_ffn_down_w"][e]), 4)
                           for e in range(8)])
    S["a2_int"] = _blkc(np.asarray(I["L2_attn_w_in"]).T, 128)
    S["a2_outt"] = _blkc(np.asarray(I["L2_attn_w_out"]).T, 128)
    S["f2_lns"] = _pp(I["L2_ffn_ln_scale"], 8)
    S["f2_lnb"] = _pp(I["L2_ffn_ln_bias"], 8)
    S["f2_gatew"] = _pkn(I["L2_ffn_gate_w"].T)
    S["f2_wg"] = np.stack([_blkc(np.asarray(I["L2_ffn_Wg"][e]), 512)
                           for e in range(8)])
    S["f2_wu"] = np.stack([_blkc(np.asarray(I["L2_ffn_Wu"][e]), 512)
                           for e in range(8)])
    S["f2_wd"] = np.stack([_blkr(np.asarray(I["L2_ffn_Wd"][e]), 4)
                           for e in range(8)])
    S["f2_swg"] = _blkc(np.asarray(I["L2_ffn_sWg"]), 512)
    S["f2_swu"] = _blkc(np.asarray(I["L2_ffn_sWu"]), 512)
    S["f2_swd"] = _blkr(np.asarray(I["L2_ffn_sWd"]), 4)
    S["a3_nw"] = _pp(I["L3_attn_norm_weight"], 8)
    S["a3_int"] = _blkc(np.asarray(I["L3_attn_in_proj_w"]).T, 128)
    S["a3_convw"] = _c(np.asarray(I["L3_attn_conv_w"]).reshape(16, 128, 4)
                       .transpose(1, 0, 2).reshape(128, 64))
    S["a3_convb"] = _pp(I["L3_attn_conv_b"], 16)
    S["a3_xpt"] = _pkn(I["L3_attn_x_proj_w"].T)
    S["a3_dtpt"] = _c(I["L3_attn_dt_proj_w"].T)
    S["a3_dtb"] = _pp(I["L3_attn_dt_proj_b"], 16)
    S["a3_alog"] = _c(np.asarray(I["L3_attn_A_log"]).reshape(16, 128, 16)
                      .transpose(1, 0, 2).reshape(128, 256))
    S["a3_d"] = _pp(I["L3_attn_D"], 16)
    S["a3_outt"] = _pkn(I["L3_attn_out_proj_w"].T)
    S["f3_ln1g"] = _pp(I["L3_ffn_ln1_g"], 8)
    S["f3_ln1b"] = _pp(I["L3_ffn_ln1_b"], 8)
    S["f3_w1t"] = _blkc(np.asarray(I["L3_ffn_w1"]).T, 256)
    S["f3_b1"] = _pp(I["L3_ffn_b1"], 32)
    S["f3_ln2g"] = _pp(I["L3_ffn_ln2_g"], 32)
    S["f3_ln2b"] = _pp(I["L3_ffn_ln2_b"], 32)
    S["f3_w2t"] = _blkr(np.asarray(I["L3_ffn_w2"]).T, 2)
    S["f3_b2"] = _pp(I["L3_ffn_b2"], 8)
    return S


CONSTS_BASE = ["ones_col", "ones_row"]
CONSTS_BY_SUB = {
    0: ["sel8", "ident_f32"], 2: ["sel8", "ident_f32"],
    3: ["sel8", "ident_f32"], 5: ["sel8", "ident_f32"],
    6: ["sel8", "ident_f32"], 8: ["sel8", "ident_f32"],
    9: [], 11: [],
    1: [],
    4: [f"cmask{j}" for j in range(4)] + ["ident_f32"],
    7: [f"l2delta{g}" for g in range(8)] + [f"l2mask{g}" for g in range(8)]
       + ["qidx", "ident_f32"],
    10: ["sel16", "ident_f32"],
}
NEED_IDENT_BF = {4, 7, 10}

SUB_WEIGHTS = {
    0: ["f0_gatew", "f0_gateb", "f0_w1t", "f0_w3t", "f0_w2t", "f0_sw1t",
        "f0_sw3t", "f0_sw2t"],
    1: ["a0_int", "a0_outt", "a0_convw", "a0_convb", "a0_a", "a0_lnw", "a0_lnb"],
    3: ["f1_routw", "f1_up", "f1_dn"],
    4: ["a1_wqkv", "a1_wout", "a1_qs", "a1_ks"],
    6: ["f2_lns", "f2_lnb", "f2_gatew", "f2_wg", "f2_wu", "f2_wd", "f2_swg",
        "f2_swu", "f2_swd"],
    7: ["a2_int", "a2_outt"],
    9: ["f3_ln1g", "f3_ln1b", "f3_w1t", "f3_b1", "f3_ln2g", "f3_ln2b",
        "f3_w2t", "f3_b2"],
    10: ["a3_nw", "a3_int", "a3_convw", "a3_convb", "a3_xpt", "a3_dtpt",
         "a3_dtb", "a3_alog", "a3_d", "a3_outt"],
}
SUB_WEIGHTS[2] = SUB_WEIGHTS[0]
SUB_WEIGHTS[5] = SUB_WEIGHTS[3]
SUB_WEIGHTS[8] = SUB_WEIGHTS[6]
SUB_WEIGHTS[11] = SUB_WEIGHTS[9]


def _emit_dispatch(E_, C, es, k, W, xin, xout, extra=None):
    nc = E_.nc
    if k in (3, 5):
        experts = []
        for e in range(8):
            up_e = W["f1_up"][e]
            dn_e = W["f1_dn"][e]
            experts.append({
                "nhb": 8, "act": "gelu_tanh", "gated": e,
                "wg_ap": (lambda hb, a=up_e: a[hb]),
                "wd_ap": (lambda hb, a=dn_e: a[hb]),
            })
        cfg = {"experts": experts,
               "router": {"topk": 4, "norm": True, "norm_eps": 0.0}}
        emit_moe_ffn(E_, C, {"gatew": W["f1_routw"]}, xin, xout, cfg)
    elif k in (0, 2):
        experts = []
        for e in range(8):
            experts.append({
                "nhb": 2, "act": "silu_mul", "gated": e,
                "wg_ap": (lambda hb, a=W["f0_w1t"][e]: a[hb]),
                "wu_ap": (lambda hb, a=W["f0_w3t"][e]: a[hb]),
                "wd_ap": (lambda hb, a=W["f0_w2t"][e]: a[hb]),
            })
        experts.append({
            "nhb": 4, "act": "silu_mul", "gated": None,
            "wg_ap": (lambda hb, a=W["f0_sw1t"]: a[hb]),
            "wu_ap": (lambda hb, a=W["f0_sw3t"]: a[hb]),
            "wd_ap": (lambda hb, a=W["f0_sw2t"]: a[hb]),
        })
        cfg = {"experts": experts,
               "router": {"topk": 2, "norm": False, "bias": True}}
        emit_moe_ffn(E_, C, {"gatew": W["f0_gatew"], "gatebias": W["f0_gateb"]},
                     xin, xout, cfg)
    elif k in (6, 8):
        lns_sb = es.enter_context(
            nc.sbuf_tensor(f"f2_lns_sb_{k}", [128, G], F32))
        lnb_sb = es.enter_context(
            nc.sbuf_tensor(f"f2_lnb_sb_{k}", [128, G], F32))
        with E_.block():
            def f_l(sy):
                ins = sy.dma_start(lns_sb[:], W["f2_lns"])
                E_.i(ins, "qc", 16)
                ins = sy.dma_start(lnb_sb[:], W["f2_lnb"])
                E_.i(ins, "qc", 16)
            E_.sy(f_l)
            E_.setfree("ln_gamma_ld", E_.ev("qc"))
        experts = []
        for e in range(8):
            experts.append({
                "nhb": 4, "act": "silu_mul", "gated": e,
                "wg_ap": (lambda hb, a=W["f2_wg"][e]: a[hb]),
                "wu_ap": (lambda hb, a=W["f2_wu"][e]: a[hb]),
                "wd_ap": (lambda hb, a=W["f2_wd"][e]: a[hb]),
            })
        experts.append({
            "nhb": 4, "act": "silu_mul", "gated": None,
            "wg_ap": (lambda hb, a=W["f2_swg"]: a[hb]),
            "wu_ap": (lambda hb, a=W["f2_swu"]: a[hb]),
            "wd_ap": (lambda hb, a=W["f2_swd"]: a[hb]),
        })
        cfg = {"experts": experts, "ln2_eps": 1e-5,
               "router": {"topk": 2, "norm": True, "norm_eps": 1e-5,
                          "internal_ln": True}}
        emit_moe_ffn(E_, C, {"gatew": W["f2_gatew"], "lns_sb": lns_sb,
                             "lnb_sb": lnb_sb}, xin, xout, cfg)
    elif k in (9, 11):
        qc = (extra or {}).get("f3_qcache")
        emit_ffn3(E_, C, W, xin, xout, qcache=qc, first=(k == 9 or qc is None))
    elif k == 1:
        emit_att0(E_, C, W, xin, xout)
    elif k == 4:
        emit_att1(E_, C, W, xin, xout)
    elif k == 7:
        emit_att2(E_, C, W, xin, xout)
    elif k == 10:
        emit_att3(E_, C, W, xin, xout)
    else:
        raise NotImplementedError(f"sublayer {k}")


def _emit_sub_body(E_, nc, es_outer, k, S, const_dram, xin, xout, W,
                   extra=None):
    ckeys = CONSTS_BASE + CONSTS_BY_SUB[k]
    with contextlib.ExitStack() as es:
        C = {}
        for key in ckeys:
            arr = S[key]
            C[key] = es.enter_context(
                nc.sbuf_tensor(f"c{k}_{key}", list(arr.shape), F32))
        if k in NEED_IDENT_BF:
            C["ident_bf16"] = es.enter_context(
                nc.sbuf_tensor(f"c{k}_ident_bf16", [128, 128], BF16))
        C["ones_col_bf"] = es.enter_context(
            nc.sbuf_tensor(f"c{k}_ones_col_bf", [128, 1], BF16))
        with E_.block():
            fence = [E_.ev(s) for s in ("qs", "qx", "qw", "qv", "qg", "qc")]

            def f_fence_s(sy):
                sy.wait(fence)

            def f_fence_g(g_):
                g_.wait(fence)
            E_.sy(f_fence_s)
            E_.po(f_fence_g)

            def f_cl(sy):
                for key in ckeys:
                    ins = sy.dma_start(C[key][:], const_dram[key])
                    E_.i(ins, "qc", 16)
            E_.sy(f_cl)

            def f_ib(g_):
                if k in NEED_IDENT_BF:
                    ins = g_.dma_start(C["ident_bf16"][:],
                                       const_dram["ident_f32"])
                    E_.i(ins, "qg", 16)
                ins = g_.dma_start(C["ones_col_bf"][:], const_dram["ones_col"])
                E_.i(ins, "qg", 16)
            E_.po(f_ib)
            cl_ev = [E_.ev("qc"), E_.ev("qg")]

            def f_cw(sy):
                sy.wait(cl_ev)
            E_.sy(f_cw)

            def f_cw2(g_):
                g_.wait(cl_ev)
            E_.po(f_cw2)
        _emit_dispatch(E_, C, es, k, W, xin, xout, extra=extra)


def _register_const_floats(nc, es):
    for i, val in enumerate(CONST_FLOATS):
        t_ = es.enter_context(nc.sbuf_tensor(f"c_f{i}", [128, 1], F32))
        nc.const_aps.register(F32, float(val), t_[:])
        yield t_, float(val)


def _build_sub_program(k, S):
    _UID[0] = 0
    nc = bass.Bass("TRN2", target_bir_lowering=False)
    xin = nc.dram_tensor("xin", [8, 128, G, 512], F32, kind="ExternalInput").ap()
    xout = nc.dram_tensor("xout", [8, 128, G, 512], F32,
                          kind="ExternalOutput").ap()
    W = {}
    for key in SUB_WEIGHTS[k]:
        arr = S[key]
        W[key] = nc.dram_tensor(key, list(arr.shape), F32,
                                kind="ExternalInput").ap()
    const_dram = {}
    for key in CONSTS_BASE + CONSTS_BY_SUB[k]:
        arr = S[key]
        const_dram[key] = nc.dram_tensor(key, list(arr.shape), F32,
                                         kind="ExternalInput").ap()
    with contextlib.ExitStack() as es:
        E_ = E(nc, es)
        cf_tiles = list(_register_const_floats(nc, es))
        with E_.block():
            def f_cf(g_):
                for t_, val in cf_tiles:
                    g_.memset(t_[:], val)
            E_.po(f_cf)
        _emit_sub_body(E_, nc, es, k, S, const_dram, xin, xout, W)
        with E_.block():
            def f_fin(sy):
                sy.wait(E_.ev("qs"))
            E_.sy(f_fin)
    return nc


ALL_CONSTS = sorted({k2 for ks in ([CONSTS_BASE] + list(CONSTS_BY_SUB.values()))
                     for k2 in ks})
ALL_WEIGHTS = sorted({k2 for ks in SUB_WEIGHTS.values() for k2 in ks})


def _build_fused(S):
    _UID[0] = 0
    nc = bass.Bass("TRN2", target_bir_lowering=False)
    xin = nc.dram_tensor("xin", [8, 128, G, 512], F32, kind="ExternalInput").ap()
    y = nc.dram_tensor("y", [8, 128, G, 512], F32, kind="ExternalOutput").ap()
    r0 = nc.dram_tensor("r0", [8, 128, G, 512], F32).ap()
    r1 = nc.dram_tensor("r1", [8, 128, G, 512], F32).ap()
    f3q1 = nc.dram_tensor("f3q1", [128, G, 4096], BF16).ap()
    f3q2 = nc.dram_tensor("f3q2", [128, 32, 1024], BF16).ap()
    extra = {"f3_qcache": (f3q1, f3q2)}
    W = {}
    for key in ALL_WEIGHTS:
        arr = S[key]
        W[key] = nc.dram_tensor(key, list(arr.shape), F32,
                                kind="ExternalInput").ap()
    const_dram = {}
    for key in ALL_CONSTS:
        arr = S[key]
        const_dram[key] = nc.dram_tensor(key, list(arr.shape), F32,
                                         kind="ExternalInput").ap()
    with contextlib.ExitStack() as es:
        E_ = E(nc, es)
        cf_tiles = list(_register_const_floats(nc, es))
        with E_.block():
            def f_cf(g_):
                for t_, val in cf_tiles:
                    g_.memset(t_[:], val)
            E_.po(f_cf)
        bufs = [r0, r1]
        for k in range(12):
            src_ap = xin if k == 0 else bufs[(k + 1) % 2]
            dst_ap = y if k == 11 else bufs[k % 2]
            Wk = {key: W[key] for key in SUB_WEIGHTS[k]}
            _emit_sub_body(E_, nc, es, k, S, const_dram, src_ap, dst_ap, Wk,
                           extra=extra)
        with E_.block():
            def f_fin(sy):
                sy.wait(E_.ev("qs"))
            E_.sy(f_fin)
    return nc


def kernel(**inputs):
    S = _prep_shared(inputs)
    x = inputs["L0_attn_x"]
    xs = [_blkx(np.asarray(x[b]).astype(np.float32)) for b in range(NCORES)]
    if FUSED:
        nc = _build_fused(S)
        in_maps = []
        for b in range(NCORES):
            im = {"xin": xs[b]}
            for key in ALL_WEIGHTS:
                im[key] = S[key]
            for key in ALL_CONSTS:
                im[key] = S[key]
            in_maps.append(im)
        res = run_bass_kernel_spmd(nc, in_maps, core_ids=list(range(NCORES)))
        out = np.stack([_unblkx(np.asarray(r["y"])) for r in res.results],
                       axis=0)
        return np.ascontiguousarray(out.astype(np.float32))
    for k in range(12):
        nc = _build_sub_program(k, S)
        in_maps = []
        for b in range(NCORES):
            im = {"xin": xs[b]}
            for key in SUB_WEIGHTS[k]:
                im[key] = S[key]
            for key in CONSTS_BASE + CONSTS_BY_SUB[k]:
                im[key] = S[key]
            in_maps.append(im)
        res = run_bass_kernel_spmd(nc, in_maps, core_ids=list(range(NCORES)))
        xs = [np.asarray(r["xout"]) for r in res.results]
    out = np.stack([_unblkx(xs[b]) for b in range(NCORES)], axis=0)
    return np.ascontiguousarray(out.astype(np.float32))
